# version 03
# baseline (speedup 1.0000x reference)
import contextlib
import numpy as np
import ml_dtypes

import concourse.bass as bass
import concourse.mybir as mybir
from concourse.bass_utils import run_bass_kernel_spmd

F32 = mybir.dt.float32
BF16 = mybir.dt.bfloat16
OP = mybir.AluOpType
AF = mybir.ActivationFunctionType
AX = mybir.AxisListType

B, LSEQ, DM = 2, 16384, 1024
DIN, NST, DTR = 2048, 16, 64
DC = 512
TTOK = 4096
NDT = DC // 128
P2C = 2048
NP2 = LSEQ // P2C
P4C = 512
NP4 = LSEQ // P4C
EPS = 1e-5
TIE_STEP = 1e-6 / 15.0
BF = ml_dtypes.bfloat16


class Sem:
    def __init__(self, nc, es, name):
        self.h = es.enter_context(nc.semaphore(name))
        self.n = 0

    def inc(self, instr, k=1):
        instr.then_inc(self.h, k)
        self.n += k

    def dma(self, instr):
        self.inc(instr, 16)

    def wait(self, eng):
        eng.wait_ge(self.h, self.n)


def build(dbg=(), upto="full"):
    nc = bass.Bass("TRN2", target_bir_lowering=False)
    E = {}

    x_tok = nc.dram_tensor("x_tok", [TTOK, DM], F32, kind="ExternalInput")
    w_xc = nc.dram_tensor("w_xc", [128, 8 * DC], F32, kind="ExternalInput")
    w_z = nc.dram_tensor("w_z", [128, 8 * DC], F32, kind="ExternalInput")
    w_conv = nc.dram_tensor("w_conv", [DC, 4], F32, kind="ExternalInput")
    w_xp = nc.dram_tensor("w_xp", [128, 4 * 96], F32, kind="ExternalInput")
    w_dt = nc.dram_tensor("w_dt", [64, DC], F32, kind="ExternalInput")
    b_dt = nc.dram_tensor("b_dt", [DC, 1], F32, kind="ExternalInput")
    a_log = nc.dram_tensor("a_log", [DC, NST], F32, kind="ExternalInput")
    dvec = nc.dram_tensor("dvec", [DC, 1], F32, kind="ExternalInput")
    w_out = nc.dram_tensor("w_out", [128, 16 * DM], F32, kind="ExternalInput")
    ctrl = nc.dram_tensor("ctrl", [128, 8 * 64], F32, kind="ExternalInput")
    f1t = nc.dram_tensor("f1t", [64, 128, 8 * 64], F32, kind="ExternalInput")
    f2m = nc.dram_tensor("f2m", [64, 64, DM], F32, kind="ExternalInput")
    fbias = nc.dram_tensor("fbias", [64, 64], F32, kind="ExternalInput")
    coff = nc.dram_tensor("coff", [1, 32], mybir.dt.int32, kind="ExternalInput")
    out_tok = nc.dram_tensor("out_tok", [TTOK, DM], F32, kind="ExternalOutput")

    ln1_my = nc.dram_tensor("ln1_my", [DM, TTOK], BF16)
    ln1_all = nc.dram_tensor("ln1_all", [4 * DM, TTOK], BF16)
    dbc_part = nc.dram_tensor("dbc_part", [96, LSEQ], F32)
    dbc_full = nc.dram_tensor("dbc_full", [96, LSEQ], F32)
    xc_sp = nc.dram_tensor("xc_sp", [DC, LSEQ], F32)
    ymid = nc.dram_tensor("ymid", [DC, LSEQ], BF16)
    y_all = nc.dram_tensor("y_all", [4 * DC, LSEQ], BF16)
    xmt_sp = nc.dram_tensor("xmt_sp", [32 * 512, DM], BF16)
    int_sp = nc.dram_tensor("int_sp", [64 * 256, DM], BF16)
    o1_sp = nc.dram_tensor("o1_sp", [TTOK, DM], F32)
    y_loc = nc.dram_tensor("y_loc", [4 * DC, TTOK], BF16)

    def dbg_t(name, shape, dtype=F32):
        if name in dbg:
            return nc.dram_tensor("dbg_" + name, shape, dtype, kind="ExternalOutput")
        return None

    d_ln = dbg_t("ln", [4 * DM, TTOK], BF16)
    d_xc = dbg_t("xc", [DC, LSEQ], F32)
    d_dbc = dbg_t("dbc", [96, LSEQ], F32)
    d_delta = dbg_t("delta", [DC, LSEQ], F32)
    d_ymid = dbg_t("ymid", [DC, LSEQ], BF16)
    d_o1 = dbg_t("o1", [TTOK, DM], F32)
    d_perm = dbg_t("perm", [64, TTOK], F32)
    d_xmt = dbg_t("xmt", [32 * 512, DM], BF16)
    d_int = dbg_t("int", [64 * 256, DM], BF16)

    es_ = contextlib.ExitStack()
    with es_:
        ld = Sem(nc, es_, "ld")
        gd = Sem(nc, es_, "gd")
        st = Sem(nc, es_, "st")
        ev = Sem(nc, es_, "ev")
        sc = Sem(nc, es_, "sc")
        ws = Sem(nc, es_, "ws")
        cc1 = Sem(nc, es_, "cc1")
        cc2 = Sem(nc, es_, "cc2")
        cc3 = Sem(nc, es_, "cc3")
        pc = Sem(nc, es_, "pc")
        E.update(locals())

        wxc_sb = es_.enter_context(nc.sbuf_tensor("wxc_sb", [128, 8 * DC], BF16))
        wz_sb = es_.enter_context(nc.sbuf_tensor("wz_sb", [128, 8 * DC], BF16))
        wxp_sb = es_.enter_context(nc.sbuf_tensor("wxp_sb", [128, 4 * 96], BF16))
        wdt_sb = es_.enter_context(nc.sbuf_tensor("wdt_sb", [64, DC], BF16))
        bdt_sb = es_.enter_context(nc.sbuf_tensor("bdt_sb", [128, NDT], F32))
        dv_sb = es_.enter_context(nc.sbuf_tensor("dv_sb", [128, NDT], F32))
        amat_sb = es_.enter_context(nc.sbuf_tensor("amat_sb", [128, NDT * NST], F32))
        wcv_sb = es_.enter_context(nc.sbuf_tensor("wcv_sb", [128, NDT * 4], F32))
        convd_sb = es_.enter_context(nc.sbuf_tensor("convd_sb", [128, NDT * 4 * 128], BF16))
        idf_sb = es_.enter_context(nc.sbuf_tensor("idf_sb", [128, 128], F32))
        idb_sb = es_.enter_context(nc.sbuf_tensor("idb_sb", [128, 128], BF16))
        sel16_sb = es_.enter_context(nc.sbuf_tensor("sel16_sb", [16, 16 * 128], F32))
        ppi_sb = es_.enter_context(nc.sbuf_tensor("ppi_sb", [128, 2], F32))
        cst_sb = es_.enter_context(nc.sbuf_tensor("cst_sb", [128, 4], F32))
        nc.const_aps.register(F32, EPS, cst_sb[:, 0:1])
        nc.const_aps.register(F32, 1.0, cst_sb[:, 1:2])
        nc.const_aps.register(F32, -1.0, cst_sb[:, 2:3])
        E.update(locals())

        with nc.Block() as blk:
            @blk.gpsimd
            def _(g):
                for s_ in (ld, gd, st, ev, sc, ws, cc1, cc2, cc3, pc):
                    g.sem_clear(s_.h)

        with nc.Block() as blk:
            @blk.sync
            def _(sy):
                for t in range(NDT):
                    ld.dma(sy.dma_start(out=bdt_sb[:, t:t + 1], in_=b_dt[t * 128:(t + 1) * 128, :]))
                    ld.dma(sy.dma_start(out=dv_sb[:, t:t + 1], in_=dvec[t * 128:(t + 1) * 128, :]))
                    ld.dma(sy.dma_start(out=amat_sb[:, t * NST:(t + 1) * NST],
                                        in_=a_log[t * 128:(t + 1) * 128, :]))
                    ld.dma(sy.dma_start(out=wcv_sb[:, t * 4:(t + 1) * 4],
                                        in_=w_conv[t * 128:(t + 1) * 128, :]))
                ld.wait(sy)

            @blk.vector
            def _(v):
                v.memset(cst_sb[:, 0:1], EPS)
                v.memset(cst_sb[:, 1:2], 1.0)
                v.memset(cst_sb[:, 2:3], -1.0)
                ws.inc(v.memset(cst_sb[:, 3:4], 0.0))

            @blk.gpsimd
            def _(g):
                gd.dma(g.dma_start(out=wxc_sb[:, :], in_=w_xc[:, :]))
                gd.dma(g.dma_start(out=wz_sb[:, :], in_=w_z[:, :]))
                gd.dma(g.dma_start(out=wxp_sb[:, :], in_=w_xp[:, :]))
                gd.dma(g.dma_start(out=wdt_sb[:, :], in_=w_dt[:, :]))
                gd.wait(g)
                g.iota(ppi_sb[:, 0:1], [[0, 1]], channel_multiplier=1,
                       allow_small_or_imprecise_dtypes=True)
                g.iota(idf_sb[:, :], [[1, 128]], channel_multiplier=0,
                       allow_small_or_imprecise_dtypes=True)
                g.iota(sel16_sb[:, :], [[1, 16], [0, 128]], channel_multiplier=0,
                       allow_small_or_imprecise_dtypes=True)

        with nc.Block() as blk:
            @blk.scalar
            def _(s):
                s.activation(amat_sb[:, :], amat_sb[:, :], AF.Exp, bias=0.0, scale=1.0)

        with nc.Block() as blk:
            @blk.vector
            def _(v):
                v.tensor_scalar(amat_sb[:, :], amat_sb[:, :], -1.0, None, OP.mult)
                v.tensor_scalar(idf_sb[:, :], idf_sb[:, :], ppi_sb[:, 0:1], None, OP.is_equal)
                ws.inc(v.tensor_scalar(sel16_sb[:, :], sel16_sb[:, :], ppi_sb[0:16, 0:1],
                                       None, OP.is_equal))

            @blk.vector
            def _(v):
                ws.wait(v)
                v.tensor_copy(idb_sb[:, :], idf_sb[:, :])
                for dt in range(NDT):
                    for tap in range(4):
                        v.scalar_tensor_tensor(
                            convd_sb[:, (dt * 4 + tap) * 128:(dt * 4 + tap + 1) * 128],
                            idf_sb[:, :], wcv_sb[:, dt * 4 + tap:dt * 4 + tap + 1],
                            idf_sb[:, :], OP.mult, OP.bypass)

        _es = contextlib.ExitStack()
        with _es:
            xt8 = _es.enter_context(nc.sbuf_tensor("xt8", [128, 8 * DM], F32))
            sq8 = _es.enter_context(nc.sbuf_tensor("sq8", [128, 8 * DM], F32))
            ln8 = _es.enter_context(nc.sbuf_tensor("ln8", [128, 8 * DM], BF16))
            lnT8 = _es.enter_context(nc.sbuf_tensor("lnT8", [128, 8 * DM], BF16))
            lst = _es.enter_context(nc.sbuf_tensor("lst", [128, 64], F32))
            psT0 = _es.enter_context(nc.psum_tensor("psT0", [128, 512], BF16))
            psT1 = _es.enter_context(nc.psum_tensor("psT1", [128, 512], BF16))
            psT = [psT0, psT1]
            for tb in range(TTOK // 128 // 8):
                with nc.Block() as blk:
                    @blk.sync
                    def _(sy):
                        for i in range(8):
                            ti = tb * 8 + i
                            ld.dma(sy.dma_start(out=xt8[:, i * DM:(i + 1) * DM],
                                                in_=x_tok[ti * 128:(ti + 1) * 128, :]))

                    @blk.scalar
                    def _(s):
                        ld.wait(s)
                        for i in range(8):
                            sc.inc(s.activation(sq8[:, i * DM:(i + 1) * DM],
                                                xt8[:, i * DM:(i + 1) * DM],
                                                AF.Square, bias=0.0, scale=1.0))

                    @blk.vector
                    def _(v):
                        ld.wait(v)
                        for i in range(8):
                            ins = v.reduce_sum(lst[:, i * 8:i * 8 + 1], xt8[:, i * DM:(i + 1) * DM],
                                               AX.X)
                        sc.wait(v)
                        for i in range(8):
                            ins = v.reduce_sum(lst[:, i * 8 + 1:i * 8 + 2],
                                               sq8[:, i * DM:(i + 1) * DM], AX.X)
                        ws.inc(ins)
                        ws.wait(v)
                        for i in range(8):
                            c = i * 8
                            ins = v.tensor_scalar(lst[:, c + 2:c + 3], lst[:, c:c + 1],
                                                  1.0 / DM, None, OP.mult)
                            ins = v.tensor_scalar(lst[:, c + 3:c + 4], lst[:, c + 1:c + 2],
                                                  1.0 / DM, None, OP.mult)
                        ws.inc(ins)
                        ws.wait(v)
                        for i in range(8):
                            c = i * 8
                            ins = v.tensor_tensor(lst[:, c + 4:c + 5], lst[:, c + 2:c + 3],
                                                  lst[:, c + 2:c + 3], OP.mult)
                        ws.inc(ins)
                        ws.wait(v)
                        for i in range(8):
                            c = i * 8
                            ins = v.tensor_tensor(lst[:, c + 5:c + 6], lst[:, c + 3:c + 4],
                                                  lst[:, c + 4:c + 5], OP.subtract)
                        ws.inc(ins)

                    @blk.scalar
                    def _(s):
                        ws.wait(s)
                        for i in range(8):
                            c = i * 8
                            ins = s.activation(lst[:, c + 6:c + 7], lst[:, c + 5:c + 6],
                                               AF.Ln, bias=EPS, scale=1.0)
                        sc.inc(ins)
                        sc.wait(s)
                        for i in range(8):
                            c = i * 8
                            sc.inc(s.activation(lst[:, c + 7:c + 8], lst[:, c + 6:c + 7],
                                                AF.Exp, bias=0.0, scale=-0.5))

                    @blk.vector
                    def _(v):
                        sc.wait(v)
                        for i in range(8):
                            c = i * 8
                            ws.inc(v.tensor_scalar(ln8[:, i * DM:(i + 1) * DM],
                                                   xt8[:, i * DM:(i + 1) * DM],
                                                   lst[:, c + 2:c + 3], lst[:, c + 7:c + 8],
                                                   OP.subtract, OP.mult))

                    ev_b = ev.n
                    sc_b = sc.n

                    @blk.tensor
                    def _(t):
                        ws.wait(t)
                        for grp in range(16):
                            if grp >= 2:
                                t.wait_ge(sc.h, sc_b + grp - 1)
                            for j in range(4):
                                g = grp * 4 + j
                                i, c = g // 8, g % 8
                                mm = t.transpose(psT[grp % 2][:, j * 128:(j + 1) * 128],
                                                 ln8[:, i * DM + c * 128:i * DM + (c + 1) * 128],
                                                 idb_sb[:, :])
                            ev.inc(mm)

                    @blk.scalar
                    def _(s):
                        for grp in range(16):
                            s.wait_ge(ev.h, ev_b + grp + 1)
                            g0 = grp * 4
                            i = g0 // 8
                            c = g0 % 8
                            sc.inc(s.activation(lnT8[:, i * DM + c * 128:i * DM + (c + 4) * 128],
                                                psT[grp % 2][:, :], AF.Copy, bias=0.0, scale=1.0))

                    @blk.sync
                    def _(sy):
                        sc.wait(sy)
                        for i in range(8):
                            ti = tb * 8 + i
                            dst = bass.AP(ln1_my, ti * 128, [[TTOK, 128], [128 * TTOK, 8], [1, 128]])
                            src = bass.AP(lnT8, i * DM, [[8 * DM, 128], [128, 8], [1, 128]])
                            st.dma(sy.dma_start(out=dst, in_=src))
                        st.wait(sy)

        with nc.Block() as blk:
            @blk.gpsimd
            def _(g):
                cc1.inc(g.collective_compute(
                    "AllGather", OP.bypass,
                    replica_groups=[[0, 1, 2, 3], [4, 5, 6, 7]],
                    ins=[ln1_my.ap().opt()], outs=[ln1_all.ap().opt()]))
                g.wait_ge(cc1.h, 1)

        if upto != "p1":
            _es = contextlib.ExitStack()
            with _es:
                rhs8 = _es.enter_context(nc.sbuf_tensor("rhs8", [128, 8 * P2C], BF16))
                xcpre = _es.enter_context(nc.sbuf_tensor("xcpre", [128, NDT * (P2C + 3)], BF16))
                xcf = _es.enter_context(nc.sbuf_tensor("xcf", [128, NDT * P2C], F32))
                xcb = _es.enter_context(nc.sbuf_tensor("xcb", [128, NDT * P2C], BF16))
                dbc_sb = _es.enter_context(nc.sbuf_tensor("dbc_sb", [96, P2C], F32))
                psA0 = _es.enter_context(nc.psum_tensor("psA0", [128, 512], F32))
                psA1 = _es.enter_context(nc.psum_tensor("psA1", [128, 512], F32))
                psB0 = _es.enter_context(nc.psum_tensor("psB0", [128, 512], F32))
                psB1 = _es.enter_context(nc.psum_tensor("psB1", [128, 512], F32))
                psC = _es.enter_context(nc.psum_tensor("psC", [128, 512], F32))
                psA = [psA0, psA1]
                psB = [psB0, psB1]
                XP3 = P2C + 3
                for ch in range(NP2):
                    rank = (ch * P2C) // TTOK
                    col0 = ch * P2C - rank * TTOK
                    with nc.Block() as blk:
                        @blk.sync
                        def _(sy):
                            for cc in range(8):
                                src = bass.AP(ln1_all, (rank * DM + cc * 128) * TTOK + col0,
                                              [[TTOK, 128], [1, P2C]])
                                ld.dma(sy.dma_start(out=rhs8[:, cc * P2C:(cc + 1) * P2C], in_=src))

                        @blk.vector
                        def _(v):
                            for dt in range(NDT):
                                o = dt * XP3
                                if ch == 0:
                                    ws.inc(v.memset(xcpre[:, o:o + 3], 0.0))
                                else:
                                    ws.inc(v.tensor_copy(xcpre[:, o:o + 3],
                                                         xcpre[:, o + P2C:o + P2C + 3]))

                        ev_b = ev.n
                        sc_b = sc.n

                        @blk.tensor
                        def _(t):
                            ld.wait(t)
                            ws.wait(t)
                            for dt in range(NDT):
                                for f in range(4):
                                    g = dt * 4 + f
                                    if g >= 2:
                                        t.wait_ge(sc.h, sc_b + g - 1)
                                    for k in range(8):
                                        mm = t.matmul(psA[g % 2][:, :],
                                                      wxc_sb[:, k * DC + dt * 128:k * DC + (dt + 1) * 128],
                                                      rhs8[:, k * P2C + f * 512:k * P2C + (f + 1) * 512],
                                                      start=(k == 0), stop=(k == 7))
                                    ev.inc(mm)
                            for dt in range(NDT):
                                for f in range(4):
                                    g2 = dt * 4 + f
                                    t.wait_ge(sc.h, sc_b + (dt + 1) * 4)
                                    if g2 >= 2:
                                        t.wait_ge(sc.h, sc_b + 16 + g2 - 1)
                                    o = dt * XP3 + f * 512
                                    for tap in range(4):
                                        mm = t.matmul(psB[g2 % 2][:, :],
                                                      convd_sb[:, (dt * 4 + tap) * 128:(dt * 4 + tap + 1) * 128],
                                                      xcpre[:, o + tap:o + tap + 512],
                                                      start=(tap == 0), stop=(tap == 3))
                                    ev.inc(mm)

                        @blk.scalar
                        def _(s):
                            for g in range(16):
                                dt, f = g // 4, g % 4
                                s.wait_ge(ev.h, ev_b + g + 1)
                                sc.inc(s.activation(
                                    xcpre[:, dt * XP3 + 3 + f * 512:dt * XP3 + 3 + (f + 1) * 512],
                                    psA[g % 2][:, :], AF.Copy, bias=0.0, scale=1.0))
                            for g2 in range(16):
                                dt, f = g2 // 4, g2 % 4
                                s.wait_ge(ev.h, ev_b + 16 + g2 + 1)
                                sc.inc(s.activation(xcf[:, dt * P2C + f * 512:dt * P2C + (f + 1) * 512],
                                                    psB[g2 % 2][:, :], AF.Silu, bias=0.0, scale=1.0))

                        @blk.vector
                        def _(v):
                            for g2 in range(16):
                                dt, f = g2 // 4, g2 % 4
                                v.wait_ge(sc.h, sc_b + 16 + g2 + 1)
                                ws.inc(v.tensor_copy(xcb[:, dt * P2C + f * 512:dt * P2C + (f + 1) * 512],
                                                     xcf[:, dt * P2C + f * 512:dt * P2C + (f + 1) * 512]))

                    with nc.Block() as blk:
                        ev_b2 = ev.n
                        sc_b2 = sc.n

                        @blk.tensor
                        def _(t):
                            for f in range(4):
                                if f >= 1:
                                    t.wait_ge(sc.h, sc_b2 + f)
                                for k in range(NDT):
                                    mm = t.matmul(psC[0:96, :],
                                                  wxp_sb[:, k * 96:(k + 1) * 96],
                                                  xcb[:, k * P2C + f * 512:k * P2C + (f + 1) * 512],
                                                  start=(k == 0), stop=(k == NDT - 1))
                                ev.inc(mm)

                        @blk.scalar
                        def _(s):
                            for f in range(4):
                                s.wait_ge(ev.h, ev_b2 + f + 1)
                                sc.inc(s.activation(dbc_sb[:, f * 512:(f + 1) * 512],
                                                    psC[0:96, :], AF.Copy, bias=0.0, scale=1.0))

                        @blk.sync
                        def _(sy):
                            sc.wait(sy)
                            dst = bass.AP(dbc_part, ch * P2C, [[LSEQ, 96], [1, P2C]])
                            st.dma(sy.dma_start(out=dst, in_=dbc_sb[:, :]))
                            for dt in range(NDT):
                                dstx = bass.AP(xc_sp, dt * 128 * LSEQ + ch * P2C, [[LSEQ, 128], [1, P2C]])
                                st.dma(sy.dma_start(out=dstx, in_=xcf[:, dt * P2C:(dt + 1) * P2C]))
                            st.wait(sy)

        if upto not in ("p1", "p2"):
            with nc.Block() as blk:
                @blk.gpsimd
                def _(g):
                    cc2.inc(g.collective_compute(
                        "AllReduce", OP.add,
                        replica_groups=[[0, 1, 2, 3], [4, 5, 6, 7]],
                        ins=[dbc_part.ap().opt()], outs=[dbc_full.ap().opt()]))
                    g.wait_ge(cc2.h, 1)

        E.update(dict(x_tok=x_tok, w_out=w_out, ctrl=ctrl, f1t=f1t, f2m=f2m, fbias=fbias, coff=coff, y_loc=y_loc,
                      out_tok=out_tok, ln1_all=ln1_all, dbc_full=dbc_full, xc_sp=xc_sp,
                      ymid=ymid, y_all=y_all, xmt_sp=xmt_sp, int_sp=int_sp, o1_sp=o1_sp,
                      d_delta=d_delta, d_perm=d_perm, nc=nc))

        if upto in ("p4", "p5", "full"):
            _p4(E)
        if upto in ("p5", "full"):
            with nc.Block() as blk:
                @blk.gpsimd
                def _(g):
                    cc3.inc(g.collective_compute(
                        "AllGather", OP.bypass,
                        replica_groups=[[0, 1, 2, 3], [4, 5, 6, 7]],
                        ins=[ymid.ap().opt()], outs=[y_all.ap().opt()]))
                    g.wait_ge(cc3.h, 1)
        if upto == "full":
            _p6(E)
        else:
            with (
                nc.sbuf_tensor("zz", [128, DM], F32) as zz,
                nc.Block() as blk,
            ):
                @blk.vector
                def _(v):
                    ws.inc(v.memset(zz[:, :], 0.0))

                @blk.sync
                def _(sy):
                    ws.wait(sy)
                    for ti in range(TTOK // 128):
                        st.dma(sy.dma_start(out=out_tok[ti * 128:(ti + 1) * 128, :], in_=zz[:, :]))
                    st.wait(sy)

        if dbg:
            with nc.Block() as blk:
                @blk.sync
                def _(sy):
                    for name, dd, src in (("ln", d_ln, ln1_all), ("xc", d_xc, xc_sp),
                                          ("dbc", d_dbc, dbc_full), ("ymid", d_ymid, ymid),
                                          ("o1", d_o1, o1_sp), ("xmt", d_xmt, xmt_sp),
                                          ("int", d_int, int_sp)):
                        if dd is not None:
                            st.dma(sy.dma_start(out=dd[:, :], in_=src[:, :]))
                    st.wait(sy)
        with nc.Block() as blk:
            @blk.gpsimd
            def _(g):
                for s_ in (ld, gd, st, ev, sc, ws, cc1, cc2, cc3, pc):
                    g.sem_clear(s_.h)
    return nc


def _p4(E):
    nc = E["nc"]
    ld, st, ev, sc, ws, pc = E["ld"], E["st"], E["ev"], E["sc"], E["ws"], E["pc"]
    wdt_sb, bdt_sb, dv_sb, amat_sb, sel16_sb, wz_sb = (
        E["wdt_sb"], E["bdt_sb"], E["dv_sb"], E["amat_sb"], E["sel16_sb"], E["wz_sb"])
    ln1_all, dbc_full, xc_sp, ymid, d_delta = (
        E["ln1_all"], E["dbc_full"], E["xc_sp"], E["ymid"], E["d_delta"])
    C = P4C
    NH = NST // 2

    _es = contextlib.ExitStack()
    with _es:
        rz8 = _es.enter_context(nc.sbuf_tensor("rz8", [128, 8 * C], BF16))
        dltb = _es.enter_context(nc.sbuf_tensor("dltb", [64, C], BF16))
        dltf = _es.enter_context(nc.sbuf_tensor("dltf", [64, C], F32))
        bm_sb = _es.enter_context(nc.sbuf_tensor("bm_sb", [16, C], F32))
        cm_sb = _es.enter_context(nc.sbuf_tensor("cm_sb", [16, C], F32))
        delta = _es.enter_context(nc.sbuf_tensor("delta", [128, NDT * C], F32))
        xc4 = _es.enter_context(nc.sbuf_tensor("xc4", [128, NDT * C], F32))
        u4 = _es.enter_context(nc.sbuf_tensor("u4", [128, NDT * C], F32))
        gz4 = _es.enter_context(nc.sbuf_tensor("gz4", [128, NDT * C], F32))
        y4 = _es.enter_context(nc.sbuf_tensor("y4", [128, NDT * C], F32))
        yh4 = _es.enter_context(nc.sbuf_tensor("yh4", [128, NDT * C], F32))
        ymb = _es.enter_context(nc.sbuf_tensor("ymb", [128, NDT * C], BF16))
        bmb = _es.enter_context(nc.sbuf_tensor("bmb", [128, NH * C], F32))
        cmb = _es.enter_context(nc.sbuf_tensor("cmb", [128, NH * C], F32))
        hbuf = _es.enter_context(nc.sbuf_tensor("hbuf", [128, NH * C], F32))
        hc8 = _es.enter_context(nc.sbuf_tensor("hc8", [128, NH * C], F32))
        abuf8 = _es.enter_context(nc.sbuf_tensor("abuf8", [128, 8 * C], F32))
        bbuf8 = _es.enter_context(nc.sbuf_tensor("bbuf8", [128, 8 * C], F32))
        hstate = _es.enter_context(nc.sbuf_tensor("hstate", [128, NDT * NST], F32))
        ps0 = _es.enter_context(nc.psum_tensor("ps0", [128, 512], F32))
        ps1 = _es.enter_context(nc.psum_tensor("ps1", [128, 512], F32))
        ps2 = _es.enter_context(nc.psum_tensor("ps2", [128, 512], F32))
        ps3 = _es.enter_context(nc.psum_tensor("ps3", [128, 512], F32))
        ps = [ps0, ps1, ps2, ps3]
        with nc.Block() as blk:
            @blk.vector
            def _(v):
                v.memset(hstate[:, :], 0.0)

        for ch in range(NP4):
            rank = (ch * C) // TTOK
            col0 = ch * C - rank * TTOK
            with nc.Block() as blk:
                @blk.sync
                def _(sy):
                    for cc in range(8):
                        src = bass.AP(ln1_all, (rank * DM + cc * 128) * TTOK + col0,
                                      [[TTOK, 128], [1, C]])
                        ld.dma(sy.dma_start(out=rz8[:, cc * C:(cc + 1) * C], in_=src))
                    ld.dma(sy.dma_start(out=dltf[:, :],
                                        in_=bass.AP(dbc_full, ch * C, [[LSEQ, 64], [1, C]])))
                    ld.dma(sy.dma_start(out=bm_sb[:, :],
                                        in_=bass.AP(dbc_full, 64 * LSEQ + ch * C, [[LSEQ, 16], [1, C]])))
                    ld.dma(sy.dma_start(out=cm_sb[:, :],
                                        in_=bass.AP(dbc_full, 80 * LSEQ + ch * C, [[LSEQ, 16], [1, C]])))
                    for dt in range(NDT):
                        ld.dma(sy.dma_start(out=xc4[:, dt * C:(dt + 1) * C],
                                            in_=bass.AP(xc_sp, dt * 128 * LSEQ + ch * C,
                                                        [[LSEQ, 128], [1, C]])))

                @blk.vector
                def _(v):
                    ld.wait(v)
                    ws.inc(v.tensor_copy(dltb[:, :], dltf[:, :]))

                ev_b = ev.n
                sc_b = sc.n
                groups = [("z", dt) for dt in range(NDT)] + [("d", dt) for dt in range(NDT)]

                @blk.tensor
                def _(t):
                    ld.wait(t)
                    ws.wait(t)
                    for g, (kind, dt) in enumerate(groups):
                        if g >= 4:
                            t.wait_ge(sc.h, sc_b + g - 3)
                        if kind == "z":
                            for k in range(8):
                                mm = t.matmul(ps[g % 4][:, :],
                                              wz_sb[:, k * DC + dt * 128:k * DC + (dt + 1) * 128],
                                              rz8[:, k * C:(k + 1) * C],
                                              start=(k == 0), stop=(k == 7))
                        else:
                            mm = t.matmul(ps[g % 4][:, :], wdt_sb[:, dt * 128:(dt + 1) * 128],
                                          dltb[:, :], start=True, stop=True)
                        ev.inc(mm)

                @blk.scalar
                def _(s):
                    for g, (kind, dt) in enumerate(groups):
                        s.wait_ge(ev.h, ev_b + g + 1)
                        if kind == "z":
                            sc.inc(s.activation(gz4[:, dt * C:(dt + 1) * C], ps[g % 4][:, :],
                                                AF.Silu, bias=0.0, scale=1.0))
                        else:
                            sc.inc(s.activation(delta[:, dt * C:(dt + 1) * C], ps[g % 4][:, :],
                                                AF.Identity, bias=bdt_sb[:, dt:dt + 1], scale=1.0))

            with nc.Block() as blk:
                @blk.scalar
                def _(s):
                    for dt in range(NDT):
                        sl = slice(dt * C, (dt + 1) * C)
                        ins = s.activation(y4[:, sl], delta[:, sl], AF.Abs, bias=0.0, scale=1.0)
                    sc.inc(ins)
                    sc.wait(s)
                    for dt in range(NDT):
                        sl = slice(dt * C, (dt + 1) * C)
                        ins = s.activation(yh4[:, sl], y4[:, sl], AF.Exp, bias=0.0, scale=-1.0)
                    sc.inc(ins)
                    sc.wait(s)
                    for dt in range(NDT):
                        sl = slice(dt * C, (dt + 1) * C)
                        ins = s.activation(y4[:, sl], yh4[:, sl], AF.Ln, bias=1.0, scale=1.0)
                    sc.inc(ins)

                @blk.vector
                def _(v):
                    sc.wait(v)
                    for dt in range(NDT):
                        sl = slice(dt * C, (dt + 1) * C)
                        ins = v.scalar_tensor_tensor(delta[:, sl], delta[:, sl], 0.0, y4[:, sl],
                                                     OP.max, OP.add)
                    ws.inc(ins)
                    ws.wait(v)
                    for dt in range(NDT):
                        sl = slice(dt * C, (dt + 1) * C)
                        ins = v.tensor_tensor(u4[:, sl], delta[:, sl], xc4[:, sl], OP.mult)
                    ws.inc(ins)

            if d_delta is not None:
                with nc.Block() as blk:
                    @blk.sync
                    def _(sy):
                        for dt in range(NDT):
                            dstd = bass.AP(d_delta, dt * 128 * LSEQ + ch * C, [[LSEQ, 128], [1, C]])
                            st.dma(sy.dma_start(out=dstd, in_=delta[:, dt * C:(dt + 1) * C]))
                        st.wait(sy)

            for half in range(2):
                with nc.Block() as blk:
                    ev_b = ev.n
                    sc_b = sc.n
                    ws_b = ws.n
                    pc_b = pc.n
                    rgroups = [("b", n) for n in range(NH)] + [("c", n) for n in range(NH)]

                    @blk.tensor
                    def _(t):
                        for g, (kind, n) in enumerate(rgroups):
                            gn = half * NH + n
                            if g >= 4:
                                t.wait_ge(sc.h, sc_b + g - 3)
                            src_ = bm_sb if kind == "b" else cm_sb
                            mm = t.matmul(ps[g % 4][:, :], sel16_sb[:, gn * 128:(gn + 1) * 128],
                                          src_[:, :], start=True, stop=True)
                            ev.inc(mm)

                    @blk.scalar
                    def _(s):
                        for g, (kind, n) in enumerate(rgroups):
                            s.wait_ge(ev.h, ev_b + g + 1)
                            dstt = bmb if kind == "b" else cmb
                            sc.inc(s.activation(dstt[:, n * C:(n + 1) * C], ps[g % 4][:, :],
                                                AF.Copy, bias=0.0, scale=1.0))
                        for dt in range(NDT):
                            if dt >= 1:
                                s.wait_ge(ws.h, ws_b + (dt - 1) * 10 + 8)
                            for n in range(NH):
                                gn = half * NH + n
                                sc.inc(s.activation(abuf8[:, n * C:(n + 1) * C],
                                                    delta[:, dt * C:(dt + 1) * C], AF.Exp,
                                                    bias=0.0,
                                                    scale=amat_sb[:, dt * NST + gn:dt * NST + gn + 1]))

                    @blk.gpsimd
                    def _(g):
                        for dt in range(NDT):
                            g.wait_ge(sc.h, sc_b + 16)
                            if dt >= 1:
                                g.wait_ge(ws.h, ws_b + (dt - 1) * 10 + 8)
                            for n in range(NH):
                                pc.inc(g.tensor_tensor(bbuf8[:, n * C:(n + 1) * C],
                                                       u4[:, dt * C:(dt + 1) * C],
                                                       bmb[:, n * C:(n + 1) * C], OP.mult))
                            g.wait_ge(ws.h, ws_b + dt * 10 + 8)
                            pc.inc(g.tensor_tensor(hc8[:, :], hbuf[:, :], cmb[:, :], OP.mult))

                    @blk.vector
                    def _(v):
                        for dt in range(NDT):
                            for n in range(NH):
                                gn = half * NH + n
                                v.wait_ge(sc.h, sc_b + 16 + dt * NH + n + 1)
                                v.wait_ge(pc.h, pc_b + dt * (NH + 1) + n + 1)
                                ws.inc(v.tensor_tensor_scan(
                                    hbuf[:, n * C:(n + 1) * C],
                                    abuf8[:, n * C:(n + 1) * C],
                                    bbuf8[:, n * C:(n + 1) * C],
                                    hstate[:, dt * NST + gn:dt * NST + gn + 1],
                                    OP.mult, OP.add))
                            v.wait_ge(ws.h, ws.n)
                            lastc = bass.AP(hbuf, C - 1, [[NH * C, 128], [C, NH]])
                            ws.inc(v.tensor_copy(
                                bass.AP(hstate, dt * NST + half * NH, [[NDT * NST, 128], [1, NH]]),
                                lastc))
                            v.wait_ge(pc.h, pc_b + dt * (NH + 1) + NH + 1)
                            h3 = bass.AP(hc8, 0, [[NH * C, 128], [1, C], [C, NH]])
                            dst_y = y4 if half == 0 else yh4
                            ws.inc(v.tensor_reduce(dst_y[:, dt * C:(dt + 1) * C], h3, AX.X, OP.add))
                            v.wait_ge(ws.h, ws.n)
            with nc.Block() as blk:
                @blk.vector
                def _(v):
                    for dt in range(NDT):
                        sl = slice(dt * C, (dt + 1) * C)
                        ins = v.tensor_tensor(y4[:, sl], y4[:, sl], yh4[:, sl], OP.add)
                    ws.inc(ins)
                    ws.wait(v)
                    for dt in range(NDT):
                        sl = slice(dt * C, (dt + 1) * C)
                        ins = v.scalar_tensor_tensor(y4[:, sl], xc4[:, sl], dv_sb[:, dt:dt + 1],
                                                     y4[:, sl], OP.mult, OP.add)
                    ws.inc(ins)
                    ws.wait(v)
                    for dt in range(NDT):
                        sl = slice(dt * C, (dt + 1) * C)
                        ws.inc(v.tensor_tensor(ymb[:, sl], y4[:, sl], gz4[:, sl], OP.mult))

                @blk.sync
                def _(sy):
                    ws.wait(sy)
                    for dt in range(NDT):
                        dsty = bass.AP(ymid, dt * 128 * LSEQ + ch * C, [[LSEQ, 128], [1, C]])
                        st.dma(sy.dma_start(out=dsty, in_=ymb[:, dt * C:(dt + 1) * C]))
                    st.wait(sy)


def _p6(E):
    nc = E["nc"]
    ld, gd, st, ev, sc, ws = E["ld"], E["gd"], E["st"], E["ev"], E["sc"], E["ws"]
    idf_sb, idb_sb, ppi_sb = E["idf_sb"], E["idb_sb"], E["ppi_sb"]
    x_tok, w_out, ctrl, f1t, f2m, fbias, out_tok = (
        E["x_tok"], E["w_out"], E["ctrl"], E["f1t"], E["f2m"], E["fbias"], E["out_tok"])
    coff = E["coff"]
    y_loc = E["y_loc"]

    with nc.Block() as blk:
        @blk.gpsimd
        def _(g):
            pid = g.partition_id()
            goff_v = (pid % 4) * TTOK
            gd.dma(g.dma_start(
                out=y_loc.ap(),
                in_=bass.AP(E["y_all"], goff_v, [[LSEQ, 4 * DC], [1, TTOK]])))
            gd.wait(g)
    y_all, xmt_sp, int_sp, o1_sp, d_perm = (
        E["y_all"], E["xmt_sp"], E["int_sp"], E["o1_sp"], E["d_perm"])

    _es = contextlib.ExitStack()
    with _es:
        wo_sb = _es.enter_context(nc.sbuf_tensor("wo_sb", [128, 16 * DM], BF16))
        ctrl_sb = _es.enter_context(nc.sbuf_tensor("ctrl_sb", [128, 8 * 64], F32))
        tie_sb = _es.enter_context(nc.sbuf_tensor("tie_sb", [64, 128], F32))
        mbd2_sb = _es.enter_context(nc.sbuf_tensor("mbd2_sb", [128, 512], BF16))
        mbdt_sb = _es.enter_context(nc.sbuf_tensor("mbdt_sb", [128, 4096], BF16))
        esg_sb = _es.enter_context(nc.sbuf_tensor("esg_sb", [64, 4 * 128], F32))
        perm_sb = _es.enter_context(nc.sbuf_tensor("perm_sb", [64, TTOK], F32))
        scr_sb = _es.enter_context(nc.sbuf_tensor("scr_sb", [128, 4096], F32))
        pint_sb = _es.enter_context(nc.sbuf_tensor("pint_sb", [128, 2], F32))
        coff_sb = _es.enter_context(nc.sbuf_tensor("coff_sb", [1, 32], mybir.dt.int32))
        with nc.Block() as blk:
            @blk.gpsimd
            def _(g):
                gd.dma(g.dma_start(out=wo_sb[:, :], in_=w_out[:, :]))
                gd.wait(g)
                g.iota(tie_sb[:, :], [[0, 8], [1, 16]], channel_multiplier=0,
                       allow_small_or_imprecise_dtypes=True)
                g.iota(scr_sb[:, 0:512], [[0, 64], [1, 8]], channel_multiplier=0,
                       allow_small_or_imprecise_dtypes=True)
                g.iota(scr_sb[:, 512:512 + 4096], [[0, 32], [1, 8], [0, 16]],
                       channel_multiplier=0, allow_small_or_imprecise_dtypes=True) if False else None

            @blk.sync
            def _(sy):
                ld.dma(sy.dma_start(out=ctrl_sb[:, :], in_=ctrl[:, :]))
                ld.dma(sy.dma_start(out=coff_sb[:, :], in_=coff[:, :]))
                ld.wait(sy)

        _es = contextlib.ExitStack()
        with _es:
            scr2_sb = _es.enter_context(nc.sbuf_tensor("scr2_sb", [128, 4096], F32))
            esgi_sb = _es.enter_context(nc.sbuf_tensor("esgi_sb", [64, 4 * 128], F32))
            with nc.Block() as blk:
                @blk.gpsimd
                def _(g):
                    g.iota(scr2_sb[:, :], [[0, 32], [1, 8], [0, 16]], channel_multiplier=0,
                           allow_small_or_imprecise_dtypes=True)
                    pass

                @blk.vector
                def _(v):
                    pass

            with (
                nc.sbuf_tensor("ii_sb", [128, 2], mybir.dt.int32) as ii_sb,
                nc.Block() as blk,
            ):
                @blk.gpsimd
                def _(g):
                    ws.inc(g.iota(ii_sb[:, 0:1], [[0, 1]], channel_multiplier=1))

                @blk.vector
                def _(v):
                    ws.wait(v)
                    ws.inc(v.tensor_scalar(ii_sb[:, 1:2], ii_sb[:, 0:1], 4, None,
                                           OP.logical_shift_right))
                    ws.wait(v)
                    ws.inc(v.tensor_scalar(ii_sb[:, 0:1], ii_sb[:, 0:1], 7, None, OP.bitwise_and))

                @blk.vector
                def _(v):
                    ws.wait(v)
                    v.tensor_copy(pint_sb[:, 0:1], ii_sb[:, 1:2])
                    v.tensor_copy(pint_sb[:, 1:2], ii_sb[:, 0:1])

            with nc.Block() as blk:
                @blk.vector
                def _(v):
                    v.tensor_scalar(tie_sb[:, :], tie_sb[:, :], TIE_STEP, None, OP.mult)
                    ins = v.tensor_scalar(scr_sb[:, 0:512], scr_sb[:, 0:512], pint_sb[:, 0:1],
                                          None, OP.is_equal)
                    ins = v.tensor_scalar(scr2_sb[:, :], scr2_sb[:, :], pint_sb[:, 1:2],
                                          None, OP.is_equal)
                    ws.inc(ins)
                    ws.wait(v)
                    v.tensor_copy(mbd2_sb[:, :], scr_sb[:, 0:512])
                    v.tensor_copy(mbdt_sb[:, :], scr2_sb[:, :])

            with nc.Block() as blk:
                @blk.gpsimd
                def _(g):
                    for esg in range(4):
                        ins = g.iota(esgi_sb[:, esg * 128:(esg + 1) * 128], [[1, 16], [0, 8]],
                                     base=esg * 16, channel_multiplier=0,
                                     allow_small_or_imprecise_dtypes=True)
                    ws.inc(ins)

                @blk.vector
                def _(v):
                    ws.wait(v)
                    v.tensor_scalar(esg_sb[:, :], esgi_sb[:, :], ppi_sb[0:64, 0:1],
                                    None, OP.is_equal)

        _es = contextlib.ExitStack()
        with _es:
            ya_sb = _es.enter_context(nc.sbuf_tensor("ya_sb", [128, 16 * 128], BF16))
            xt_sb = _es.enter_context(nc.sbuf_tensor("xt_sb", [128, DM], F32))
            o1t = _es.enter_context(nc.sbuf_tensor("o1t", [128, DM], F32))
            sqt = _es.enter_context(nc.sbuf_tensor("sqt", [128, DM], F32))
            l2 = _es.enter_context(nc.sbuf_tensor("l2", [128, DM], F32))
            l2b = _es.enter_context(nc.sbuf_tensor("l2b", [128, DM], BF16))
            l2f = _es.enter_context(nc.sbuf_tensor("l2f", [128, DM], F32))
            lgt = _es.enter_context(nc.sbuf_tensor("lgt", [64, 128], F32))
            mx_sb = _es.enter_context(nc.sbuf_tensor("mx_sb", [64, 136], F32))
            prmb = _es.enter_context(nc.sbuf_tensor("prmb", [64, 128], BF16))
            prT = _es.enter_context(nc.sbuf_tensor("prT", [128, 64], BF16))
            prTE = _es.enter_context(nc.sbuf_tensor("prTE", [128, 512], BF16))
            prBD = _es.enter_context(nc.sbuf_tensor("prBD", [128, 512], BF16))
            xmt4 = _es.enter_context(nc.sbuf_tensor("xmt4", [128, 4 * DM], BF16))
            lstt = _es.enter_context(nc.sbuf_tensor("lstt", [128, 8], F32))
            q0 = _es.enter_context(nc.psum_tensor("q0", [128, 512], F32))
            q1 = _es.enter_context(nc.psum_tensor("q1", [128, 512], F32))
            qb0 = _es.enter_context(nc.psum_tensor("qb0", [128, 512], BF16))
            qb1 = _es.enter_context(nc.psum_tensor("qb1", [128, 512], BF16))
            q = [q0, q1]
            qb = [qb0, qb1]
            for ti in range(TTOK // 128):
                with nc.Block() as blk:
                    @blk.sync
                    def _(sy):
                        ld.dma(sy.dma_start(out=xt_sb[:, :], in_=x_tok[ti * 128:(ti + 1) * 128, :]))

                    @blk.gpsimd
                    def _(g):
                        gd.dma(g.dma_start(
                            out=bass.AP(ya_sb, 0, [[2048, 128], [128, 16], [1, 128]]),
                            in_=bass.AP(y_loc, ti * 128,
                                        [[TTOK, 128], [128 * TTOK, 16], [1, 128]])))
                        gd.wait(g)

                    ev_b = ev.n
                    ws_b = ws.n
                    sc_b = sc.n

                    @blk.tensor
                    def _(t):
                        ld.wait(t)
                        gd.wait(t)
                        for f in range(2):
                            for k in range(16):
                                mm = t.matmul(q[f][:, :], ya_sb[:, k * 128:(k + 1) * 128],
                                              wo_sb[:, k * DM + f * 512:k * DM + (f + 1) * 512],
                                              start=(k == 0), stop=(k == 15))
                            ev.inc(mm)

                    @blk.vector
                    def _(v):
                        for f in range(2):
                            v.wait_ge(ev.h, ev_b + f + 1)
                            ws.inc(v.tensor_tensor(o1t[:, f * 512:(f + 1) * 512], q[f][:, :],
                                                   xt_sb[:, f * 512:(f + 1) * 512], OP.add))
                        ws.wait(v)
                        ws.inc(v.reduce_sum(lstt[:, 0:1], o1t[:, :], AX.X))

                    @blk.scalar
                    def _(s):
                        ws.wait(s)
                        sc.inc(s.activation(sqt[:, :], o1t[:, :], AF.Square, bias=0.0, scale=1.0))

                    @blk.vector
                    def _(v):
                        sc.wait(v)
                        ws.inc(v.reduce_sum(lstt[:, 1:2], sqt[:, :], AX.X))
                        ws.wait(v)
                        v.tensor_scalar(lstt[:, 2:3], lstt[:, 0:1], 1.0 / DM, None, OP.mult)
                        ws.inc(v.tensor_scalar(lstt[:, 3:4], lstt[:, 1:2], 1.0 / DM, None, OP.mult))
                        ws.wait(v)
                        ws.inc(v.tensor_tensor(lstt[:, 4:5], lstt[:, 2:3], lstt[:, 2:3], OP.mult))
                        ws.wait(v)
                        ws.inc(v.tensor_tensor(lstt[:, 5:6], lstt[:, 3:4], lstt[:, 4:5], OP.subtract))

                    @blk.scalar
                    def _(s):
                        ws.wait(s)
                        sc.inc(s.activation(lstt[:, 6:7], lstt[:, 5:6], AF.Ln, bias=EPS, scale=1.0))
                        sc.wait(s)
                        sc.inc(s.activation(lstt[:, 7:8], lstt[:, 6:7], AF.Exp, bias=0.0, scale=-0.5))

                    @blk.vector
                    def _(v):
                        sc.wait(v)
                        ws.inc(v.tensor_scalar(l2[:, :], o1t[:, :], lstt[:, 2:3], lstt[:, 7:8],
                                               OP.subtract, OP.mult))
                        ws.wait(v)
                        ws.inc(v.tensor_copy(l2b[:, :], l2[:, :]))

                    ev_b2 = ev.n
                    sc_b2 = sc.n

                    @blk.tensor
                    def _(t):
                        ws.wait(t)
                        for grp in range(2):
                            for j in range(4):
                                c = grp * 4 + j
                                mm = t.transpose(q[grp][:, j * 128:(j + 1) * 128],
                                                 l2[:, c * 128:(c + 1) * 128], idf_sb[:, :])
                            ev.inc(mm)

                    @blk.scalar
                    def _(s):
                        for grp in range(2):
                            s.wait_ge(ev.h, ev_b2 + grp + 1)
                            sc.inc(s.activation(l2f[:, grp * 512:(grp + 1) * 512], q[grp][:, :],
                                                AF.Copy, bias=0.0, scale=1.0))

                    ev_b3 = ev.n
                    ws_b3 = ws.n

                    @blk.tensor
                    def _(t):
                        t.wait_ge(sc.h, sc.n)
                        for k in range(8):
                            mm = t.matmul(q0[0:64, 0:128], ctrl_sb[:, k * 64:(k + 1) * 64],
                                          l2f[:, k * 128:(k + 1) * 128],
                                          start=(k == 0), stop=(k == 7))
                        ev.inc(mm)

                    @blk.vector
                    def _(v):
                        v.wait_ge(ev.h, ev_b3 + 1)
                        ws.inc(v.tensor_tensor(lgt[:, :], q0[0:64, 0:128], tie_sb[:, :], OP.add))
                        ws.wait(v)
                        lg3 = bass.AP(lgt, 0, [[128, 64], [16, 8], [1, 16]])
                        ws.inc(v.tensor_reduce(mx_sb[:, 0:8], lg3, AX.X, OP.max))
                        ws.wait(v)
                        for tt_ in range(16):
                            ins = v.tensor_copy(bass.AP(mx_sb, 8 + tt_, [[136, 64], [16, 8]]),
                                                mx_sb[:, 0:8])
                        ws.inc(ins)
                        ws.wait(v)
                        ws.inc(v.tensor_tensor(perm_sb[:, ti * 128:(ti + 1) * 128], lgt[:, :],
                                               mx_sb[:, 8:136], OP.is_equal))
                        ws.wait(v)
                        ws.inc(v.tensor_copy(prmb[:, :], perm_sb[:, ti * 128:(ti + 1) * 128]))

                    ev_b4 = ev.n
                    sc_b4 = sc.n

                    @blk.tensor
                    def _(t):
                        ws.wait(t)
                        ev.inc(t.transpose(qb0[0:128, 0:64], prmb[:, :], idb_sb[0:64, 0:64]))

                    @blk.scalar
                    def _(s):
                        s.wait_ge(ev.h, ev_b4 + 1)
                        sc.inc(s.activation(prT[:, :], qb0[0:128, 0:64], AF.Copy, bias=0.0, scale=1.0))

                    @blk.vector
                    def _(v):
                        v.wait_ge(sc.h, sc_b4 + 1)
                        for j in range(8):
                            ins = v.tensor_copy(bass.AP(prTE, j, [[512, 128], [8, 64]]), prT[:, :])
                        ws.inc(ins)
                        ws.wait(v)
                        ws.inc(v.tensor_tensor(prBD[:, :], prTE[:, :], mbd2_sb[:, :], OP.mult))

                    ev_b5 = ev.n
                    sc_b5 = sc.n

                    @blk.tensor
                    def _(t):
                        ws.wait(t)
                        for qq in range(4):
                            for f in range(2):
                                g = qq * 2 + f
                                if g >= 2:
                                    t.wait_ge(sc.h, sc_b5 + g - 1)
                                mm = t.matmul(q[g % 2][:, :], prBD[:, qq * 128:(qq + 1) * 128],
                                              l2b[:, f * 512:(f + 1) * 512], start=True, stop=True)
                                ev.inc(mm)

                    @blk.scalar
                    def _(s):
                        for g in range(8):
                            qq, f = g // 2, g % 2
                            s.wait_ge(ev.h, ev_b5 + g + 1)
                            sc.inc(s.activation(
                                xmt4[:, qq * DM + f * 512:qq * DM + (f + 1) * 512],
                                q[g % 2][:, :], AF.Copy, bias=0.0, scale=1.0))

                    @blk.sync
                    def _(sy):
                        sc.wait(sy)
                        st.dma(sy.dma_start(out=o1_sp[ti * 128:(ti + 1) * 128, :], in_=o1t[:, :]))
                        for qq in range(4):
                            st.dma(sy.dma_start(
                                out=xmt_sp[ti * 512 + qq * 128:ti * 512 + (qq + 1) * 128, :],
                                in_=xmt4[:, qq * DM:(qq + 1) * DM]))
                        st.wait(sy)

        _p6b(E, dict(locals()))
        _p6c(E, dict(locals()))

        if d_perm is not None:
            with nc.Block() as blk:
                @blk.sync
                def _(sy):
                    st.dma(sy.dma_start(out=d_perm[:, :], in_=perm_sb[:, :]))
                    st.wait(sy)


def _p6b(E, env):
    nc = E["nc"]
    ld, gd, st, ev, sc, ws = E["ld"], E["gd"], E["st"], E["ev"], E["sc"], E["ws"]
    f1t, f2m, fbias = E["f1t"], E["f2m"], E["fbias"]
    xmt_sp, int_sp = E["xmt_sp"], E["int_sp"]

    _es = contextlib.ExitStack()
    with _es:
        xmte = _es.enter_context(nc.sbuf_tensor("xmte", [128, 2048], BF16))
        xme = _es.enter_context(nc.sbuf_tensor("xme", [128, 8 * 256], BF16))
        f1es = _es.enter_context(nc.sbuf_tensor("f1es", [128, 512], BF16))
        f2es = _es.enter_context(nc.sbuf_tensor("f2es", [64, DM], BF16))
        fb_sb = _es.enter_context(nc.sbuf_tensor("fb_sb", [64, 1], F32))
        inner_sb = _es.enter_context(nc.sbuf_tensor("inner_sb", [64, 256], BF16))
        intr = _es.enter_context(nc.sbuf_tensor("intr", [128, 2 * DM], BF16))
        pb0 = _es.enter_context(nc.psum_tensor("pb0", [128, 512], BF16))
        pb1 = _es.enter_context(nc.psum_tensor("pb1", [128, 512], BF16))
        pf = _es.enter_context(nc.psum_tensor("pf", [128, 512], F32))
        pg0 = _es.enter_context(nc.psum_tensor("pg0", [128, 512], F32))
        pg1 = _es.enter_context(nc.psum_tensor("pg1", [128, 512], F32))
        pb = [pb0, pb1]
        pg = [pg0, pg1]
        for es in range(64):
            q = es // 16
            off = (q * 128 + (es % 16) * 8) * DM
            with nc.Block() as blk:
                @blk.sync
                def _(sy):
                    for half in range(2):
                        src = bass.AP(xmt_sp, off + half * 16 * 512 * DM,
                                      [[512 * DM, 16], [DM, 8], [1, DM]])
                        ld.dma(sy.dma_start(out=xmte[:, half * DM:(half + 1) * DM], in_=src))
                    ld.dma(sy.dma_start(out=fb_sb[:, 0:1],
                                        in_=bass.AP(fbias, es * 64, [[1, 64], [1, 1]])))

                @blk.gpsimd
                def _(g):
                    gd.dma(g.dma_start(out=f1es[:, :],
                                       in_=bass.AP(f1t, es * 128 * 512, [[512, 128], [1, 512]])))
                    gd.dma(g.dma_start(out=f2es[:, :],
                                       in_=bass.AP(f2m, es * 64 * DM, [[DM, 64], [1, DM]])))

                ev_b = ev.n
                sc_b = sc.n

                @blk.tensor
                def _(t):
                    ld.wait(t)
                    gd.wait(t)
                    for grp in range(4):
                        if grp >= 2:
                            t.wait_ge(sc.h, sc_b + (grp - 1) * 4)
                        for j in range(4):
                            g = grp * 4 + j
                            half, c = g // 8, g % 8
                            mm = t.transpose(pb[grp % 2][:, j * 128:(j + 1) * 128],
                                             xmte[:, half * DM + c * 128:half * DM + (c + 1) * 128],
                                             E["idb_sb"][:, :])
                        ev.inc(mm)

                @blk.scalar
                def _(s):
                    for grp in range(4):
                        s.wait_ge(ev.h, ev_b + grp + 1)
                        for j in range(4):
                            g = grp * 4 + j
                            half, c = g // 8, g % 8
                            sc.inc(s.activation(
                                xme[:, c * 256 + half * 128:c * 256 + (half + 1) * 128],
                                pb[grp % 2][:, j * 128:(j + 1) * 128],
                                AF.Copy, bias=0.0, scale=1.0))

                ev_b2 = ev.n
                sc_b2 = sc.n

                @blk.tensor
                def _(t):
                    t.wait_ge(sc.h, sc.n)
                    for c in range(8):
                        mm = t.matmul(pf[0:64, 0:256], f1es[:, c * 64:(c + 1) * 64],
                                      xme[:, c * 256:(c + 1) * 256],
                                      start=(c == 0), stop=(c == 7))
                    ev.inc(mm)

                @blk.scalar
                def _(s):
                    s.wait_ge(ev.h, ev_b2 + 1)
                    sc.inc(s.activation(inner_sb[:, :], pf[0:64, 0:256], AF.Relu,
                                        bias=fb_sb[:, 0:1], scale=1.0))

                ev_b3 = ev.n
                sc_b3 = sc.n

                @blk.tensor
                def _(t):
                    t.wait_ge(sc.h, sc.n)
                    for m in range(2):
                        for f in range(2):
                            g = m * 2 + f
                            if g >= 2:
                                t.wait_ge(sc.h, sc_b3 + g - 1)
                            mm = t.matmul(pg[g % 2][:, :], inner_sb[:, m * 128:(m + 1) * 128],
                                          f2es[:, f * 512:(f + 1) * 512], start=True, stop=True)
                            ev.inc(mm)

                @blk.scalar
                def _(s):
                    for g in range(4):
                        m, f = g // 2, g % 2
                        s.wait_ge(ev.h, ev_b3 + g + 1)
                        sc.inc(s.activation(intr[:, m * DM + f * 512:m * DM + (f + 1) * 512],
                                            pg[g % 2][:, :], AF.Copy, bias=0.0, scale=1.0))

                @blk.sync
                def _(sy):
                    sc.wait(sy)
                    for m in range(2):
                        st.dma(sy.dma_start(
                            out=int_sp[es * 256 + m * 128:es * 256 + (m + 1) * 128, :],
                            in_=intr[:, m * DM:(m + 1) * DM]))
                    st.wait(sy)


def _p6c(E, env):
    nc = E["nc"]
    ld, st, ev, sc, ws = E["ld"], E["st"], E["ev"], E["sc"], E["ws"]
    x_tok, out_tok, int_sp, o1_sp = E["x_tok"], E["out_tok"], E["int_sp"], E["o1_sp"]
    perm_sb = env["perm_sb"]
    esg_sb = env["esg_sb"]
    mbdt_sb = env["mbdt_sb"]

    _es = contextlib.ExitStack()
    with _es:
        pbdt = _es.enter_context(nc.sbuf_tensor("pbdt", [128, 4 * TTOK], BF16))
        scrE = _es.enter_context(nc.sbuf_tensor("scrE", [128, 512], F32))
        ir0 = _es.enter_context(nc.sbuf_tensor("ir0", [128, DM], BF16))
        ir1 = _es.enter_context(nc.sbuf_tensor("ir1", [128, DM], BF16))
        ir2 = _es.enter_context(nc.sbuf_tensor("ir2", [128, DM], BF16))
        ir3 = _es.enter_context(nc.sbuf_tensor("ir3", [128, DM], BF16))
        o1r = _es.enter_context(nc.sbuf_tensor("o1r", [128, DM], F32))
        outt = _es.enter_context(nc.sbuf_tensor("outt", [128, DM], F32))
        pp0 = _es.enter_context(nc.psum_tensor("pp0", [128, 512], F32))
        pp1 = _es.enter_context(nc.psum_tensor("pp1", [128, 512], F32))
        irs = [ir0, ir1, ir2, ir3]
        pp = [pp0, pp1]
        for esg in range(4):
            with nc.Block() as blk:
                ev_b = ev.n
                sc_b = sc.n
                ws_b = ws.n

                @blk.tensor
                def _(t):
                    for F in range(8):
                        if F >= 2:
                            t.wait_ge(sc.h, sc_b + F - 1)
                        mm = t.matmul(pp[F % 2][:, :], esg_sb[:, esg * 128:(esg + 1) * 128],
                                      perm_sb[:, F * 512:(F + 1) * 512], start=True, stop=True)
                        ev.inc(mm)

                @blk.scalar
                def _(s):
                    for F in range(8):
                        s.wait_ge(ev.h, ev_b + F + 1)
                        s.wait_ge(ws.h, ws_b + F)
                        sc.inc(s.activation(scrE[:, :], pp[F % 2][:, :], AF.Copy,
                                            bias=0.0, scale=1.0))

                @blk.vector
                def _(v):
                    for F in range(8):
                        v.wait_ge(sc.h, sc_b + F + 1)
                        ws.inc(v.tensor_tensor(pbdt[:, esg * TTOK + F * 512:esg * TTOK + (F + 1) * 512],
                                               scrE[:, :], mbdt_sb[:, F * 512 % 4096:(F * 512 % 4096) + 512],
                                               OP.mult))

        for ti in range(TTOK // 128):
            with nc.Block() as blk:
                @blk.sync
                def _(sy):
                    for esg in range(4):
                        src = bass.AP(int_sp, ((esg * 16) * 256 + ti * 8) * DM,
                                      [[256 * DM, 16], [DM, 8], [1, DM]])
                        ld.dma(sy.dma_start(out=irs[esg][:, :], in_=src))
                    ld.dma(sy.dma_start(out=o1r[:, :], in_=o1_sp[ti * 128:(ti + 1) * 128, :]))

                ev_b = ev.n

                @blk.tensor
                def _(t):
                    ld.wait(t)
                    for f in range(2):
                        for esg in range(4):
                            mm = t.matmul(pp[f][:, :], pbdt[:, esg * TTOK + ti * 128:esg * TTOK + (ti + 1) * 128],
                                          irs[esg][:, f * 512:(f + 1) * 512],
                                          start=(esg == 0), stop=(esg == 3))
                        ev.inc(mm)

                @blk.vector
                def _(v):
                    for f in range(2):
                        v.wait_ge(ev.h, ev_b + f + 1)
                        ws.inc(v.tensor_tensor(outt[:, f * 512:(f + 1) * 512], pp[f][:, :],
                                               o1r[:, f * 512:(f + 1) * 512], OP.add))

                @blk.sync
                def _(sy):
                    ws.wait(sy)
                    st.dma(sy.dma_start(out=out_tok[ti * 128:(ti + 1) * 128, :], in_=outt[:, :]))
                    st.wait(sy)


_NC_CACHE = {}


def _host_prep(inputs):
    x = np.ascontiguousarray(np.asarray(inputs["L0_attn_x"], dtype=np.float32))
    W_in = np.asarray(inputs["L0_attn_W_in"], dtype=np.float32)
    W_conv = np.asarray(inputs["L0_attn_W_conv"], dtype=np.float32)
    W_xproj = np.asarray(inputs["L0_attn_W_xproj"], dtype=np.float32)
    W_dt = np.asarray(inputs["L0_attn_W_dt"], dtype=np.float32)
    b_dt_a = np.asarray(inputs["L0_attn_b_dt"], dtype=np.float32)
    A_log = np.asarray(inputs["L0_attn_A_log"], dtype=np.float32)
    D_a = np.asarray(inputs["L0_attn_D"], dtype=np.float32)
    W_out = np.asarray(inputs["L0_attn_W_out"], dtype=np.float32)
    controller = np.asarray(inputs["L0_ffn_controller"], dtype=np.float32)
    f1 = np.asarray(inputs["L0_ffn_f1"], dtype=np.float32)
    f2 = np.asarray(inputs["L0_ffn_f2"], dtype=np.float32)
    fb = np.asarray(inputs["L0_ffn_bias"], dtype=np.float32)

    def ktiles(a):
        K, M = a.shape
        return np.ascontiguousarray(a.reshape(K // 128, 128, M).transpose(1, 0, 2).reshape(128, -1))

    w_out_h = ktiles(np.ascontiguousarray(W_out.T))
    ctrl_h = ktiles(controller.reshape(DM, 64))
    f1t_h = np.ascontiguousarray(
        f1.transpose(1, 2, 0, 3).reshape(64, DM, 64).reshape(64, 8, 128, 64)
        .transpose(0, 2, 1, 3).reshape(64, 128, 512))
    f2m_h = np.ascontiguousarray(f2.reshape(64, 64, DM))
    fb_h = np.ascontiguousarray(fb.reshape(64, 64))

    in_maps = []
    for core in range(8):
        b, g = core // 4, core % 4
        sl = slice(g * DC, (g + 1) * DC)
        d = dict(
            x_tok=np.ascontiguousarray(x[b, g * TTOK:(g + 1) * TTOK, :]),
            w_xc=ktiles(np.ascontiguousarray(W_in[sl].T)),
            w_z=ktiles(np.ascontiguousarray(W_in[DIN + g * DC:DIN + (g + 1) * DC].T)),
            w_conv=np.ascontiguousarray(W_conv[sl, 0, :]),
            w_xp=ktiles(np.ascontiguousarray(W_xproj[:, sl].T)),
            w_dt=np.ascontiguousarray(W_dt[sl].T),
            b_dt=np.ascontiguousarray(b_dt_a[sl, None]),
            a_log=np.ascontiguousarray(A_log[sl]),
            dvec=np.ascontiguousarray(D_a[sl, None]),
            w_out=w_out_h,
            ctrl=ctrl_h,
            f1t=f1t_h,
            f2m=f2m_h,
            fbias=fb_h,
            coff=(g * TTOK + 128 * np.arange(32, dtype=np.int32))[None, :],
        )
        in_maps.append(d)
    return in_maps


def run(inputs, dbg=(), upto="full"):
    key = (tuple(sorted(dbg)), upto)
    if key not in _NC_CACHE:
        _NC_CACHE[key] = build(dbg, upto)
    nc = _NC_CACHE[key]
    in_maps = _host_prep(inputs)
    res = run_bass_kernel_spmd(nc, in_maps, core_ids=list(range(8)))
    return res


def kernel(**inputs):
    res = run(inputs)
    out = np.empty((B, LSEQ, DM), np.float32)
    for core in range(8):
        b, g = core // 4, core % 4
        out[b, g * TTOK:(g + 1) * TTOK, :] = res.results[core]["out_tok"]
    return out


# version 04
# speedup vs baseline: 1.0694x; 1.0694x over previous
import contextlib
import numpy as np
import ml_dtypes

import concourse.bass as bass
import concourse.mybir as mybir
from concourse.bass_utils import run_bass_kernel_spmd

F32 = mybir.dt.float32
BF16 = mybir.dt.bfloat16
OP = mybir.AluOpType
AF = mybir.ActivationFunctionType
AX = mybir.AxisListType

B, LSEQ, DM = 2, 16384, 1024
DIN, NST, DTR = 2048, 16, 64
DC = 512
TTOK = 4096
NDT = DC // 128
P2C = 2048
NP2 = LSEQ // P2C
P4C = 512
NP4 = LSEQ // P4C
EPS = 1e-5
TIE_STEP = 1e-6 / 15.0
BF = ml_dtypes.bfloat16


class Sem:
    def __init__(self, nc, es, name):
        self.h = es.enter_context(nc.semaphore(name))
        self.n = 0

    def inc(self, instr, k=1):
        instr.then_inc(self.h, k)
        self.n += k

    def dma(self, instr):
        self.inc(instr, 16)

    def wait(self, eng):
        eng.wait_ge(self.h, self.n)


def build(dbg=(), upto="full"):
    nc = bass.Bass("TRN2", target_bir_lowering=False)
    E = {}

    x_tok = nc.dram_tensor("x_tok", [TTOK, DM], F32, kind="ExternalInput")
    w_xc = nc.dram_tensor("w_xc", [128, 8 * DC], F32, kind="ExternalInput")
    w_z = nc.dram_tensor("w_z", [128, 8 * DC], F32, kind="ExternalInput")
    w_conv = nc.dram_tensor("w_conv", [DC, 4], F32, kind="ExternalInput")
    w_xp = nc.dram_tensor("w_xp", [128, 4 * 96], F32, kind="ExternalInput")
    w_dt = nc.dram_tensor("w_dt", [64, DC], F32, kind="ExternalInput")
    b_dt = nc.dram_tensor("b_dt", [DC, 1], F32, kind="ExternalInput")
    a_log = nc.dram_tensor("a_log", [DC, NST], F32, kind="ExternalInput")
    dvec = nc.dram_tensor("dvec", [DC, 1], F32, kind="ExternalInput")
    w_out = nc.dram_tensor("w_out", [128, 16 * DM], F32, kind="ExternalInput")
    ctrl = nc.dram_tensor("ctrl", [128, 8 * 64], F32, kind="ExternalInput")
    f1t = nc.dram_tensor("f1t", [64, 128, 8 * 64], F32, kind="ExternalInput")
    f2m = nc.dram_tensor("f2m", [64, 64, DM], F32, kind="ExternalInput")
    fbias = nc.dram_tensor("fbias", [64, 64], F32, kind="ExternalInput")
    coff = nc.dram_tensor("coff", [1, 32], mybir.dt.int32, kind="ExternalInput")
    out_tok = nc.dram_tensor("out_tok", [TTOK, DM], F32, kind="ExternalOutput")

    ln1_my = nc.dram_tensor("ln1_my", [DM, TTOK], BF16)
    ln1_all = nc.dram_tensor("ln1_all", [4 * DM, TTOK], BF16)
    dbc_part = nc.dram_tensor("dbc_part", [96, LSEQ], F32)
    dbc_full = nc.dram_tensor("dbc_full", [96, LSEQ], F32)
    xc_sp = nc.dram_tensor("xc_sp", [DC, LSEQ], F32)
    ymid = nc.dram_tensor("ymid", [DC, LSEQ], BF16)
    y_all = nc.dram_tensor("y_all", [4 * DC, LSEQ], BF16)
    xmt_sp = nc.dram_tensor("xmt_sp", [32 * 512, DM], BF16)
    int_sp = nc.dram_tensor("int_sp", [64 * 256, DM], BF16)
    o1_sp = nc.dram_tensor("o1_sp", [TTOK, DM], F32)
    y_loc = nc.dram_tensor("y_loc", [4 * DC, TTOK], BF16)

    def dbg_t(name, shape, dtype=F32):
        if name in dbg:
            return nc.dram_tensor("dbg_" + name, shape, dtype, kind="ExternalOutput")
        return None

    d_ln = dbg_t("ln", [4 * DM, TTOK], BF16)
    d_xc = dbg_t("xc", [DC, LSEQ], F32)
    d_dbc = dbg_t("dbc", [96, LSEQ], F32)
    d_delta = dbg_t("delta", [DC, LSEQ], F32)
    d_ymid = dbg_t("ymid", [DC, LSEQ], BF16)
    d_o1 = dbg_t("o1", [TTOK, DM], F32)
    d_perm = dbg_t("perm", [64, TTOK], F32)
    d_xmt = dbg_t("xmt", [32 * 512, DM], BF16)
    d_int = dbg_t("int", [64 * 256, DM], BF16)

    es_ = contextlib.ExitStack()
    with es_:
        ld = Sem(nc, es_, "ld")
        gd = Sem(nc, es_, "gd")
        st = Sem(nc, es_, "st")
        ev = Sem(nc, es_, "ev")
        sc = Sem(nc, es_, "sc")
        ws = Sem(nc, es_, "ws")
        cc1 = Sem(nc, es_, "cc1")
        cc2 = Sem(nc, es_, "cc2")
        cc3 = Sem(nc, es_, "cc3")
        pc = Sem(nc, es_, "pc")
        E.update(locals())

        wxc_sb = es_.enter_context(nc.sbuf_tensor("wxc_sb", [128, 8 * DC], BF16))
        wz_sb = es_.enter_context(nc.sbuf_tensor("wz_sb", [128, 8 * DC], BF16))
        wxp_sb = es_.enter_context(nc.sbuf_tensor("wxp_sb", [128, 4 * 96], BF16))
        wdt_sb = es_.enter_context(nc.sbuf_tensor("wdt_sb", [64, DC], BF16))
        bdt_sb = es_.enter_context(nc.sbuf_tensor("bdt_sb", [128, NDT], F32))
        dv_sb = es_.enter_context(nc.sbuf_tensor("dv_sb", [128, NDT], F32))
        amat_sb = es_.enter_context(nc.sbuf_tensor("amat_sb", [128, NDT * NST], F32))
        wcv_sb = es_.enter_context(nc.sbuf_tensor("wcv_sb", [128, NDT * 4], F32))
        convd_sb = es_.enter_context(nc.sbuf_tensor("convd_sb", [128, NDT * 4 * 128], BF16))
        idf_sb = es_.enter_context(nc.sbuf_tensor("idf_sb", [128, 128], F32))
        idb_sb = es_.enter_context(nc.sbuf_tensor("idb_sb", [128, 128], BF16))
        sel16_sb = es_.enter_context(nc.sbuf_tensor("sel16_sb", [16, 16 * 128], F32))
        ppi_sb = es_.enter_context(nc.sbuf_tensor("ppi_sb", [128, 2], F32))
        cst_sb = es_.enter_context(nc.sbuf_tensor("cst_sb", [128, 4], F32))
        nc.const_aps.register(F32, EPS, cst_sb[:, 0:1])
        nc.const_aps.register(F32, 1.0, cst_sb[:, 1:2])
        nc.const_aps.register(F32, -1.0, cst_sb[:, 2:3])
        E.update(locals())

        with nc.Block() as blk:
            @blk.gpsimd
            def _(g):
                for s_ in (ld, gd, st, ev, sc, ws, cc1, cc2, cc3, pc):
                    g.sem_clear(s_.h)

        with nc.Block() as blk:
            @blk.sync
            def _(sy):
                for t in range(NDT):
                    ld.dma(sy.dma_start(out=bdt_sb[:, t:t + 1], in_=b_dt[t * 128:(t + 1) * 128, :]))
                    ld.dma(sy.dma_start(out=dv_sb[:, t:t + 1], in_=dvec[t * 128:(t + 1) * 128, :]))
                    ld.dma(sy.dma_start(out=amat_sb[:, t * NST:(t + 1) * NST],
                                        in_=a_log[t * 128:(t + 1) * 128, :]))
                    ld.dma(sy.dma_start(out=wcv_sb[:, t * 4:(t + 1) * 4],
                                        in_=w_conv[t * 128:(t + 1) * 128, :]))
                ld.wait(sy)

            @blk.vector
            def _(v):
                v.memset(cst_sb[:, 0:1], EPS)
                v.memset(cst_sb[:, 1:2], 1.0)
                v.memset(cst_sb[:, 2:3], -1.0)
                ws.inc(v.memset(cst_sb[:, 3:4], 0.0))

            @blk.gpsimd
            def _(g):
                gd.dma(g.dma_start(out=wxc_sb[:, :], in_=w_xc[:, :]))
                gd.dma(g.dma_start(out=wz_sb[:, :], in_=w_z[:, :]))
                gd.dma(g.dma_start(out=wxp_sb[:, :], in_=w_xp[:, :]))
                gd.dma(g.dma_start(out=wdt_sb[:, :], in_=w_dt[:, :]))
                gd.wait(g)
                g.iota(ppi_sb[:, 0:1], [[0, 1]], channel_multiplier=1,
                       allow_small_or_imprecise_dtypes=True)
                g.iota(idf_sb[:, :], [[1, 128]], channel_multiplier=0,
                       allow_small_or_imprecise_dtypes=True)
                g.iota(sel16_sb[:, :], [[1, 16], [0, 128]], channel_multiplier=0,
                       allow_small_or_imprecise_dtypes=True)

        with nc.Block() as blk:
            @blk.scalar
            def _(s):
                s.activation(amat_sb[:, :], amat_sb[:, :], AF.Exp, bias=0.0, scale=1.0)

        with nc.Block() as blk:
            @blk.vector
            def _(v):
                v.tensor_scalar(amat_sb[:, :], amat_sb[:, :], -1.0, None, OP.mult)
                v.tensor_scalar(idf_sb[:, :], idf_sb[:, :], ppi_sb[:, 0:1], None, OP.is_equal)
                ws.inc(v.tensor_scalar(sel16_sb[:, :], sel16_sb[:, :], ppi_sb[0:16, 0:1],
                                       None, OP.is_equal))

            @blk.vector
            def _(v):
                ws.wait(v)
                v.tensor_copy(idb_sb[:, :], idf_sb[:, :])
                for dt in range(NDT):
                    for tap in range(4):
                        v.scalar_tensor_tensor(
                            convd_sb[:, (dt * 4 + tap) * 128:(dt * 4 + tap + 1) * 128],
                            idf_sb[:, :], wcv_sb[:, dt * 4 + tap:dt * 4 + tap + 1],
                            idf_sb[:, :], OP.mult, OP.bypass)

        _es = contextlib.ExitStack()
        with _es:
            xt8 = _es.enter_context(nc.sbuf_tensor("xt8", [128, 8 * DM], F32))
            sq8 = _es.enter_context(nc.sbuf_tensor("sq8", [128, 8 * DM], F32))
            ln8 = _es.enter_context(nc.sbuf_tensor("ln8", [128, 8 * DM], BF16))
            lnT8 = _es.enter_context(nc.sbuf_tensor("lnT8", [128, 8 * DM], BF16))
            lst = _es.enter_context(nc.sbuf_tensor("lst", [128, 64], F32))
            psT0 = _es.enter_context(nc.psum_tensor("psT0", [128, 512], BF16))
            psT1 = _es.enter_context(nc.psum_tensor("psT1", [128, 512], BF16))
            psT = [psT0, psT1]
            for tb in range(TTOK // 128 // 8):
                with nc.Block() as blk:
                    @blk.sync
                    def _(sy):
                        for i in range(8):
                            ti = tb * 8 + i
                            ld.dma(sy.dma_start(out=xt8[:, i * DM:(i + 1) * DM],
                                                in_=x_tok[ti * 128:(ti + 1) * 128, :]))

                    @blk.scalar
                    def _(s):
                        ld.wait(s)
                        for i in range(8):
                            sc.inc(s.activation(sq8[:, i * DM:(i + 1) * DM],
                                                xt8[:, i * DM:(i + 1) * DM],
                                                AF.Square, bias=0.0, scale=1.0))

                    @blk.vector
                    def _(v):
                        ld.wait(v)
                        for i in range(8):
                            ins = v.reduce_sum(lst[:, i * 8:i * 8 + 1], xt8[:, i * DM:(i + 1) * DM],
                                               AX.X)
                        sc.wait(v)
                        for i in range(8):
                            ins = v.reduce_sum(lst[:, i * 8 + 1:i * 8 + 2],
                                               sq8[:, i * DM:(i + 1) * DM], AX.X)
                        ws.inc(ins)
                        ws.wait(v)
                        for i in range(8):
                            c = i * 8
                            ins = v.tensor_scalar(lst[:, c + 2:c + 3], lst[:, c:c + 1],
                                                  1.0 / DM, None, OP.mult)
                            ins = v.tensor_scalar(lst[:, c + 3:c + 4], lst[:, c + 1:c + 2],
                                                  1.0 / DM, None, OP.mult)
                        ws.inc(ins)
                        ws.wait(v)
                        for i in range(8):
                            c = i * 8
                            ins = v.tensor_tensor(lst[:, c + 4:c + 5], lst[:, c + 2:c + 3],
                                                  lst[:, c + 2:c + 3], OP.mult)
                        ws.inc(ins)
                        ws.wait(v)
                        for i in range(8):
                            c = i * 8
                            ins = v.tensor_tensor(lst[:, c + 5:c + 6], lst[:, c + 3:c + 4],
                                                  lst[:, c + 4:c + 5], OP.subtract)
                        ws.inc(ins)

                    @blk.scalar
                    def _(s):
                        ws.wait(s)
                        for i in range(8):
                            c = i * 8
                            ins = s.activation(lst[:, c + 6:c + 7], lst[:, c + 5:c + 6],
                                               AF.Ln, bias=EPS, scale=1.0)
                        sc.inc(ins)
                        sc.wait(s)
                        for i in range(8):
                            c = i * 8
                            sc.inc(s.activation(lst[:, c + 7:c + 8], lst[:, c + 6:c + 7],
                                                AF.Exp, bias=0.0, scale=-0.5))

                    @blk.vector
                    def _(v):
                        sc.wait(v)
                        for i in range(8):
                            c = i * 8
                            ws.inc(v.tensor_scalar(ln8[:, i * DM:(i + 1) * DM],
                                                   xt8[:, i * DM:(i + 1) * DM],
                                                   lst[:, c + 2:c + 3], lst[:, c + 7:c + 8],
                                                   OP.subtract, OP.mult))

                    ev_b = ev.n
                    sc_b = sc.n

                    @blk.tensor
                    def _(t):
                        ws.wait(t)
                        for grp in range(16):
                            if grp >= 2:
                                t.wait_ge(sc.h, sc_b + grp - 1)
                            for j in range(4):
                                g = grp * 4 + j
                                i, c = g // 8, g % 8
                                mm = t.transpose(psT[grp % 2][:, j * 128:(j + 1) * 128],
                                                 ln8[:, i * DM + c * 128:i * DM + (c + 1) * 128],
                                                 idb_sb[:, :])
                            ev.inc(mm)

                    @blk.scalar
                    def _(s):
                        for grp in range(16):
                            s.wait_ge(ev.h, ev_b + grp + 1)
                            g0 = grp * 4
                            i = g0 // 8
                            c = g0 % 8
                            sc.inc(s.activation(lnT8[:, i * DM + c * 128:i * DM + (c + 4) * 128],
                                                psT[grp % 2][:, :], AF.Copy, bias=0.0, scale=1.0))

                    @blk.sync
                    def _(sy):
                        sc.wait(sy)
                        for i in range(8):
                            ti = tb * 8 + i
                            dst = bass.AP(ln1_my, ti * 128, [[TTOK, 128], [128 * TTOK, 8], [1, 128]])
                            src = bass.AP(lnT8, i * DM, [[8 * DM, 128], [128, 8], [1, 128]])
                            st.dma(sy.dma_start(out=dst, in_=src))
                        st.wait(sy)

        with nc.Block() as blk:
            @blk.gpsimd
            def _(g):
                cc1.inc(g.collective_compute(
                    "AllGather", OP.bypass,
                    replica_groups=[[0, 1, 2, 3], [4, 5, 6, 7]],
                    ins=[ln1_my.ap().opt()], outs=[ln1_all.ap().opt()]))
                g.wait_ge(cc1.h, 1)

        if upto != "p1":
            _es = contextlib.ExitStack()
            with _es:
                rhs8 = _es.enter_context(nc.sbuf_tensor("rhs8", [128, 8 * P2C], BF16))
                xcpre = _es.enter_context(nc.sbuf_tensor("xcpre", [128, NDT * (P2C + 3)], BF16))
                xcf = _es.enter_context(nc.sbuf_tensor("xcf", [128, NDT * P2C], F32))
                xcb = _es.enter_context(nc.sbuf_tensor("xcb", [128, NDT * P2C], BF16))
                dbc_sb = _es.enter_context(nc.sbuf_tensor("dbc_sb", [96, P2C], F32))
                psA0 = _es.enter_context(nc.psum_tensor("psA0", [128, 512], F32))
                psA1 = _es.enter_context(nc.psum_tensor("psA1", [128, 512], F32))
                psB0 = _es.enter_context(nc.psum_tensor("psB0", [128, 512], F32))
                psB1 = _es.enter_context(nc.psum_tensor("psB1", [128, 512], F32))
                psC = _es.enter_context(nc.psum_tensor("psC", [128, 512], F32))
                psA = [psA0, psA1]
                psB = [psB0, psB1]
                XP3 = P2C + 3
                for ch in range(NP2):
                    rank = (ch * P2C) // TTOK
                    col0 = ch * P2C - rank * TTOK
                    with nc.Block() as blk:
                        @blk.sync
                        def _(sy):
                            for cc in range(8):
                                src = bass.AP(ln1_all, (rank * DM + cc * 128) * TTOK + col0,
                                              [[TTOK, 128], [1, P2C]])
                                ld.dma(sy.dma_start(out=rhs8[:, cc * P2C:(cc + 1) * P2C], in_=src))

                        @blk.vector
                        def _(v):
                            for dt in range(NDT):
                                o = dt * XP3
                                if ch == 0:
                                    ws.inc(v.memset(xcpre[:, o:o + 3], 0.0))
                                else:
                                    ws.inc(v.tensor_copy(xcpre[:, o:o + 3],
                                                         xcpre[:, o + P2C:o + P2C + 3]))

                        ev_b = ev.n
                        sc_b = sc.n

                        @blk.tensor
                        def _(t):
                            ld.wait(t)
                            ws.wait(t)
                            for dt in range(NDT):
                                for f in range(4):
                                    g = dt * 4 + f
                                    if g >= 2:
                                        t.wait_ge(sc.h, sc_b + g - 1)
                                    for k in range(8):
                                        mm = t.matmul(psA[g % 2][:, :],
                                                      wxc_sb[:, k * DC + dt * 128:k * DC + (dt + 1) * 128],
                                                      rhs8[:, k * P2C + f * 512:k * P2C + (f + 1) * 512],
                                                      start=(k == 0), stop=(k == 7))
                                    ev.inc(mm)
                            for dt in range(NDT):
                                for f in range(4):
                                    g2 = dt * 4 + f
                                    t.wait_ge(sc.h, sc_b + (dt + 1) * 4)
                                    if g2 >= 2:
                                        t.wait_ge(sc.h, sc_b + 16 + g2 - 1)
                                    o = dt * XP3 + f * 512
                                    for tap in range(4):
                                        mm = t.matmul(psB[g2 % 2][:, :],
                                                      convd_sb[:, (dt * 4 + tap) * 128:(dt * 4 + tap + 1) * 128],
                                                      xcpre[:, o + tap:o + tap + 512],
                                                      start=(tap == 0), stop=(tap == 3))
                                    ev.inc(mm)

                        @blk.scalar
                        def _(s):
                            for g in range(16):
                                dt, f = g // 4, g % 4
                                s.wait_ge(ev.h, ev_b + g + 1)
                                sc.inc(s.activation(
                                    xcpre[:, dt * XP3 + 3 + f * 512:dt * XP3 + 3 + (f + 1) * 512],
                                    psA[g % 2][:, :], AF.Copy, bias=0.0, scale=1.0))
                            for g2 in range(16):
                                dt, f = g2 // 4, g2 % 4
                                s.wait_ge(ev.h, ev_b + 16 + g2 + 1)
                                sc.inc(s.activation(xcf[:, dt * P2C + f * 512:dt * P2C + (f + 1) * 512],
                                                    psB[g2 % 2][:, :], AF.Silu, bias=0.0, scale=1.0))

                        @blk.vector
                        def _(v):
                            for g2 in range(16):
                                dt, f = g2 // 4, g2 % 4
                                v.wait_ge(sc.h, sc_b + 16 + g2 + 1)
                                ws.inc(v.tensor_copy(xcb[:, dt * P2C + f * 512:dt * P2C + (f + 1) * 512],
                                                     xcf[:, dt * P2C + f * 512:dt * P2C + (f + 1) * 512]))

                    with nc.Block() as blk:
                        ev_b2 = ev.n
                        sc_b2 = sc.n

                        @blk.tensor
                        def _(t):
                            for f in range(4):
                                if f >= 1:
                                    t.wait_ge(sc.h, sc_b2 + f)
                                for k in range(NDT):
                                    mm = t.matmul(psC[0:96, :],
                                                  wxp_sb[:, k * 96:(k + 1) * 96],
                                                  xcb[:, k * P2C + f * 512:k * P2C + (f + 1) * 512],
                                                  start=(k == 0), stop=(k == NDT - 1))
                                ev.inc(mm)

                        @blk.scalar
                        def _(s):
                            for f in range(4):
                                s.wait_ge(ev.h, ev_b2 + f + 1)
                                sc.inc(s.activation(dbc_sb[:, f * 512:(f + 1) * 512],
                                                    psC[0:96, :], AF.Copy, bias=0.0, scale=1.0))

                        @blk.sync
                        def _(sy):
                            sc.wait(sy)
                            dst = bass.AP(dbc_part, ch * P2C, [[LSEQ, 96], [1, P2C]])
                            st.dma(sy.dma_start(out=dst, in_=dbc_sb[:, :]))
                            for dt in range(NDT):
                                dstx = bass.AP(xc_sp, dt * 128 * LSEQ + ch * P2C, [[LSEQ, 128], [1, P2C]])
                                st.dma(sy.dma_start(out=dstx, in_=xcf[:, dt * P2C:(dt + 1) * P2C]))
                            st.wait(sy)

        if upto not in ("p1", "p2"):
            with nc.Block() as blk:
                @blk.gpsimd
                def _(g):
                    cc2.inc(g.collective_compute(
                        "AllReduce", OP.add,
                        replica_groups=[[0, 1, 2, 3], [4, 5, 6, 7]],
                        ins=[dbc_part.ap().opt()], outs=[dbc_full.ap().opt()]))
                    g.wait_ge(cc2.h, 1)

        E.update(dict(x_tok=x_tok, w_out=w_out, ctrl=ctrl, f1t=f1t, f2m=f2m, fbias=fbias, coff=coff, y_loc=y_loc,
                      out_tok=out_tok, ln1_all=ln1_all, dbc_full=dbc_full, xc_sp=xc_sp,
                      ymid=ymid, y_all=y_all, xmt_sp=xmt_sp, int_sp=int_sp, o1_sp=o1_sp,
                      d_delta=d_delta, d_perm=d_perm, nc=nc))

        if upto in ("p4", "p5", "full"):
            _p4(E)
        if upto in ("p5", "full"):
            with nc.Block() as blk:
                @blk.gpsimd
                def _(g):
                    cc3.inc(g.collective_compute(
                        "AllGather", OP.bypass,
                        replica_groups=[[0, 1, 2, 3], [4, 5, 6, 7]],
                        ins=[ymid.ap().opt()], outs=[y_all.ap().opt()]))
                    g.wait_ge(cc3.h, 1)
        if upto == "full":
            _p6(E)
        else:
            with (
                nc.sbuf_tensor("zz", [128, DM], F32) as zz,
                nc.Block() as blk,
            ):
                @blk.vector
                def _(v):
                    ws.inc(v.memset(zz[:, :], 0.0))

                @blk.sync
                def _(sy):
                    ws.wait(sy)
                    for ti in range(TTOK // 128):
                        st.dma(sy.dma_start(out=out_tok[ti * 128:(ti + 1) * 128, :], in_=zz[:, :]))
                    st.wait(sy)

        if dbg:
            with nc.Block() as blk:
                @blk.sync
                def _(sy):
                    for name, dd, src in (("ln", d_ln, ln1_all), ("xc", d_xc, xc_sp),
                                          ("dbc", d_dbc, dbc_full), ("ymid", d_ymid, ymid),
                                          ("o1", d_o1, o1_sp), ("xmt", d_xmt, xmt_sp),
                                          ("int", d_int, int_sp)):
                        if dd is not None:
                            st.dma(sy.dma_start(out=dd[:, :], in_=src[:, :]))
                    st.wait(sy)
        with nc.Block() as blk:
            @blk.gpsimd
            def _(g):
                for s_ in (ld, gd, st, ev, sc, ws, cc1, cc2, cc3, pc):
                    g.sem_clear(s_.h)
    return nc


def _p4(E):
    nc = E["nc"]
    ld, st, ev, sc, ws, pc = E["ld"], E["st"], E["ev"], E["sc"], E["ws"], E["pc"]
    wdt_sb, bdt_sb, dv_sb, amat_sb, sel16_sb, wz_sb = (
        E["wdt_sb"], E["bdt_sb"], E["dv_sb"], E["amat_sb"], E["sel16_sb"], E["wz_sb"])
    ln1_all, dbc_full, xc_sp, ymid, d_delta = (
        E["ln1_all"], E["dbc_full"], E["xc_sp"], E["ymid"], E["d_delta"])
    C = P4C
    NH = NST // 2

    _es = contextlib.ExitStack()
    with _es:
        rz8 = _es.enter_context(nc.sbuf_tensor("rz8", [128, 8 * C], BF16))
        dltb = _es.enter_context(nc.sbuf_tensor("dltb", [64, C], BF16))
        dltf = _es.enter_context(nc.sbuf_tensor("dltf", [64, C], F32))
        bm_sb = _es.enter_context(nc.sbuf_tensor("bm_sb", [16, C], F32))
        cm_sb = _es.enter_context(nc.sbuf_tensor("cm_sb", [16, C], F32))
        delta = _es.enter_context(nc.sbuf_tensor("delta", [128, NDT * C], F32))
        xc4 = _es.enter_context(nc.sbuf_tensor("xc4", [128, NDT * C], F32))
        u4 = _es.enter_context(nc.sbuf_tensor("u4", [128, NDT * C], F32))
        gz4 = _es.enter_context(nc.sbuf_tensor("gz4", [128, NDT * C], F32))
        y4 = _es.enter_context(nc.sbuf_tensor("y4", [128, NDT * C], F32))
        yh4 = _es.enter_context(nc.sbuf_tensor("yh4", [128, NDT * C], F32))
        ymb = _es.enter_context(nc.sbuf_tensor("ymb", [128, NDT * C], BF16))
        bmb = _es.enter_context(nc.sbuf_tensor("bmb", [128, NH * C], F32))
        cmb = _es.enter_context(nc.sbuf_tensor("cmb", [128, NH * C], F32))
        hbuf = _es.enter_context(nc.sbuf_tensor("hbuf", [128, NH * C], F32))
        hc8 = _es.enter_context(nc.sbuf_tensor("hc8", [128, NH * C], F32))
        abuf8 = _es.enter_context(nc.sbuf_tensor("abuf8", [128, 8 * C], F32))
        bbuf8 = _es.enter_context(nc.sbuf_tensor("bbuf8", [128, 8 * C], F32))
        hstate = _es.enter_context(nc.sbuf_tensor("hstate", [128, NDT * NST], F32))
        ps0 = _es.enter_context(nc.psum_tensor("ps0", [128, 512], F32))
        ps1 = _es.enter_context(nc.psum_tensor("ps1", [128, 512], F32))
        ps2 = _es.enter_context(nc.psum_tensor("ps2", [128, 512], F32))
        ps3 = _es.enter_context(nc.psum_tensor("ps3", [128, 512], F32))
        ps = [ps0, ps1, ps2, ps3]
        with nc.Block() as blk:
            @blk.vector
            def _(v):
                v.memset(hstate[:, :], 0.0)

        for ch in range(NP4):
            rank = (ch * C) // TTOK
            col0 = ch * C - rank * TTOK
            with nc.Block() as blk:
                @blk.sync
                def _(sy):
                    for cc in range(8):
                        src = bass.AP(ln1_all, (rank * DM + cc * 128) * TTOK + col0,
                                      [[TTOK, 128], [1, C]])
                        ld.dma(sy.dma_start(out=rz8[:, cc * C:(cc + 1) * C], in_=src))
                    ld.dma(sy.dma_start(out=dltf[:, :],
                                        in_=bass.AP(dbc_full, ch * C, [[LSEQ, 64], [1, C]])))
                    ld.dma(sy.dma_start(out=bm_sb[:, :],
                                        in_=bass.AP(dbc_full, 64 * LSEQ + ch * C, [[LSEQ, 16], [1, C]])))
                    ld.dma(sy.dma_start(out=cm_sb[:, :],
                                        in_=bass.AP(dbc_full, 80 * LSEQ + ch * C, [[LSEQ, 16], [1, C]])))
                    for dt in range(NDT):
                        ld.dma(sy.dma_start(out=xc4[:, dt * C:(dt + 1) * C],
                                            in_=bass.AP(xc_sp, dt * 128 * LSEQ + ch * C,
                                                        [[LSEQ, 128], [1, C]])))

                @blk.vector
                def _(v):
                    ld.wait(v)
                    ws.inc(v.tensor_copy(dltb[:, :], dltf[:, :]))

                ev_b = ev.n
                sc_b = sc.n
                groups = [("z", dt) for dt in range(NDT)] + [("d", dt) for dt in range(NDT)]

                @blk.tensor
                def _(t):
                    ld.wait(t)
                    ws.wait(t)
                    for g, (kind, dt) in enumerate(groups):
                        if g >= 4:
                            t.wait_ge(sc.h, sc_b + g - 3)
                        if kind == "z":
                            for k in range(8):
                                mm = t.matmul(ps[g % 4][:, :],
                                              wz_sb[:, k * DC + dt * 128:k * DC + (dt + 1) * 128],
                                              rz8[:, k * C:(k + 1) * C],
                                              start=(k == 0), stop=(k == 7))
                        else:
                            mm = t.matmul(ps[g % 4][:, :], wdt_sb[:, dt * 128:(dt + 1) * 128],
                                          dltb[:, :], start=True, stop=True)
                        ev.inc(mm)

                @blk.scalar
                def _(s):
                    for g, (kind, dt) in enumerate(groups):
                        s.wait_ge(ev.h, ev_b + g + 1)
                        if kind == "z":
                            sc.inc(s.activation(gz4[:, dt * C:(dt + 1) * C], ps[g % 4][:, :],
                                                AF.Silu, bias=0.0, scale=1.0))
                        else:
                            sc.inc(s.activation(delta[:, dt * C:(dt + 1) * C], ps[g % 4][:, :],
                                                AF.Identity, bias=bdt_sb[:, dt:dt + 1], scale=1.0))

            with nc.Block() as blk:
                @blk.scalar
                def _(s):
                    for dt in range(NDT):
                        sl = slice(dt * C, (dt + 1) * C)
                        ins = s.activation(y4[:, sl], delta[:, sl], AF.Abs, bias=0.0, scale=1.0)
                    sc.inc(ins)
                    sc.wait(s)
                    for dt in range(NDT):
                        sl = slice(dt * C, (dt + 1) * C)
                        ins = s.activation(yh4[:, sl], y4[:, sl], AF.Exp, bias=0.0, scale=-1.0)
                    sc.inc(ins)
                    sc.wait(s)
                    for dt in range(NDT):
                        sl = slice(dt * C, (dt + 1) * C)
                        ins = s.activation(y4[:, sl], yh4[:, sl], AF.Ln, bias=1.0, scale=1.0)
                    sc.inc(ins)

                @blk.vector
                def _(v):
                    sc.wait(v)
                    for dt in range(NDT):
                        sl = slice(dt * C, (dt + 1) * C)
                        ins = v.scalar_tensor_tensor(delta[:, sl], delta[:, sl], 0.0, y4[:, sl],
                                                     OP.max, OP.add)
                    ws.inc(ins)
                    ws.wait(v)
                    for dt in range(NDT):
                        sl = slice(dt * C, (dt + 1) * C)
                        ins = v.tensor_tensor(u4[:, sl], delta[:, sl], xc4[:, sl], OP.mult)
                    ws.inc(ins)

            if d_delta is not None:
                with nc.Block() as blk:
                    @blk.sync
                    def _(sy):
                        for dt in range(NDT):
                            dstd = bass.AP(d_delta, dt * 128 * LSEQ + ch * C, [[LSEQ, 128], [1, C]])
                            st.dma(sy.dma_start(out=dstd, in_=delta[:, dt * C:(dt + 1) * C]))
                        st.wait(sy)

            for half in range(2):
                with nc.Block() as blk:
                    ev_b = ev.n
                    sc_b = sc.n
                    ws_b = ws.n
                    pc_b = pc.n
                    rgroups = [("b", n) for n in range(NH)] + [("c", n) for n in range(NH)]

                    @blk.tensor
                    def _(t):
                        for g, (kind, n) in enumerate(rgroups):
                            gn = half * NH + n
                            if g >= 4:
                                t.wait_ge(sc.h, sc_b + g - 3)
                            src_ = bm_sb if kind == "b" else cm_sb
                            mm = t.matmul(ps[g % 4][:, :], sel16_sb[:, gn * 128:(gn + 1) * 128],
                                          src_[:, :], start=True, stop=True)
                            ev.inc(mm)

                    @blk.scalar
                    def _(s):
                        for g, (kind, n) in enumerate(rgroups):
                            s.wait_ge(ev.h, ev_b + g + 1)
                            dstt = bmb if kind == "b" else cmb
                            sc.inc(s.activation(dstt[:, n * C:(n + 1) * C], ps[g % 4][:, :],
                                                AF.Copy, bias=0.0, scale=1.0))
                        for dt in range(NDT):
                            if dt >= 1:
                                s.wait_ge(ws.h, ws_b + (dt - 1) * 11 + 8)
                            for n in range(NH):
                                gn = half * NH + n
                                sc.inc(s.activation(abuf8[:, n * C:(n + 1) * C],
                                                    delta[:, dt * C:(dt + 1) * C], AF.Exp,
                                                    bias=0.0,
                                                    scale=amat_sb[:, dt * NST + gn:dt * NST + gn + 1]))

                    @blk.gpsimd
                    def _(g):
                        for dt in range(NDT):
                            g.wait_ge(sc.h, sc_b + 16)
                            if dt >= 1:
                                g.wait_ge(ws.h, ws_b + (dt - 1) * 11 + 8)
                            for n in range(NH):
                                pc.inc(g.tensor_tensor(bbuf8[:, n * C:(n + 1) * C],
                                                       u4[:, dt * C:(dt + 1) * C],
                                                       bmb[:, n * C:(n + 1) * C], OP.mult))


                    @blk.vector
                    def _(v):
                        for dt in range(NDT):
                            for n in range(NH):
                                gn = half * NH + n
                                v.wait_ge(sc.h, sc_b + 16 + dt * NH + n + 1)
                                v.wait_ge(pc.h, pc_b + dt * NH + n + 1)
                                ws.inc(v.tensor_tensor_scan(
                                    hbuf[:, n * C:(n + 1) * C],
                                    abuf8[:, n * C:(n + 1) * C],
                                    bbuf8[:, n * C:(n + 1) * C],
                                    hstate[:, dt * NST + gn:dt * NST + gn + 1],
                                    OP.mult, OP.add))
                            v.wait_ge(ws.h, ws.n)
                            lastc = bass.AP(hbuf, C - 1, [[NH * C, 128], [C, NH]])
                            ws.inc(v.tensor_copy(
                                bass.AP(hstate, dt * NST + half * NH, [[NDT * NST, 128], [1, NH]]),
                                lastc))
                            v.wait_ge(ws.h, ws.n)
                            ws.inc(v.tensor_tensor(hc8[:, :], hbuf[:, :], cmb[:, :], OP.mult))
                            v.wait_ge(ws.h, ws.n)
                            h3 = bass.AP(hc8, 0, [[NH * C, 128], [1, C], [C, NH]])
                            dst_y = y4 if half == 0 else yh4
                            ws.inc(v.tensor_reduce(dst_y[:, dt * C:(dt + 1) * C], h3, AX.X, OP.add))
                            v.wait_ge(ws.h, ws.n)
            with nc.Block() as blk:
                @blk.vector
                def _(v):
                    for dt in range(NDT):
                        sl = slice(dt * C, (dt + 1) * C)
                        ins = v.tensor_tensor(y4[:, sl], y4[:, sl], yh4[:, sl], OP.add)
                    ws.inc(ins)
                    ws.wait(v)
                    for dt in range(NDT):
                        sl = slice(dt * C, (dt + 1) * C)
                        ins = v.scalar_tensor_tensor(y4[:, sl], xc4[:, sl], dv_sb[:, dt:dt + 1],
                                                     y4[:, sl], OP.mult, OP.add)
                    ws.inc(ins)
                    ws.wait(v)
                    for dt in range(NDT):
                        sl = slice(dt * C, (dt + 1) * C)
                        ws.inc(v.tensor_tensor(ymb[:, sl], y4[:, sl], gz4[:, sl], OP.mult))

                @blk.sync
                def _(sy):
                    ws.wait(sy)
                    for dt in range(NDT):
                        dsty = bass.AP(ymid, dt * 128 * LSEQ + ch * C, [[LSEQ, 128], [1, C]])
                        st.dma(sy.dma_start(out=dsty, in_=ymb[:, dt * C:(dt + 1) * C]))
                    st.wait(sy)


def _p6(E):
    nc = E["nc"]
    ld, gd, st, ev, sc, ws = E["ld"], E["gd"], E["st"], E["ev"], E["sc"], E["ws"]
    idf_sb, idb_sb, ppi_sb = E["idf_sb"], E["idb_sb"], E["ppi_sb"]
    x_tok, w_out, ctrl, f1t, f2m, fbias, out_tok = (
        E["x_tok"], E["w_out"], E["ctrl"], E["f1t"], E["f2m"], E["fbias"], E["out_tok"])
    coff = E["coff"]
    y_loc = E["y_loc"]

    with nc.Block() as blk:
        @blk.gpsimd
        def _(g):
            pid = g.partition_id()
            goff_v = (pid % 4) * TTOK
            gd.dma(g.dma_start(
                out=y_loc.ap(),
                in_=bass.AP(E["y_all"], goff_v, [[LSEQ, 4 * DC], [1, TTOK]])))
            gd.wait(g)
    y_all, xmt_sp, int_sp, o1_sp, d_perm = (
        E["y_all"], E["xmt_sp"], E["int_sp"], E["o1_sp"], E["d_perm"])

    _es = contextlib.ExitStack()
    with _es:
        wo_sb = _es.enter_context(nc.sbuf_tensor("wo_sb", [128, 16 * DM], BF16))
        ctrl_sb = _es.enter_context(nc.sbuf_tensor("ctrl_sb", [128, 8 * 64], F32))
        tie_sb = _es.enter_context(nc.sbuf_tensor("tie_sb", [64, 128], F32))
        mbd2_sb = _es.enter_context(nc.sbuf_tensor("mbd2_sb", [128, 512], BF16))
        mbdt_sb = _es.enter_context(nc.sbuf_tensor("mbdt_sb", [128, 4096], BF16))
        esg_sb = _es.enter_context(nc.sbuf_tensor("esg_sb", [64, 4 * 128], F32))
        perm_sb = _es.enter_context(nc.sbuf_tensor("perm_sb", [64, TTOK], F32))
        scr_sb = _es.enter_context(nc.sbuf_tensor("scr_sb", [128, 4096], F32))
        pint_sb = _es.enter_context(nc.sbuf_tensor("pint_sb", [128, 2], F32))
        coff_sb = _es.enter_context(nc.sbuf_tensor("coff_sb", [1, 32], mybir.dt.int32))
        with nc.Block() as blk:
            @blk.gpsimd
            def _(g):
                gd.dma(g.dma_start(out=wo_sb[:, :], in_=w_out[:, :]))
                gd.wait(g)
                g.iota(tie_sb[:, :], [[0, 8], [1, 16]], channel_multiplier=0,
                       allow_small_or_imprecise_dtypes=True)
                g.iota(scr_sb[:, 0:512], [[0, 64], [1, 8]], channel_multiplier=0,
                       allow_small_or_imprecise_dtypes=True)
                g.iota(scr_sb[:, 512:512 + 4096], [[0, 32], [1, 8], [0, 16]],
                       channel_multiplier=0, allow_small_or_imprecise_dtypes=True) if False else None

            @blk.sync
            def _(sy):
                ld.dma(sy.dma_start(out=ctrl_sb[:, :], in_=ctrl[:, :]))
                ld.dma(sy.dma_start(out=coff_sb[:, :], in_=coff[:, :]))
                ld.wait(sy)

        _es = contextlib.ExitStack()
        with _es:
            scr2_sb = _es.enter_context(nc.sbuf_tensor("scr2_sb", [128, 4096], F32))
            esgi_sb = _es.enter_context(nc.sbuf_tensor("esgi_sb", [64, 4 * 128], F32))
            with nc.Block() as blk:
                @blk.gpsimd
                def _(g):
                    g.iota(scr2_sb[:, :], [[0, 32], [1, 8], [0, 16]], channel_multiplier=0,
                           allow_small_or_imprecise_dtypes=True)
                    pass

                @blk.vector
                def _(v):
                    pass

            with (
                nc.sbuf_tensor("ii_sb", [128, 2], mybir.dt.int32) as ii_sb,
                nc.Block() as blk,
            ):
                @blk.gpsimd
                def _(g):
                    ws.inc(g.iota(ii_sb[:, 0:1], [[0, 1]], channel_multiplier=1))

                @blk.vector
                def _(v):
                    ws.wait(v)
                    ws.inc(v.tensor_scalar(ii_sb[:, 1:2], ii_sb[:, 0:1], 4, None,
                                           OP.logical_shift_right))
                    ws.wait(v)
                    ws.inc(v.tensor_scalar(ii_sb[:, 0:1], ii_sb[:, 0:1], 7, None, OP.bitwise_and))

                @blk.vector
                def _(v):
                    ws.wait(v)
                    v.tensor_copy(pint_sb[:, 0:1], ii_sb[:, 1:2])
                    v.tensor_copy(pint_sb[:, 1:2], ii_sb[:, 0:1])

            with nc.Block() as blk:
                @blk.vector
                def _(v):
                    v.tensor_scalar(tie_sb[:, :], tie_sb[:, :], TIE_STEP, None, OP.mult)
                    ins = v.tensor_scalar(scr_sb[:, 0:512], scr_sb[:, 0:512], pint_sb[:, 0:1],
                                          None, OP.is_equal)
                    ins = v.tensor_scalar(scr2_sb[:, :], scr2_sb[:, :], pint_sb[:, 1:2],
                                          None, OP.is_equal)
                    ws.inc(ins)
                    ws.wait(v)
                    v.tensor_copy(mbd2_sb[:, :], scr_sb[:, 0:512])
                    v.tensor_copy(mbdt_sb[:, :], scr2_sb[:, :])

            with nc.Block() as blk:
                @blk.gpsimd
                def _(g):
                    for esg in range(4):
                        ins = g.iota(esgi_sb[:, esg * 128:(esg + 1) * 128], [[1, 16], [0, 8]],
                                     base=esg * 16, channel_multiplier=0,
                                     allow_small_or_imprecise_dtypes=True)
                    ws.inc(ins)

                @blk.vector
                def _(v):
                    ws.wait(v)
                    v.tensor_scalar(esg_sb[:, :], esgi_sb[:, :], ppi_sb[0:64, 0:1],
                                    None, OP.is_equal)

        _es = contextlib.ExitStack()
        with _es:
            ya_sb = _es.enter_context(nc.sbuf_tensor("ya_sb", [128, 16 * 128], BF16))
            xt_sb = _es.enter_context(nc.sbuf_tensor("xt_sb", [128, DM], F32))
            o1t = _es.enter_context(nc.sbuf_tensor("o1t", [128, DM], F32))
            sqt = _es.enter_context(nc.sbuf_tensor("sqt", [128, DM], F32))
            l2 = _es.enter_context(nc.sbuf_tensor("l2", [128, DM], F32))
            l2b = _es.enter_context(nc.sbuf_tensor("l2b", [128, DM], BF16))
            l2f = _es.enter_context(nc.sbuf_tensor("l2f", [128, DM], F32))
            lgt = _es.enter_context(nc.sbuf_tensor("lgt", [64, 128], F32))
            mx_sb = _es.enter_context(nc.sbuf_tensor("mx_sb", [64, 136], F32))
            prmb = _es.enter_context(nc.sbuf_tensor("prmb", [64, 128], BF16))
            prT = _es.enter_context(nc.sbuf_tensor("prT", [128, 64], BF16))
            prTE = _es.enter_context(nc.sbuf_tensor("prTE", [128, 512], BF16))
            prBD = _es.enter_context(nc.sbuf_tensor("prBD", [128, 512], BF16))
            xmt4 = _es.enter_context(nc.sbuf_tensor("xmt4", [128, 4 * DM], BF16))
            lstt = _es.enter_context(nc.sbuf_tensor("lstt", [128, 8], F32))
            q0 = _es.enter_context(nc.psum_tensor("q0", [128, 512], F32))
            q1 = _es.enter_context(nc.psum_tensor("q1", [128, 512], F32))
            qb0 = _es.enter_context(nc.psum_tensor("qb0", [128, 512], BF16))
            qb1 = _es.enter_context(nc.psum_tensor("qb1", [128, 512], BF16))
            q = [q0, q1]
            qb = [qb0, qb1]
            for ti in range(TTOK // 128):
                with nc.Block() as blk:
                    @blk.sync
                    def _(sy):
                        ld.dma(sy.dma_start(out=xt_sb[:, :], in_=x_tok[ti * 128:(ti + 1) * 128, :]))

                    @blk.gpsimd
                    def _(g):
                        gd.dma(g.dma_start(
                            out=bass.AP(ya_sb, 0, [[2048, 128], [128, 16], [1, 128]]),
                            in_=bass.AP(y_loc, ti * 128,
                                        [[TTOK, 128], [128 * TTOK, 16], [1, 128]])))
                        gd.wait(g)

                    ev_b = ev.n
                    ws_b = ws.n
                    sc_b = sc.n

                    @blk.tensor
                    def _(t):
                        ld.wait(t)
                        gd.wait(t)
                        for f in range(2):
                            for k in range(16):
                                mm = t.matmul(q[f][:, :], ya_sb[:, k * 128:(k + 1) * 128],
                                              wo_sb[:, k * DM + f * 512:k * DM + (f + 1) * 512],
                                              start=(k == 0), stop=(k == 15))
                            ev.inc(mm)

                    @blk.vector
                    def _(v):
                        for f in range(2):
                            v.wait_ge(ev.h, ev_b + f + 1)
                            ws.inc(v.tensor_tensor(o1t[:, f * 512:(f + 1) * 512], q[f][:, :],
                                                   xt_sb[:, f * 512:(f + 1) * 512], OP.add))
                        ws.wait(v)
                        ws.inc(v.reduce_sum(lstt[:, 0:1], o1t[:, :], AX.X))

                    @blk.scalar
                    def _(s):
                        ws.wait(s)
                        sc.inc(s.activation(sqt[:, :], o1t[:, :], AF.Square, bias=0.0, scale=1.0))

                    @blk.vector
                    def _(v):
                        sc.wait(v)
                        ws.inc(v.reduce_sum(lstt[:, 1:2], sqt[:, :], AX.X))
                        ws.wait(v)
                        v.tensor_scalar(lstt[:, 2:3], lstt[:, 0:1], 1.0 / DM, None, OP.mult)
                        ws.inc(v.tensor_scalar(lstt[:, 3:4], lstt[:, 1:2], 1.0 / DM, None, OP.mult))
                        ws.wait(v)
                        ws.inc(v.tensor_tensor(lstt[:, 4:5], lstt[:, 2:3], lstt[:, 2:3], OP.mult))
                        ws.wait(v)
                        ws.inc(v.tensor_tensor(lstt[:, 5:6], lstt[:, 3:4], lstt[:, 4:5], OP.subtract))

                    @blk.scalar
                    def _(s):
                        ws.wait(s)
                        sc.inc(s.activation(lstt[:, 6:7], lstt[:, 5:6], AF.Ln, bias=EPS, scale=1.0))
                        sc.wait(s)
                        sc.inc(s.activation(lstt[:, 7:8], lstt[:, 6:7], AF.Exp, bias=0.0, scale=-0.5))

                    @blk.vector
                    def _(v):
                        sc.wait(v)
                        ws.inc(v.tensor_scalar(l2[:, :], o1t[:, :], lstt[:, 2:3], lstt[:, 7:8],
                                               OP.subtract, OP.mult))
                        ws.wait(v)
                        ws.inc(v.tensor_copy(l2b[:, :], l2[:, :]))

                    ev_b2 = ev.n
                    sc_b2 = sc.n

                    @blk.tensor
                    def _(t):
                        ws.wait(t)
                        for grp in range(2):
                            for j in range(4):
                                c = grp * 4 + j
                                mm = t.transpose(q[grp][:, j * 128:(j + 1) * 128],
                                                 l2[:, c * 128:(c + 1) * 128], idf_sb[:, :])
                            ev.inc(mm)

                    @blk.scalar
                    def _(s):
                        for grp in range(2):
                            s.wait_ge(ev.h, ev_b2 + grp + 1)
                            sc.inc(s.activation(l2f[:, grp * 512:(grp + 1) * 512], q[grp][:, :],
                                                AF.Copy, bias=0.0, scale=1.0))

                    ev_b3 = ev.n
                    ws_b3 = ws.n

                    @blk.tensor
                    def _(t):
                        t.wait_ge(sc.h, sc.n)
                        for k in range(8):
                            mm = t.matmul(q0[0:64, 0:128], ctrl_sb[:, k * 64:(k + 1) * 64],
                                          l2f[:, k * 128:(k + 1) * 128],
                                          start=(k == 0), stop=(k == 7))
                        ev.inc(mm)

                    @blk.vector
                    def _(v):
                        v.wait_ge(ev.h, ev_b3 + 1)
                        ws.inc(v.tensor_tensor(lgt[:, :], q0[0:64, 0:128], tie_sb[:, :], OP.add))
                        ws.wait(v)
                        lg3 = bass.AP(lgt, 0, [[128, 64], [16, 8], [1, 16]])
                        ws.inc(v.tensor_reduce(mx_sb[:, 0:8], lg3, AX.X, OP.max))
                        ws.wait(v)
                        for tt_ in range(16):
                            ins = v.tensor_copy(bass.AP(mx_sb, 8 + tt_, [[136, 64], [16, 8]]),
                                                mx_sb[:, 0:8])
                        ws.inc(ins)
                        ws.wait(v)
                        ws.inc(v.tensor_tensor(perm_sb[:, ti * 128:(ti + 1) * 128], lgt[:, :],
                                               mx_sb[:, 8:136], OP.is_equal))
                        ws.wait(v)
                        ws.inc(v.tensor_copy(prmb[:, :], perm_sb[:, ti * 128:(ti + 1) * 128]))

                    ev_b4 = ev.n
                    sc_b4 = sc.n

                    @blk.tensor
                    def _(t):
                        ws.wait(t)
                        ev.inc(t.transpose(qb0[0:128, 0:64], prmb[:, :], idb_sb[0:64, 0:64]))

                    @blk.scalar
                    def _(s):
                        s.wait_ge(ev.h, ev_b4 + 1)
                        sc.inc(s.activation(prT[:, :], qb0[0:128, 0:64], AF.Copy, bias=0.0, scale=1.0))

                    @blk.vector
                    def _(v):
                        v.wait_ge(sc.h, sc_b4 + 1)
                        for j in range(8):
                            ins = v.tensor_copy(bass.AP(prTE, j, [[512, 128], [8, 64]]), prT[:, :])
                        ws.inc(ins)
                        ws.wait(v)
                        ws.inc(v.tensor_tensor(prBD[:, :], prTE[:, :], mbd2_sb[:, :], OP.mult))

                    ev_b5 = ev.n
                    sc_b5 = sc.n

                    @blk.tensor
                    def _(t):
                        ws.wait(t)
                        for qq in range(4):
                            for f in range(2):
                                g = qq * 2 + f
                                if g >= 2:
                                    t.wait_ge(sc.h, sc_b5 + g - 1)
                                mm = t.matmul(q[g % 2][:, :], prBD[:, qq * 128:(qq + 1) * 128],
                                              l2b[:, f * 512:(f + 1) * 512], start=True, stop=True)
                                ev.inc(mm)

                    @blk.scalar
                    def _(s):
                        for g in range(8):
                            qq, f = g // 2, g % 2
                            s.wait_ge(ev.h, ev_b5 + g + 1)
                            sc.inc(s.activation(
                                xmt4[:, qq * DM + f * 512:qq * DM + (f + 1) * 512],
                                q[g % 2][:, :], AF.Copy, bias=0.0, scale=1.0))

                    @blk.sync
                    def _(sy):
                        sc.wait(sy)
                        st.dma(sy.dma_start(out=o1_sp[ti * 128:(ti + 1) * 128, :], in_=o1t[:, :]))
                        for qq in range(4):
                            st.dma(sy.dma_start(
                                out=xmt_sp[ti * 512 + qq * 128:ti * 512 + (qq + 1) * 128, :],
                                in_=xmt4[:, qq * DM:(qq + 1) * DM]))
                        st.wait(sy)

        _p6b(E, dict(locals()))
        _p6c(E, dict(locals()))

        if d_perm is not None:
            with nc.Block() as blk:
                @blk.sync
                def _(sy):
                    st.dma(sy.dma_start(out=d_perm[:, :], in_=perm_sb[:, :]))
                    st.wait(sy)


def _p6b(E, env):
    nc = E["nc"]
    ld, gd, st, ev, sc, ws = E["ld"], E["gd"], E["st"], E["ev"], E["sc"], E["ws"]
    f1t, f2m, fbias = E["f1t"], E["f2m"], E["fbias"]
    xmt_sp, int_sp = E["xmt_sp"], E["int_sp"]

    _es = contextlib.ExitStack()
    with _es:
        xmte = _es.enter_context(nc.sbuf_tensor("xmte", [128, 2048], BF16))
        xme = _es.enter_context(nc.sbuf_tensor("xme", [128, 8 * 256], BF16))
        f1es = _es.enter_context(nc.sbuf_tensor("f1es", [128, 512], BF16))
        f2es = _es.enter_context(nc.sbuf_tensor("f2es", [64, DM], BF16))
        fb_sb = _es.enter_context(nc.sbuf_tensor("fb_sb", [64, 1], F32))
        inner_sb = _es.enter_context(nc.sbuf_tensor("inner_sb", [64, 256], BF16))
        intr = _es.enter_context(nc.sbuf_tensor("intr", [128, 2 * DM], BF16))
        pb0 = _es.enter_context(nc.psum_tensor("pb0", [128, 512], BF16))
        pb1 = _es.enter_context(nc.psum_tensor("pb1", [128, 512], BF16))
        pf = _es.enter_context(nc.psum_tensor("pf", [128, 512], F32))
        pg0 = _es.enter_context(nc.psum_tensor("pg0", [128, 512], F32))
        pg1 = _es.enter_context(nc.psum_tensor("pg1", [128, 512], F32))
        pb = [pb0, pb1]
        pg = [pg0, pg1]
        for es in range(64):
            q = es // 16
            off = (q * 128 + (es % 16) * 8) * DM
            with nc.Block() as blk:
                @blk.sync
                def _(sy):
                    for half in range(2):
                        src = bass.AP(xmt_sp, off + half * 16 * 512 * DM,
                                      [[512 * DM, 16], [DM, 8], [1, DM]])
                        ld.dma(sy.dma_start(out=xmte[:, half * DM:(half + 1) * DM], in_=src))
                    ld.dma(sy.dma_start(out=fb_sb[:, 0:1],
                                        in_=bass.AP(fbias, es * 64, [[1, 64], [1, 1]])))

                @blk.gpsimd
                def _(g):
                    gd.dma(g.dma_start(out=f1es[:, :],
                                       in_=bass.AP(f1t, es * 128 * 512, [[512, 128], [1, 512]])))
                    gd.dma(g.dma_start(out=f2es[:, :],
                                       in_=bass.AP(f2m, es * 64 * DM, [[DM, 64], [1, DM]])))

                ev_b = ev.n
                sc_b = sc.n

                @blk.tensor
                def _(t):
                    ld.wait(t)
                    gd.wait(t)
                    for grp in range(4):
                        if grp >= 2:
                            t.wait_ge(sc.h, sc_b + (grp - 1) * 4)
                        for j in range(4):
                            g = grp * 4 + j
                            half, c = g // 8, g % 8
                            mm = t.transpose(pb[grp % 2][:, j * 128:(j + 1) * 128],
                                             xmte[:, half * DM + c * 128:half * DM + (c + 1) * 128],
                                             E["idb_sb"][:, :])
                        ev.inc(mm)

                @blk.scalar
                def _(s):
                    for grp in range(4):
                        s.wait_ge(ev.h, ev_b + grp + 1)
                        for j in range(4):
                            g = grp * 4 + j
                            half, c = g // 8, g % 8
                            sc.inc(s.activation(
                                xme[:, c * 256 + half * 128:c * 256 + (half + 1) * 128],
                                pb[grp % 2][:, j * 128:(j + 1) * 128],
                                AF.Copy, bias=0.0, scale=1.0))

                ev_b2 = ev.n
                sc_b2 = sc.n

                @blk.tensor
                def _(t):
                    t.wait_ge(sc.h, sc.n)
                    for c in range(8):
                        mm = t.matmul(pf[0:64, 0:256], f1es[:, c * 64:(c + 1) * 64],
                                      xme[:, c * 256:(c + 1) * 256],
                                      start=(c == 0), stop=(c == 7))
                    ev.inc(mm)

                @blk.scalar
                def _(s):
                    s.wait_ge(ev.h, ev_b2 + 1)
                    sc.inc(s.activation(inner_sb[:, :], pf[0:64, 0:256], AF.Relu,
                                        bias=fb_sb[:, 0:1], scale=1.0))

                ev_b3 = ev.n
                sc_b3 = sc.n

                @blk.tensor
                def _(t):
                    t.wait_ge(sc.h, sc.n)
                    for m in range(2):
                        for f in range(2):
                            g = m * 2 + f
                            if g >= 2:
                                t.wait_ge(sc.h, sc_b3 + g - 1)
                            mm = t.matmul(pg[g % 2][:, :], inner_sb[:, m * 128:(m + 1) * 128],
                                          f2es[:, f * 512:(f + 1) * 512], start=True, stop=True)
                            ev.inc(mm)

                @blk.scalar
                def _(s):
                    for g in range(4):
                        m, f = g // 2, g % 2
                        s.wait_ge(ev.h, ev_b3 + g + 1)
                        sc.inc(s.activation(intr[:, m * DM + f * 512:m * DM + (f + 1) * 512],
                                            pg[g % 2][:, :], AF.Copy, bias=0.0, scale=1.0))

                @blk.sync
                def _(sy):
                    sc.wait(sy)
                    for m in range(2):
                        st.dma(sy.dma_start(
                            out=int_sp[es * 256 + m * 128:es * 256 + (m + 1) * 128, :],
                            in_=intr[:, m * DM:(m + 1) * DM]))
                    st.wait(sy)


def _p6c(E, env):
    nc = E["nc"]
    ld, st, ev, sc, ws = E["ld"], E["st"], E["ev"], E["sc"], E["ws"]
    x_tok, out_tok, int_sp, o1_sp = E["x_tok"], E["out_tok"], E["int_sp"], E["o1_sp"]
    perm_sb = env["perm_sb"]
    esg_sb = env["esg_sb"]
    mbdt_sb = env["mbdt_sb"]

    _es = contextlib.ExitStack()
    with _es:
        pbdt = _es.enter_context(nc.sbuf_tensor("pbdt", [128, 4 * TTOK], BF16))
        scrE = _es.enter_context(nc.sbuf_tensor("scrE", [128, 512], F32))
        ir0 = _es.enter_context(nc.sbuf_tensor("ir0", [128, DM], BF16))
        ir1 = _es.enter_context(nc.sbuf_tensor("ir1", [128, DM], BF16))
        ir2 = _es.enter_context(nc.sbuf_tensor("ir2", [128, DM], BF16))
        ir3 = _es.enter_context(nc.sbuf_tensor("ir3", [128, DM], BF16))
        o1r = _es.enter_context(nc.sbuf_tensor("o1r", [128, DM], F32))
        outt = _es.enter_context(nc.sbuf_tensor("outt", [128, DM], F32))
        pp0 = _es.enter_context(nc.psum_tensor("pp0", [128, 512], F32))
        pp1 = _es.enter_context(nc.psum_tensor("pp1", [128, 512], F32))
        irs = [ir0, ir1, ir2, ir3]
        pp = [pp0, pp1]
        for esg in range(4):
            with nc.Block() as blk:
                ev_b = ev.n
                sc_b = sc.n
                ws_b = ws.n

                @blk.tensor
                def _(t):
                    for F in range(8):
                        if F >= 2:
                            t.wait_ge(sc.h, sc_b + F - 1)
                        mm = t.matmul(pp[F % 2][:, :], esg_sb[:, esg * 128:(esg + 1) * 128],
                                      perm_sb[:, F * 512:(F + 1) * 512], start=True, stop=True)
                        ev.inc(mm)

                @blk.scalar
                def _(s):
                    for F in range(8):
                        s.wait_ge(ev.h, ev_b + F + 1)
                        s.wait_ge(ws.h, ws_b + F)
                        sc.inc(s.activation(scrE[:, :], pp[F % 2][:, :], AF.Copy,
                                            bias=0.0, scale=1.0))

                @blk.vector
                def _(v):
                    for F in range(8):
                        v.wait_ge(sc.h, sc_b + F + 1)
                        ws.inc(v.tensor_tensor(pbdt[:, esg * TTOK + F * 512:esg * TTOK + (F + 1) * 512],
                                               scrE[:, :], mbdt_sb[:, F * 512 % 4096:(F * 512 % 4096) + 512],
                                               OP.mult))

        for ti in range(TTOK // 128):
            with nc.Block() as blk:
                @blk.sync
                def _(sy):
                    for esg in range(4):
                        src = bass.AP(int_sp, ((esg * 16) * 256 + ti * 8) * DM,
                                      [[256 * DM, 16], [DM, 8], [1, DM]])
                        ld.dma(sy.dma_start(out=irs[esg][:, :], in_=src))
                    ld.dma(sy.dma_start(out=o1r[:, :], in_=o1_sp[ti * 128:(ti + 1) * 128, :]))

                ev_b = ev.n

                @blk.tensor
                def _(t):
                    ld.wait(t)
                    for f in range(2):
                        for esg in range(4):
                            mm = t.matmul(pp[f][:, :], pbdt[:, esg * TTOK + ti * 128:esg * TTOK + (ti + 1) * 128],
                                          irs[esg][:, f * 512:(f + 1) * 512],
                                          start=(esg == 0), stop=(esg == 3))
                        ev.inc(mm)

                @blk.vector
                def _(v):
                    for f in range(2):
                        v.wait_ge(ev.h, ev_b + f + 1)
                        ws.inc(v.tensor_tensor(outt[:, f * 512:(f + 1) * 512], pp[f][:, :],
                                               o1r[:, f * 512:(f + 1) * 512], OP.add))

                @blk.sync
                def _(sy):
                    ws.wait(sy)
                    st.dma(sy.dma_start(out=out_tok[ti * 128:(ti + 1) * 128, :], in_=outt[:, :]))
                    st.wait(sy)


_NC_CACHE = {}


def _host_prep(inputs):
    x = np.ascontiguousarray(np.asarray(inputs["L0_attn_x"], dtype=np.float32))
    W_in = np.asarray(inputs["L0_attn_W_in"], dtype=np.float32)
    W_conv = np.asarray(inputs["L0_attn_W_conv"], dtype=np.float32)
    W_xproj = np.asarray(inputs["L0_attn_W_xproj"], dtype=np.float32)
    W_dt = np.asarray(inputs["L0_attn_W_dt"], dtype=np.float32)
    b_dt_a = np.asarray(inputs["L0_attn_b_dt"], dtype=np.float32)
    A_log = np.asarray(inputs["L0_attn_A_log"], dtype=np.float32)
    D_a = np.asarray(inputs["L0_attn_D"], dtype=np.float32)
    W_out = np.asarray(inputs["L0_attn_W_out"], dtype=np.float32)
    controller = np.asarray(inputs["L0_ffn_controller"], dtype=np.float32)
    f1 = np.asarray(inputs["L0_ffn_f1"], dtype=np.float32)
    f2 = np.asarray(inputs["L0_ffn_f2"], dtype=np.float32)
    fb = np.asarray(inputs["L0_ffn_bias"], dtype=np.float32)

    def ktiles(a):
        K, M = a.shape
        return np.ascontiguousarray(a.reshape(K // 128, 128, M).transpose(1, 0, 2).reshape(128, -1))

    w_out_h = ktiles(np.ascontiguousarray(W_out.T))
    ctrl_h = ktiles(controller.reshape(DM, 64))
    f1t_h = np.ascontiguousarray(
        f1.transpose(1, 2, 0, 3).reshape(64, DM, 64).reshape(64, 8, 128, 64)
        .transpose(0, 2, 1, 3).reshape(64, 128, 512))
    f2m_h = np.ascontiguousarray(f2.reshape(64, 64, DM))
    fb_h = np.ascontiguousarray(fb.reshape(64, 64))

    in_maps = []
    for core in range(8):
        b, g = core // 4, core % 4
        sl = slice(g * DC, (g + 1) * DC)
        d = dict(
            x_tok=np.ascontiguousarray(x[b, g * TTOK:(g + 1) * TTOK, :]),
            w_xc=ktiles(np.ascontiguousarray(W_in[sl].T)),
            w_z=ktiles(np.ascontiguousarray(W_in[DIN + g * DC:DIN + (g + 1) * DC].T)),
            w_conv=np.ascontiguousarray(W_conv[sl, 0, :]),
            w_xp=ktiles(np.ascontiguousarray(W_xproj[:, sl].T)),
            w_dt=np.ascontiguousarray(W_dt[sl].T),
            b_dt=np.ascontiguousarray(b_dt_a[sl, None]),
            a_log=np.ascontiguousarray(A_log[sl]),
            dvec=np.ascontiguousarray(D_a[sl, None]),
            w_out=w_out_h,
            ctrl=ctrl_h,
            f1t=f1t_h,
            f2m=f2m_h,
            fbias=fb_h,
            coff=(g * TTOK + 128 * np.arange(32, dtype=np.int32))[None, :],
        )
        in_maps.append(d)
    return in_maps


def run(inputs, dbg=(), upto="full"):
    key = (tuple(sorted(dbg)), upto)
    if key not in _NC_CACHE:
        _NC_CACHE[key] = build(dbg, upto)
    nc = _NC_CACHE[key]
    in_maps = _host_prep(inputs)
    res = run_bass_kernel_spmd(nc, in_maps, core_ids=list(range(8)))
    return res


def kernel(**inputs):
    res = run(inputs)
    out = np.empty((B, LSEQ, DM), np.float32)
    for core in range(8):
        b, g = core // 4, core % 4
        out[b, g * TTOK:(g + 1) * TTOK, :] = res.results[core]["out_tok"]
    return out


# version 05
# speedup vs baseline: 1.0916x; 1.0207x over previous
import contextlib
import numpy as np
import ml_dtypes

import concourse.bass as bass
import concourse.mybir as mybir
from concourse.bass_utils import run_bass_kernel_spmd

F32 = mybir.dt.float32
BF16 = mybir.dt.bfloat16
OP = mybir.AluOpType
AF = mybir.ActivationFunctionType
AX = mybir.AxisListType

B, LSEQ, DM = 2, 16384, 1024
DIN, NST, DTR = 2048, 16, 64
DC = 512
TTOK = 4096
NDT = DC // 128
P2C = 2048
NP2 = LSEQ // P2C
P4C = 512
NP4 = LSEQ // P4C
EPS = 1e-5
TIE_STEP = 1e-6 / 15.0
BF = ml_dtypes.bfloat16


class Sem:
    def __init__(self, nc, es, name):
        self.h = es.enter_context(nc.semaphore(name))
        self.n = 0

    def inc(self, instr, k=1):
        instr.then_inc(self.h, k)
        self.n += k

    def dma(self, instr):
        self.inc(instr, 16)

    def wait(self, eng):
        eng.wait_ge(self.h, self.n)


def build(dbg=(), upto="full"):
    nc = bass.Bass("TRN2", target_bir_lowering=False)
    E = {}

    x_tok = nc.dram_tensor("x_tok", [TTOK, DM], F32, kind="ExternalInput")
    w_xc = nc.dram_tensor("w_xc", [128, 8 * DC], F32, kind="ExternalInput")
    w_z = nc.dram_tensor("w_z", [128, 8 * DC], F32, kind="ExternalInput")
    w_conv = nc.dram_tensor("w_conv", [DC, 4], F32, kind="ExternalInput")
    w_xp = nc.dram_tensor("w_xp", [128, 4 * 96], F32, kind="ExternalInput")
    w_dt = nc.dram_tensor("w_dt", [64, DC], F32, kind="ExternalInput")
    b_dt = nc.dram_tensor("b_dt", [DC, 1], F32, kind="ExternalInput")
    a_log = nc.dram_tensor("a_log", [DC, NST], F32, kind="ExternalInput")
    dvec = nc.dram_tensor("dvec", [DC, 1], F32, kind="ExternalInput")
    w_out = nc.dram_tensor("w_out", [128, 16 * DM], F32, kind="ExternalInput")
    ctrl = nc.dram_tensor("ctrl", [128, 8 * 64], F32, kind="ExternalInput")
    f1t = nc.dram_tensor("f1t", [64, 128, 8 * 64], F32, kind="ExternalInput")
    f2m = nc.dram_tensor("f2m", [64, 64, DM], F32, kind="ExternalInput")
    fbias = nc.dram_tensor("fbias", [64, 64], F32, kind="ExternalInput")
    coff = nc.dram_tensor("coff", [1, 32], mybir.dt.int32, kind="ExternalInput")
    out_tok = nc.dram_tensor("out_tok", [TTOK, DM], F32, kind="ExternalOutput")

    ln1_my = nc.dram_tensor("ln1_my", [DM, TTOK], BF16)
    ln1_all = nc.dram_tensor("ln1_all", [4 * DM, TTOK], BF16)
    dbc_part = nc.dram_tensor("dbc_part", [96, LSEQ], F32)
    dbc_full = nc.dram_tensor("dbc_full", [96, LSEQ], F32)
    xc_sp = nc.dram_tensor("xc_sp", [DC, LSEQ], F32)
    ymid = nc.dram_tensor("ymid", [DC, LSEQ], BF16)
    y_all = nc.dram_tensor("y_all", [4 * DC, LSEQ], BF16)
    xmt_sp = nc.dram_tensor("xmt_sp", [32 * 512, DM], BF16)
    int_sp = nc.dram_tensor("int_sp", [64 * 256, DM], BF16)
    o1_sp = nc.dram_tensor("o1_sp", [TTOK, DM], F32)
    y_loc = nc.dram_tensor("y_loc", [4 * DC, TTOK], BF16)

    def dbg_t(name, shape, dtype=F32):
        if name in dbg:
            return nc.dram_tensor("dbg_" + name, shape, dtype, kind="ExternalOutput")
        return None

    d_ln = dbg_t("ln", [4 * DM, TTOK], BF16)
    d_xc = dbg_t("xc", [DC, LSEQ], F32)
    d_dbc = dbg_t("dbc", [96, LSEQ], F32)
    d_delta = dbg_t("delta", [DC, LSEQ], F32)
    d_ymid = dbg_t("ymid", [DC, LSEQ], BF16)
    d_o1 = dbg_t("o1", [TTOK, DM], F32)
    d_perm = dbg_t("perm", [64, TTOK], F32)
    d_xmt = dbg_t("xmt", [32 * 512, DM], BF16)
    d_int = dbg_t("int", [64 * 256, DM], BF16)

    es_ = contextlib.ExitStack()
    with es_:
        ld = Sem(nc, es_, "ld")
        gd = Sem(nc, es_, "gd")
        st = Sem(nc, es_, "st")
        ev = Sem(nc, es_, "ev")
        sc = Sem(nc, es_, "sc")
        ws = Sem(nc, es_, "ws")
        cc1 = Sem(nc, es_, "cc1")
        cc2 = Sem(nc, es_, "cc2")
        cc3 = Sem(nc, es_, "cc3")
        pc = Sem(nc, es_, "pc")
        E.update(locals())

        wxc_sb = es_.enter_context(nc.sbuf_tensor("wxc_sb", [128, 8 * DC], BF16))
        wz_sb = es_.enter_context(nc.sbuf_tensor("wz_sb", [128, 8 * DC], BF16))
        wxp_sb = es_.enter_context(nc.sbuf_tensor("wxp_sb", [128, 4 * 96], BF16))
        wdt_sb = es_.enter_context(nc.sbuf_tensor("wdt_sb", [64, DC], BF16))
        bdt_sb = es_.enter_context(nc.sbuf_tensor("bdt_sb", [128, NDT], F32))
        dv_sb = es_.enter_context(nc.sbuf_tensor("dv_sb", [128, NDT], F32))
        amat_sb = es_.enter_context(nc.sbuf_tensor("amat_sb", [128, NDT * NST], F32))
        wcv_sb = es_.enter_context(nc.sbuf_tensor("wcv_sb", [128, NDT * 4], F32))
        convd_sb = es_.enter_context(nc.sbuf_tensor("convd_sb", [128, NDT * 4 * 128], BF16))
        idf_sb = es_.enter_context(nc.sbuf_tensor("idf_sb", [128, 128], F32))
        idb_sb = es_.enter_context(nc.sbuf_tensor("idb_sb", [128, 128], BF16))
        sel16_sb = es_.enter_context(nc.sbuf_tensor("sel16_sb", [16, 16 * 128], F32))
        ppi_sb = es_.enter_context(nc.sbuf_tensor("ppi_sb", [128, 2], F32))
        cst_sb = es_.enter_context(nc.sbuf_tensor("cst_sb", [128, 4], F32))
        nc.const_aps.register(F32, EPS, cst_sb[:, 0:1])
        nc.const_aps.register(F32, 1.0, cst_sb[:, 1:2])
        nc.const_aps.register(F32, -1.0, cst_sb[:, 2:3])
        E.update(locals())

        with nc.Block() as blk:
            @blk.gpsimd
            def _(g):
                for s_ in (ld, gd, st, ev, sc, ws, cc1, cc2, cc3, pc):
                    g.sem_clear(s_.h)

        with nc.Block() as blk:
            @blk.sync
            def _(sy):
                for t in range(NDT):
                    ld.dma(sy.dma_start(out=bdt_sb[:, t:t + 1], in_=b_dt[t * 128:(t + 1) * 128, :]))
                    ld.dma(sy.dma_start(out=dv_sb[:, t:t + 1], in_=dvec[t * 128:(t + 1) * 128, :]))
                    ld.dma(sy.dma_start(out=amat_sb[:, t * NST:(t + 1) * NST],
                                        in_=a_log[t * 128:(t + 1) * 128, :]))
                    ld.dma(sy.dma_start(out=wcv_sb[:, t * 4:(t + 1) * 4],
                                        in_=w_conv[t * 128:(t + 1) * 128, :]))
                ld.wait(sy)

            @blk.vector
            def _(v):
                v.memset(cst_sb[:, 0:1], EPS)
                v.memset(cst_sb[:, 1:2], 1.0)
                v.memset(cst_sb[:, 2:3], -1.0)
                ws.inc(v.memset(cst_sb[:, 3:4], 0.0))

            @blk.gpsimd
            def _(g):
                gd.dma(g.dma_start(out=wxc_sb[:, :], in_=w_xc[:, :]))
                gd.dma(g.dma_start(out=wz_sb[:, :], in_=w_z[:, :]))
                gd.dma(g.dma_start(out=wxp_sb[:, :], in_=w_xp[:, :]))
                gd.dma(g.dma_start(out=wdt_sb[:, :], in_=w_dt[:, :]))
                gd.wait(g)
                g.iota(ppi_sb[:, 0:1], [[0, 1]], channel_multiplier=1,
                       allow_small_or_imprecise_dtypes=True)
                g.iota(idf_sb[:, :], [[1, 128]], channel_multiplier=0,
                       allow_small_or_imprecise_dtypes=True)
                g.iota(sel16_sb[:, :], [[1, 16], [0, 128]], channel_multiplier=0,
                       allow_small_or_imprecise_dtypes=True)

        with nc.Block() as blk:
            @blk.scalar
            def _(s):
                s.activation(amat_sb[:, :], amat_sb[:, :], AF.Exp, bias=0.0, scale=1.0)

        with nc.Block() as blk:
            @blk.vector
            def _(v):
                v.tensor_scalar(amat_sb[:, :], amat_sb[:, :], -1.0, None, OP.mult)
                v.tensor_scalar(idf_sb[:, :], idf_sb[:, :], ppi_sb[:, 0:1], None, OP.is_equal)
                ws.inc(v.tensor_scalar(sel16_sb[:, :], sel16_sb[:, :], ppi_sb[0:16, 0:1],
                                       None, OP.is_equal))

            @blk.vector
            def _(v):
                ws.wait(v)
                v.tensor_copy(idb_sb[:, :], idf_sb[:, :])
                for dt in range(NDT):
                    for tap in range(4):
                        v.scalar_tensor_tensor(
                            convd_sb[:, (dt * 4 + tap) * 128:(dt * 4 + tap + 1) * 128],
                            idf_sb[:, :], wcv_sb[:, dt * 4 + tap:dt * 4 + tap + 1],
                            idf_sb[:, :], OP.mult, OP.bypass)

        _es = contextlib.ExitStack()
        with _es:
            xt8 = _es.enter_context(nc.sbuf_tensor("xt8", [128, 8 * DM], F32))
            sq8 = _es.enter_context(nc.sbuf_tensor("sq8", [128, 8 * DM], F32))
            ln8 = _es.enter_context(nc.sbuf_tensor("ln8", [128, 8 * DM], BF16))
            lnT8 = _es.enter_context(nc.sbuf_tensor("lnT8", [128, 8 * DM], BF16))
            lst = _es.enter_context(nc.sbuf_tensor("lst", [128, 64], F32))
            psT0 = _es.enter_context(nc.psum_tensor("psT0", [128, 512], BF16))
            psT1 = _es.enter_context(nc.psum_tensor("psT1", [128, 512], BF16))
            psT = [psT0, psT1]
            for tb in range(TTOK // 128 // 8):
                with nc.Block() as blk:
                    @blk.sync
                    def _(sy):
                        for i in range(8):
                            ti = tb * 8 + i
                            ld.dma(sy.dma_start(out=xt8[:, i * DM:(i + 1) * DM],
                                                in_=x_tok[ti * 128:(ti + 1) * 128, :]))

                    @blk.scalar
                    def _(s):
                        ld.wait(s)
                        for i in range(8):
                            sc.inc(s.activation(sq8[:, i * DM:(i + 1) * DM],
                                                xt8[:, i * DM:(i + 1) * DM],
                                                AF.Square, bias=0.0, scale=1.0))

                    @blk.vector
                    def _(v):
                        ld.wait(v)
                        for i in range(8):
                            ins = v.reduce_sum(lst[:, i * 8:i * 8 + 1], xt8[:, i * DM:(i + 1) * DM],
                                               AX.X)
                        sc.wait(v)
                        for i in range(8):
                            ins = v.reduce_sum(lst[:, i * 8 + 1:i * 8 + 2],
                                               sq8[:, i * DM:(i + 1) * DM], AX.X)
                        ws.inc(ins)
                        ws.wait(v)
                        for i in range(8):
                            c = i * 8
                            ins = v.tensor_scalar(lst[:, c + 2:c + 3], lst[:, c:c + 1],
                                                  1.0 / DM, None, OP.mult)
                            ins = v.tensor_scalar(lst[:, c + 3:c + 4], lst[:, c + 1:c + 2],
                                                  1.0 / DM, None, OP.mult)
                        ws.inc(ins)
                        ws.wait(v)
                        for i in range(8):
                            c = i * 8
                            ins = v.tensor_tensor(lst[:, c + 4:c + 5], lst[:, c + 2:c + 3],
                                                  lst[:, c + 2:c + 3], OP.mult)
                        ws.inc(ins)
                        ws.wait(v)
                        for i in range(8):
                            c = i * 8
                            ins = v.tensor_tensor(lst[:, c + 5:c + 6], lst[:, c + 3:c + 4],
                                                  lst[:, c + 4:c + 5], OP.subtract)
                        ws.inc(ins)

                    @blk.scalar
                    def _(s):
                        ws.wait(s)
                        for i in range(8):
                            c = i * 8
                            ins = s.activation(lst[:, c + 6:c + 7], lst[:, c + 5:c + 6],
                                               AF.Ln, bias=EPS, scale=1.0)
                        sc.inc(ins)
                        sc.wait(s)
                        for i in range(8):
                            c = i * 8
                            sc.inc(s.activation(lst[:, c + 7:c + 8], lst[:, c + 6:c + 7],
                                                AF.Exp, bias=0.0, scale=-0.5))

                    @blk.vector
                    def _(v):
                        sc.wait(v)
                        for i in range(8):
                            c = i * 8
                            ws.inc(v.tensor_scalar(ln8[:, i * DM:(i + 1) * DM],
                                                   xt8[:, i * DM:(i + 1) * DM],
                                                   lst[:, c + 2:c + 3], lst[:, c + 7:c + 8],
                                                   OP.subtract, OP.mult))

                    ev_b = ev.n
                    sc_b = sc.n

                    @blk.tensor
                    def _(t):
                        ws.wait(t)
                        for grp in range(16):
                            if grp >= 2:
                                t.wait_ge(sc.h, sc_b + grp - 1)
                            for j in range(4):
                                g = grp * 4 + j
                                i, c = g // 8, g % 8
                                mm = t.transpose(psT[grp % 2][:, j * 128:(j + 1) * 128],
                                                 ln8[:, i * DM + c * 128:i * DM + (c + 1) * 128],
                                                 idb_sb[:, :])
                            ev.inc(mm)

                    @blk.scalar
                    def _(s):
                        for grp in range(16):
                            s.wait_ge(ev.h, ev_b + grp + 1)
                            g0 = grp * 4
                            i = g0 // 8
                            c = g0 % 8
                            sc.inc(s.activation(lnT8[:, i * DM + c * 128:i * DM + (c + 4) * 128],
                                                psT[grp % 2][:, :], AF.Copy, bias=0.0, scale=1.0))

                    @blk.sync
                    def _(sy):
                        sc.wait(sy)
                        for i in range(8):
                            ti = tb * 8 + i
                            dst = bass.AP(ln1_my, ti * 128, [[TTOK, 128], [128 * TTOK, 8], [1, 128]])
                            src = bass.AP(lnT8, i * DM, [[8 * DM, 128], [128, 8], [1, 128]])
                            st.dma(sy.dma_start(out=dst, in_=src))
                        st.wait(sy)

        with nc.Block() as blk:
            @blk.gpsimd
            def _(g):
                cc1.inc(g.collective_compute(
                    "AllGather", OP.bypass,
                    replica_groups=[[0, 1, 2, 3], [4, 5, 6, 7]],
                    ins=[ln1_my.ap().opt()], outs=[ln1_all.ap().opt()]))
                g.wait_ge(cc1.h, 1)

        if upto != "p1":
            _es = contextlib.ExitStack()
            with _es:
                rhs8 = _es.enter_context(nc.sbuf_tensor("rhs8", [128, 8 * P2C], BF16))
                xcpre = _es.enter_context(nc.sbuf_tensor("xcpre", [128, NDT * (P2C + 3)], BF16))
                xcf = _es.enter_context(nc.sbuf_tensor("xcf", [128, NDT * P2C], F32))
                xcb = _es.enter_context(nc.sbuf_tensor("xcb", [128, NDT * P2C], BF16))
                dbc_sb = _es.enter_context(nc.sbuf_tensor("dbc_sb", [96, P2C], F32))
                psA0 = _es.enter_context(nc.psum_tensor("psA0", [128, 512], F32))
                psA1 = _es.enter_context(nc.psum_tensor("psA1", [128, 512], F32))
                psB0 = _es.enter_context(nc.psum_tensor("psB0", [128, 512], F32))
                psB1 = _es.enter_context(nc.psum_tensor("psB1", [128, 512], F32))
                psC = _es.enter_context(nc.psum_tensor("psC", [128, 512], F32))
                psA = [psA0, psA1]
                psB = [psB0, psB1]
                XP3 = P2C + 3
                for ch in range(NP2):
                    rank = (ch * P2C) // TTOK
                    col0 = ch * P2C - rank * TTOK
                    with nc.Block() as blk:
                        @blk.sync
                        def _(sy):
                            for cc in range(8):
                                src = bass.AP(ln1_all, (rank * DM + cc * 128) * TTOK + col0,
                                              [[TTOK, 128], [1, P2C]])
                                ld.dma(sy.dma_start(out=rhs8[:, cc * P2C:(cc + 1) * P2C], in_=src))

                        @blk.vector
                        def _(v):
                            for dt in range(NDT):
                                o = dt * XP3
                                if ch == 0:
                                    ws.inc(v.memset(xcpre[:, o:o + 3], 0.0))
                                else:
                                    ws.inc(v.tensor_copy(xcpre[:, o:o + 3],
                                                         xcpre[:, o + P2C:o + P2C + 3]))

                        ev_b = ev.n
                        sc_b = sc.n

                        @blk.tensor
                        def _(t):
                            ld.wait(t)
                            ws.wait(t)
                            for dt in range(NDT):
                                for f in range(4):
                                    g = dt * 4 + f
                                    if g >= 2:
                                        t.wait_ge(sc.h, sc_b + g - 1)
                                    for k in range(8):
                                        mm = t.matmul(psA[g % 2][:, :],
                                                      wxc_sb[:, k * DC + dt * 128:k * DC + (dt + 1) * 128],
                                                      rhs8[:, k * P2C + f * 512:k * P2C + (f + 1) * 512],
                                                      start=(k == 0), stop=(k == 7))
                                    ev.inc(mm)
                            for dt in range(NDT):
                                for f in range(4):
                                    g2 = dt * 4 + f
                                    t.wait_ge(sc.h, sc_b + (dt + 1) * 4)
                                    if g2 >= 2:
                                        t.wait_ge(sc.h, sc_b + 16 + g2 - 1)
                                    o = dt * XP3 + f * 512
                                    for tap in range(4):
                                        mm = t.matmul(psB[g2 % 2][:, :],
                                                      convd_sb[:, (dt * 4 + tap) * 128:(dt * 4 + tap + 1) * 128],
                                                      xcpre[:, o + tap:o + tap + 512],
                                                      start=(tap == 0), stop=(tap == 3))
                                    ev.inc(mm)

                        @blk.scalar
                        def _(s):
                            for g in range(16):
                                dt, f = g // 4, g % 4
                                s.wait_ge(ev.h, ev_b + g + 1)
                                sc.inc(s.activation(
                                    xcpre[:, dt * XP3 + 3 + f * 512:dt * XP3 + 3 + (f + 1) * 512],
                                    psA[g % 2][:, :], AF.Copy, bias=0.0, scale=1.0))
                            for g2 in range(16):
                                dt, f = g2 // 4, g2 % 4
                                s.wait_ge(ev.h, ev_b + 16 + g2 + 1)
                                sc.inc(s.activation(xcf[:, dt * P2C + f * 512:dt * P2C + (f + 1) * 512],
                                                    psB[g2 % 2][:, :], AF.Silu, bias=0.0, scale=1.0))

                        @blk.vector
                        def _(v):
                            for g2 in range(16):
                                dt, f = g2 // 4, g2 % 4
                                v.wait_ge(sc.h, sc_b + 16 + g2 + 1)
                                ws.inc(v.tensor_copy(xcb[:, dt * P2C + f * 512:dt * P2C + (f + 1) * 512],
                                                     xcf[:, dt * P2C + f * 512:dt * P2C + (f + 1) * 512]))

                    with nc.Block() as blk:
                        ev_b2 = ev.n
                        sc_b2 = sc.n

                        @blk.tensor
                        def _(t):
                            for f in range(4):
                                if f >= 1:
                                    t.wait_ge(sc.h, sc_b2 + f)
                                for k in range(NDT):
                                    mm = t.matmul(psC[0:96, :],
                                                  wxp_sb[:, k * 96:(k + 1) * 96],
                                                  xcb[:, k * P2C + f * 512:k * P2C + (f + 1) * 512],
                                                  start=(k == 0), stop=(k == NDT - 1))
                                ev.inc(mm)

                        @blk.scalar
                        def _(s):
                            for f in range(4):
                                s.wait_ge(ev.h, ev_b2 + f + 1)
                                sc.inc(s.activation(dbc_sb[:, f * 512:(f + 1) * 512],
                                                    psC[0:96, :], AF.Copy, bias=0.0, scale=1.0))

                        @blk.sync
                        def _(sy):
                            sc.wait(sy)
                            dst = bass.AP(dbc_part, ch * P2C, [[LSEQ, 96], [1, P2C]])
                            st.dma(sy.dma_start(out=dst, in_=dbc_sb[:, :]))
                            for dt in range(NDT):
                                dstx = bass.AP(xc_sp, dt * 128 * LSEQ + ch * P2C, [[LSEQ, 128], [1, P2C]])
                                st.dma(sy.dma_start(out=dstx, in_=xcf[:, dt * P2C:(dt + 1) * P2C]))
                            st.wait(sy)

        if upto not in ("p1", "p2"):
            with nc.Block() as blk:
                @blk.gpsimd
                def _(g):
                    cc2.inc(g.collective_compute(
                        "AllReduce", OP.add,
                        replica_groups=[[0, 1, 2, 3], [4, 5, 6, 7]],
                        ins=[dbc_part.ap().opt()], outs=[dbc_full.ap().opt()]))
                    g.wait_ge(cc2.h, 1)

        E.update(dict(x_tok=x_tok, w_out=w_out, ctrl=ctrl, f1t=f1t, f2m=f2m, fbias=fbias, coff=coff, y_loc=y_loc,
                      out_tok=out_tok, ln1_all=ln1_all, dbc_full=dbc_full, xc_sp=xc_sp,
                      ymid=ymid, y_all=y_all, xmt_sp=xmt_sp, int_sp=int_sp, o1_sp=o1_sp,
                      d_delta=d_delta, d_perm=d_perm, nc=nc))

        if upto in ("p4", "p5", "full"):
            _p4(E)
        if upto in ("p5", "full"):
            with nc.Block() as blk:
                @blk.gpsimd
                def _(g):
                    cc3.inc(g.collective_compute(
                        "AllGather", OP.bypass,
                        replica_groups=[[0, 1, 2, 3], [4, 5, 6, 7]],
                        ins=[ymid.ap().opt()], outs=[y_all.ap().opt()]))
                    g.wait_ge(cc3.h, 1)
        if upto == "full":
            _p6(E)
        else:
            with (
                nc.sbuf_tensor("zz", [128, DM], F32) as zz,
                nc.Block() as blk,
            ):
                @blk.vector
                def _(v):
                    ws.inc(v.memset(zz[:, :], 0.0))

                @blk.sync
                def _(sy):
                    ws.wait(sy)
                    for ti in range(TTOK // 128):
                        st.dma(sy.dma_start(out=out_tok[ti * 128:(ti + 1) * 128, :], in_=zz[:, :]))
                    st.wait(sy)

        if dbg:
            with nc.Block() as blk:
                @blk.sync
                def _(sy):
                    for name, dd, src in (("ln", d_ln, ln1_all), ("xc", d_xc, xc_sp),
                                          ("dbc", d_dbc, dbc_full), ("ymid", d_ymid, ymid),
                                          ("o1", d_o1, o1_sp), ("xmt", d_xmt, xmt_sp),
                                          ("int", d_int, int_sp)):
                        if dd is not None:
                            st.dma(sy.dma_start(out=dd[:, :], in_=src[:, :]))
                    st.wait(sy)
        with nc.Block() as blk:
            @blk.gpsimd
            def _(g):
                for s_ in (ld, gd, st, ev, sc, ws, cc1, cc2, cc3, pc):
                    g.sem_clear(s_.h)
    return nc


def _p4(E):
    nc = E["nc"]
    ld, st, ev, sc, ws, pc = E["ld"], E["st"], E["ev"], E["sc"], E["ws"], E["pc"]
    wdt_sb, bdt_sb, dv_sb, amat_sb, sel16_sb, wz_sb = (
        E["wdt_sb"], E["bdt_sb"], E["dv_sb"], E["amat_sb"], E["sel16_sb"], E["wz_sb"])
    ln1_all, dbc_full, xc_sp, ymid, d_delta = (
        E["ln1_all"], E["dbc_full"], E["xc_sp"], E["ymid"], E["d_delta"])
    C = P4C
    NH = NST // 2

    _es = contextlib.ExitStack()
    with _es:
        rz8 = _es.enter_context(nc.sbuf_tensor("rz8", [128, 2 * 8 * C], BF16))
        dltb = _es.enter_context(nc.sbuf_tensor("dltb", [64, C], BF16))
        dltf = _es.enter_context(nc.sbuf_tensor("dltf", [64, 2 * C], F32))
        bm_sb = _es.enter_context(nc.sbuf_tensor("bm_sb", [16, 2 * C], F32))
        cm_sb = _es.enter_context(nc.sbuf_tensor("cm_sb", [16, 2 * C], F32))
        delta = _es.enter_context(nc.sbuf_tensor("delta", [128, NDT * C], F32))
        xc4 = _es.enter_context(nc.sbuf_tensor("xc4", [128, 2 * NDT * C], F32))
        u4 = _es.enter_context(nc.sbuf_tensor("u4", [128, NDT * C], F32))
        gz4 = _es.enter_context(nc.sbuf_tensor("gz4", [128, NDT * C], F32))
        y4 = _es.enter_context(nc.sbuf_tensor("y4", [128, NDT * C], F32))
        yh4 = _es.enter_context(nc.sbuf_tensor("yh4", [128, NDT * C], F32))
        ymb = _es.enter_context(nc.sbuf_tensor("ymb", [128, NDT * C], BF16))
        bmb = _es.enter_context(nc.sbuf_tensor("bmb", [128, NH * C], F32))
        cmb = _es.enter_context(nc.sbuf_tensor("cmb", [128, NH * C], F32))
        hbuf = _es.enter_context(nc.sbuf_tensor("hbuf", [128, NH * C], F32))
        abuf8 = _es.enter_context(nc.sbuf_tensor("abuf8", [128, 8 * C], F32))
        bbuf8 = _es.enter_context(nc.sbuf_tensor("bbuf8", [128, 8 * C], F32))
        hstate = _es.enter_context(nc.sbuf_tensor("hstate", [128, NDT * NST], F32))
        ps0 = _es.enter_context(nc.psum_tensor("ps0", [128, 512], F32))
        ps1 = _es.enter_context(nc.psum_tensor("ps1", [128, 512], F32))
        ps2 = _es.enter_context(nc.psum_tensor("ps2", [128, 512], F32))
        ps3 = _es.enter_context(nc.psum_tensor("ps3", [128, 512], F32))
        ps = [ps0, ps1, ps2, ps3]
        with nc.Block() as blk:
            @blk.vector
            def _(v):
                v.memset(hstate[:, :], 0.0)

        def emit_loads(sy, ch):
            p_ = ch % 2
            rank_ = (ch * C) // TTOK
            col0_ = ch * C - rank_ * TTOK
            for cc in range(8):
                srcp = bass.AP(ln1_all, (rank_ * DM + cc * 128) * TTOK + col0_,
                               [[TTOK, 128], [1, C]])
                ld.dma(sy.dma_start(out=rz8[:, (p_ * 8 + cc) * C:(p_ * 8 + cc + 1) * C], in_=srcp))
            ld.dma(sy.dma_start(out=dltf[:, p_ * C:(p_ + 1) * C],
                                in_=bass.AP(dbc_full, ch * C, [[LSEQ, 64], [1, C]])))
            ld.dma(sy.dma_start(out=bm_sb[:, p_ * C:(p_ + 1) * C],
                                in_=bass.AP(dbc_full, 64 * LSEQ + ch * C, [[LSEQ, 16], [1, C]])))
            ld.dma(sy.dma_start(out=cm_sb[:, p_ * C:(p_ + 1) * C],
                                in_=bass.AP(dbc_full, 80 * LSEQ + ch * C, [[LSEQ, 16], [1, C]])))
            for dt in range(NDT):
                ld.dma(sy.dma_start(out=xc4[:, (p_ * NDT + dt) * C:(p_ * NDT + dt + 1) * C],
                                    in_=bass.AP(xc_sp, dt * 128 * LSEQ + ch * C,
                                                [[LSEQ, 128], [1, C]])))

        with nc.Block() as blk:
            @blk.sync
            def _(sy):
                emit_loads(sy, 0)

        for ch in range(NP4):
            p = ch % 2
            rank = (ch * C) // TTOK
            col0 = ch * C - rank * TTOK
            with nc.Block() as blk:
                @blk.vector
                def _(v):
                    ld.wait(v)
                    ws.inc(v.tensor_copy(dltb[:, :], dltf[:, p * C:(p + 1) * C]))

                ev_b = ev.n
                sc_b = sc.n
                groups = [("z", dt) for dt in range(NDT)] + [("d", dt) for dt in range(NDT)]

                @blk.tensor
                def _(t):
                    ld.wait(t)
                    ws.wait(t)
                    for g, (kind, dt) in enumerate(groups):
                        if g >= 4:
                            t.wait_ge(sc.h, sc_b + g - 3)
                        if kind == "z":
                            for k in range(8):
                                mm = t.matmul(ps[g % 4][:, :],
                                              wz_sb[:, k * DC + dt * 128:k * DC + (dt + 1) * 128],
                                              rz8[:, (p * 8 + k) * C:(p * 8 + k + 1) * C],
                                              start=(k == 0), stop=(k == 7))
                        else:
                            mm = t.matmul(ps[g % 4][:, :], wdt_sb[:, dt * 128:(dt + 1) * 128],
                                          dltb[:, :], start=True, stop=True)
                        ev.inc(mm)

                @blk.scalar
                def _(s):
                    for g, (kind, dt) in enumerate(groups):
                        s.wait_ge(ev.h, ev_b + g + 1)
                        if kind == "z":
                            sc.inc(s.activation(gz4[:, dt * C:(dt + 1) * C], ps[g % 4][:, :],
                                                AF.Silu, bias=0.0, scale=1.0))
                        else:
                            sc.inc(s.activation(delta[:, dt * C:(dt + 1) * C], ps[g % 4][:, :],
                                                AF.Identity, bias=bdt_sb[:, dt:dt + 1], scale=1.0))

            with nc.Block() as blk:
                @blk.scalar
                def _(s):
                    for dt in range(NDT):
                        sl = slice(dt * C, (dt + 1) * C)
                        ins = s.activation(y4[:, sl], delta[:, sl], AF.Abs, bias=0.0, scale=1.0)
                    sc.inc(ins)
                    sc.wait(s)
                    for dt in range(NDT):
                        sl = slice(dt * C, (dt + 1) * C)
                        ins = s.activation(yh4[:, sl], y4[:, sl], AF.Exp, bias=0.0, scale=-1.0)
                    sc.inc(ins)
                    sc.wait(s)
                    for dt in range(NDT):
                        sl = slice(dt * C, (dt + 1) * C)
                        ins = s.activation(y4[:, sl], yh4[:, sl], AF.Ln, bias=1.0, scale=1.0)
                    sc.inc(ins)

                @blk.vector
                def _(v):
                    sc.wait(v)
                    for dt in range(NDT):
                        sl = slice(dt * C, (dt + 1) * C)
                        ins = v.scalar_tensor_tensor(delta[:, sl], delta[:, sl], 0.0, y4[:, sl],
                                                     OP.max, OP.add)
                    ws.inc(ins)
                    ws.wait(v)
                    for dt in range(NDT):
                        sl = slice(dt * C, (dt + 1) * C)
                        ins = v.tensor_tensor(u4[:, sl], delta[:, sl],
                                              xc4[:, (p * NDT + dt) * C:(p * NDT + dt + 1) * C],
                                              OP.mult)
                    ws.inc(ins)

            if d_delta is not None:
                with nc.Block() as blk:
                    @blk.sync
                    def _(sy):
                        for dt in range(NDT):
                            dstd = bass.AP(d_delta, dt * 128 * LSEQ + ch * C, [[LSEQ, 128], [1, C]])
                            st.dma(sy.dma_start(out=dstd, in_=delta[:, dt * C:(dt + 1) * C]))
                        st.wait(sy)

            for half in range(2):
                with nc.Block() as blk:
                    ev_b = ev.n
                    sc_b = sc.n
                    ws_b = ws.n
                    pc_b = pc.n
                    rgroups = [("b", n) for n in range(NH)] + [("c", n) for n in range(NH)]

                    @blk.tensor
                    def _(t):
                        for g, (kind, n) in enumerate(rgroups):
                            gn = half * NH + n
                            if g >= 4:
                                t.wait_ge(sc.h, sc_b + g - 3)
                            src_ = bm_sb if kind == "b" else cm_sb
                            mm = t.matmul(ps[g % 4][:, :], sel16_sb[:, gn * 128:(gn + 1) * 128],
                                          src_[:, p * C:(p + 1) * C], start=True, stop=True)
                            ev.inc(mm)

                    @blk.scalar
                    def _(s):
                        for g, (kind, n) in enumerate(rgroups):
                            s.wait_ge(ev.h, ev_b + g + 1)
                            dstt = bmb if kind == "b" else cmb
                            sc.inc(s.activation(dstt[:, n * C:(n + 1) * C], ps[g % 4][:, :],
                                                AF.Copy, bias=0.0, scale=1.0))
                        for dt in range(NDT):
                            if dt >= 1:
                                s.wait_ge(ws.h, ws_b + (dt - 1) * 11 + 8)
                            for n in range(NH):
                                gn = half * NH + n
                                sc.inc(s.activation(abuf8[:, n * C:(n + 1) * C],
                                                    delta[:, dt * C:(dt + 1) * C], AF.Exp,
                                                    bias=0.0,
                                                    scale=amat_sb[:, dt * NST + gn:dt * NST + gn + 1]))

                    @blk.gpsimd
                    def _(g):
                        for dt in range(NDT):
                            g.wait_ge(sc.h, sc_b + 16)
                            if dt >= 1:
                                g.wait_ge(ws.h, ws_b + (dt - 1) * 11 + 8)
                            for n in range(NH):
                                pc.inc(g.tensor_tensor(bbuf8[:, n * C:(n + 1) * C],
                                                       u4[:, dt * C:(dt + 1) * C],
                                                       bmb[:, n * C:(n + 1) * C], OP.mult))


                    @blk.vector
                    def _(v):
                        for dt in range(NDT):
                            for n in range(NH):
                                gn = half * NH + n
                                v.wait_ge(sc.h, sc_b + 16 + dt * NH + n + 1)
                                v.wait_ge(pc.h, pc_b + dt * NH + n + 1)
                                ws.inc(v.tensor_tensor_scan(
                                    hbuf[:, n * C:(n + 1) * C],
                                    abuf8[:, n * C:(n + 1) * C],
                                    bbuf8[:, n * C:(n + 1) * C],
                                    hstate[:, dt * NST + gn:dt * NST + gn + 1],
                                    OP.mult, OP.add))
                            v.wait_ge(ws.h, ws.n)
                            lastc = bass.AP(hbuf, C - 1, [[NH * C, 128], [C, NH]])
                            ws.inc(v.tensor_copy(
                                bass.AP(hstate, dt * NST + half * NH, [[NDT * NST, 128], [1, NH]]),
                                lastc))
                            v.wait_ge(ws.h, ws.n)
                            ws.inc(v.tensor_tensor(hbuf[:, :], hbuf[:, :], cmb[:, :], OP.mult))
                            v.wait_ge(ws.h, ws.n)
                            h3 = bass.AP(hbuf, 0, [[NH * C, 128], [1, C], [C, NH]])
                            dst_y = y4 if half == 0 else yh4
                            ws.inc(v.tensor_reduce(dst_y[:, dt * C:(dt + 1) * C], h3, AX.X, OP.add))
                            v.wait_ge(ws.h, ws.n)
            with nc.Block() as blk:
                @blk.vector
                def _(v):
                    for dt in range(NDT):
                        sl = slice(dt * C, (dt + 1) * C)
                        ins = v.tensor_tensor(y4[:, sl], y4[:, sl], yh4[:, sl], OP.add)
                    ws.inc(ins)
                    ws.wait(v)
                    for dt in range(NDT):
                        sl = slice(dt * C, (dt + 1) * C)
                        ins = v.scalar_tensor_tensor(y4[:, sl],
                                                     xc4[:, (p * NDT + dt) * C:(p * NDT + dt + 1) * C],
                                                     dv_sb[:, dt:dt + 1],
                                                     y4[:, sl], OP.mult, OP.add)
                    ws.inc(ins)
                    ws.wait(v)
                    for dt in range(NDT):
                        sl = slice(dt * C, (dt + 1) * C)
                        ws.inc(v.tensor_tensor(ymb[:, sl], y4[:, sl], gz4[:, sl], OP.mult))

                @blk.sync
                def _(sy):
                    if ch + 1 < NP4:
                        emit_loads(sy, ch + 1)
                    ws.wait(sy)
                    for dt in range(NDT):
                        dsty = bass.AP(ymid, dt * 128 * LSEQ + ch * C, [[LSEQ, 128], [1, C]])
                        st.dma(sy.dma_start(out=dsty, in_=ymb[:, dt * C:(dt + 1) * C]))
                    st.wait(sy)


def _p6(E):
    nc = E["nc"]
    ld, gd, st, ev, sc, ws = E["ld"], E["gd"], E["st"], E["ev"], E["sc"], E["ws"]
    idf_sb, idb_sb, ppi_sb = E["idf_sb"], E["idb_sb"], E["ppi_sb"]
    x_tok, w_out, ctrl, f1t, f2m, fbias, out_tok = (
        E["x_tok"], E["w_out"], E["ctrl"], E["f1t"], E["f2m"], E["fbias"], E["out_tok"])
    coff = E["coff"]
    y_loc = E["y_loc"]

    with nc.Block() as blk:
        @blk.gpsimd
        def _(g):
            pid = g.partition_id()
            goff_v = (pid % 4) * TTOK
            gd.dma(g.dma_start(
                out=y_loc.ap(),
                in_=bass.AP(E["y_all"], goff_v, [[LSEQ, 4 * DC], [1, TTOK]])))
            gd.wait(g)
    y_all, xmt_sp, int_sp, o1_sp, d_perm = (
        E["y_all"], E["xmt_sp"], E["int_sp"], E["o1_sp"], E["d_perm"])

    _es = contextlib.ExitStack()
    with _es:
        wo_sb = _es.enter_context(nc.sbuf_tensor("wo_sb", [128, 16 * DM], BF16))
        ctrl_sb = _es.enter_context(nc.sbuf_tensor("ctrl_sb", [128, 8 * 64], F32))
        tie_sb = _es.enter_context(nc.sbuf_tensor("tie_sb", [64, 128], F32))
        mbd2_sb = _es.enter_context(nc.sbuf_tensor("mbd2_sb", [128, 512], BF16))
        mbdt_sb = _es.enter_context(nc.sbuf_tensor("mbdt_sb", [128, 4096], BF16))
        esg_sb = _es.enter_context(nc.sbuf_tensor("esg_sb", [64, 4 * 128], F32))
        perm_sb = _es.enter_context(nc.sbuf_tensor("perm_sb", [64, TTOK], F32))
        scr_sb = _es.enter_context(nc.sbuf_tensor("scr_sb", [128, 4096], F32))
        pint_sb = _es.enter_context(nc.sbuf_tensor("pint_sb", [128, 2], F32))
        coff_sb = _es.enter_context(nc.sbuf_tensor("coff_sb", [1, 32], mybir.dt.int32))
        with nc.Block() as blk:
            @blk.gpsimd
            def _(g):
                gd.dma(g.dma_start(out=wo_sb[:, :], in_=w_out[:, :]))
                gd.wait(g)
                g.iota(tie_sb[:, :], [[0, 8], [1, 16]], channel_multiplier=0,
                       allow_small_or_imprecise_dtypes=True)
                g.iota(scr_sb[:, 0:512], [[0, 64], [1, 8]], channel_multiplier=0,
                       allow_small_or_imprecise_dtypes=True)
                g.iota(scr_sb[:, 512:512 + 4096], [[0, 32], [1, 8], [0, 16]],
                       channel_multiplier=0, allow_small_or_imprecise_dtypes=True) if False else None

            @blk.sync
            def _(sy):
                ld.dma(sy.dma_start(out=ctrl_sb[:, :], in_=ctrl[:, :]))
                ld.dma(sy.dma_start(out=coff_sb[:, :], in_=coff[:, :]))
                ld.wait(sy)

        _es = contextlib.ExitStack()
        with _es:
            scr2_sb = _es.enter_context(nc.sbuf_tensor("scr2_sb", [128, 4096], F32))
            esgi_sb = _es.enter_context(nc.sbuf_tensor("esgi_sb", [64, 4 * 128], F32))
            with nc.Block() as blk:
                @blk.gpsimd
                def _(g):
                    g.iota(scr2_sb[:, :], [[0, 32], [1, 8], [0, 16]], channel_multiplier=0,
                           allow_small_or_imprecise_dtypes=True)
                    pass

                @blk.vector
                def _(v):
                    pass

            with (
                nc.sbuf_tensor("ii_sb", [128, 2], mybir.dt.int32) as ii_sb,
                nc.Block() as blk,
            ):
                @blk.gpsimd
                def _(g):
                    ws.inc(g.iota(ii_sb[:, 0:1], [[0, 1]], channel_multiplier=1))

                @blk.vector
                def _(v):
                    ws.wait(v)
                    ws.inc(v.tensor_scalar(ii_sb[:, 1:2], ii_sb[:, 0:1], 4, None,
                                           OP.logical_shift_right))
                    ws.wait(v)
                    ws.inc(v.tensor_scalar(ii_sb[:, 0:1], ii_sb[:, 0:1], 7, None, OP.bitwise_and))

                @blk.vector
                def _(v):
                    ws.wait(v)
                    v.tensor_copy(pint_sb[:, 0:1], ii_sb[:, 1:2])
                    v.tensor_copy(pint_sb[:, 1:2], ii_sb[:, 0:1])

            with nc.Block() as blk:
                @blk.vector
                def _(v):
                    v.tensor_scalar(tie_sb[:, :], tie_sb[:, :], TIE_STEP, None, OP.mult)
                    ins = v.tensor_scalar(scr_sb[:, 0:512], scr_sb[:, 0:512], pint_sb[:, 0:1],
                                          None, OP.is_equal)
                    ins = v.tensor_scalar(scr2_sb[:, :], scr2_sb[:, :], pint_sb[:, 1:2],
                                          None, OP.is_equal)
                    ws.inc(ins)
                    ws.wait(v)
                    v.tensor_copy(mbd2_sb[:, :], scr_sb[:, 0:512])
                    v.tensor_copy(mbdt_sb[:, :], scr2_sb[:, :])

            with nc.Block() as blk:
                @blk.gpsimd
                def _(g):
                    for esg in range(4):
                        ins = g.iota(esgi_sb[:, esg * 128:(esg + 1) * 128], [[1, 16], [0, 8]],
                                     base=esg * 16, channel_multiplier=0,
                                     allow_small_or_imprecise_dtypes=True)
                    ws.inc(ins)

                @blk.vector
                def _(v):
                    ws.wait(v)
                    v.tensor_scalar(esg_sb[:, :], esgi_sb[:, :], ppi_sb[0:64, 0:1],
                                    None, OP.is_equal)

        _es = contextlib.ExitStack()
        with _es:
            ya_sb = _es.enter_context(nc.sbuf_tensor("ya_sb", [128, 16 * 128], BF16))
            xt_sb = _es.enter_context(nc.sbuf_tensor("xt_sb", [128, DM], F32))
            o1t = _es.enter_context(nc.sbuf_tensor("o1t", [128, DM], F32))
            sqt = _es.enter_context(nc.sbuf_tensor("sqt", [128, DM], F32))
            l2 = _es.enter_context(nc.sbuf_tensor("l2", [128, DM], F32))
            l2b = _es.enter_context(nc.sbuf_tensor("l2b", [128, DM], BF16))
            l2f = _es.enter_context(nc.sbuf_tensor("l2f", [128, DM], F32))
            lgt = _es.enter_context(nc.sbuf_tensor("lgt", [64, 128], F32))
            mx_sb = _es.enter_context(nc.sbuf_tensor("mx_sb", [64, 136], F32))
            prmb = _es.enter_context(nc.sbuf_tensor("prmb", [64, 128], BF16))
            prT = _es.enter_context(nc.sbuf_tensor("prT", [128, 64], BF16))
            prTE = _es.enter_context(nc.sbuf_tensor("prTE", [128, 512], BF16))
            prBD = _es.enter_context(nc.sbuf_tensor("prBD", [128, 512], BF16))
            xmt4 = _es.enter_context(nc.sbuf_tensor("xmt4", [128, 4 * DM], BF16))
            lstt = _es.enter_context(nc.sbuf_tensor("lstt", [128, 8], F32))
            q0 = _es.enter_context(nc.psum_tensor("q0", [128, 512], F32))
            q1 = _es.enter_context(nc.psum_tensor("q1", [128, 512], F32))
            qb0 = _es.enter_context(nc.psum_tensor("qb0", [128, 512], BF16))
            qb1 = _es.enter_context(nc.psum_tensor("qb1", [128, 512], BF16))
            q = [q0, q1]
            qb = [qb0, qb1]
            for ti in range(TTOK // 128):
                with nc.Block() as blk:
                    @blk.sync
                    def _(sy):
                        ld.dma(sy.dma_start(out=xt_sb[:, :], in_=x_tok[ti * 128:(ti + 1) * 128, :]))

                    @blk.gpsimd
                    def _(g):
                        gd.dma(g.dma_start(
                            out=bass.AP(ya_sb, 0, [[2048, 128], [128, 16], [1, 128]]),
                            in_=bass.AP(y_loc, ti * 128,
                                        [[TTOK, 128], [128 * TTOK, 16], [1, 128]])))
                        gd.wait(g)

                    ev_b = ev.n
                    ws_b = ws.n
                    sc_b = sc.n

                    @blk.tensor
                    def _(t):
                        ld.wait(t)
                        gd.wait(t)
                        for f in range(2):
                            for k in range(16):
                                mm = t.matmul(q[f][:, :], ya_sb[:, k * 128:(k + 1) * 128],
                                              wo_sb[:, k * DM + f * 512:k * DM + (f + 1) * 512],
                                              start=(k == 0), stop=(k == 15))
                            ev.inc(mm)

                    @blk.vector
                    def _(v):
                        for f in range(2):
                            v.wait_ge(ev.h, ev_b + f + 1)
                            ws.inc(v.tensor_tensor(o1t[:, f * 512:(f + 1) * 512], q[f][:, :],
                                                   xt_sb[:, f * 512:(f + 1) * 512], OP.add))
                        ws.wait(v)
                        ws.inc(v.reduce_sum(lstt[:, 0:1], o1t[:, :], AX.X))

                    @blk.scalar
                    def _(s):
                        ws.wait(s)
                        sc.inc(s.activation(sqt[:, :], o1t[:, :], AF.Square, bias=0.0, scale=1.0))

                    @blk.vector
                    def _(v):
                        sc.wait(v)
                        ws.inc(v.reduce_sum(lstt[:, 1:2], sqt[:, :], AX.X))
                        ws.wait(v)
                        v.tensor_scalar(lstt[:, 2:3], lstt[:, 0:1], 1.0 / DM, None, OP.mult)
                        ws.inc(v.tensor_scalar(lstt[:, 3:4], lstt[:, 1:2], 1.0 / DM, None, OP.mult))
                        ws.wait(v)
                        ws.inc(v.tensor_tensor(lstt[:, 4:5], lstt[:, 2:3], lstt[:, 2:3], OP.mult))
                        ws.wait(v)
                        ws.inc(v.tensor_tensor(lstt[:, 5:6], lstt[:, 3:4], lstt[:, 4:5], OP.subtract))

                    @blk.scalar
                    def _(s):
                        ws.wait(s)
                        sc.inc(s.activation(lstt[:, 6:7], lstt[:, 5:6], AF.Ln, bias=EPS, scale=1.0))
                        sc.wait(s)
                        sc.inc(s.activation(lstt[:, 7:8], lstt[:, 6:7], AF.Exp, bias=0.0, scale=-0.5))

                    @blk.vector
                    def _(v):
                        sc.wait(v)
                        ws.inc(v.tensor_scalar(l2[:, :], o1t[:, :], lstt[:, 2:3], lstt[:, 7:8],
                                               OP.subtract, OP.mult))
                        ws.wait(v)
                        ws.inc(v.tensor_copy(l2b[:, :], l2[:, :]))

                    ev_b2 = ev.n
                    sc_b2 = sc.n

                    @blk.tensor
                    def _(t):
                        ws.wait(t)
                        for grp in range(2):
                            for j in range(4):
                                c = grp * 4 + j
                                mm = t.transpose(q[grp][:, j * 128:(j + 1) * 128],
                                                 l2[:, c * 128:(c + 1) * 128], idf_sb[:, :])
                            ev.inc(mm)

                    @blk.scalar
                    def _(s):
                        for grp in range(2):
                            s.wait_ge(ev.h, ev_b2 + grp + 1)
                            sc.inc(s.activation(l2f[:, grp * 512:(grp + 1) * 512], q[grp][:, :],
                                                AF.Copy, bias=0.0, scale=1.0))

                    ev_b3 = ev.n
                    ws_b3 = ws.n

                    @blk.tensor
                    def _(t):
                        t.wait_ge(sc.h, sc.n)
                        for k in range(8):
                            mm = t.matmul(q0[0:64, 0:128], ctrl_sb[:, k * 64:(k + 1) * 64],
                                          l2f[:, k * 128:(k + 1) * 128],
                                          start=(k == 0), stop=(k == 7))
                        ev.inc(mm)

                    @blk.vector
                    def _(v):
                        v.wait_ge(ev.h, ev_b3 + 1)
                        ws.inc(v.tensor_tensor(lgt[:, :], q0[0:64, 0:128], tie_sb[:, :], OP.add))
                        ws.wait(v)
                        lg3 = bass.AP(lgt, 0, [[128, 64], [16, 8], [1, 16]])
                        ws.inc(v.tensor_reduce(mx_sb[:, 0:8], lg3, AX.X, OP.max))
                        ws.wait(v)
                        for tt_ in range(16):
                            ins = v.tensor_copy(bass.AP(mx_sb, 8 + tt_, [[136, 64], [16, 8]]),
                                                mx_sb[:, 0:8])
                        ws.inc(ins)
                        ws.wait(v)
                        ws.inc(v.tensor_tensor(perm_sb[:, ti * 128:(ti + 1) * 128], lgt[:, :],
                                               mx_sb[:, 8:136], OP.is_equal))
                        ws.wait(v)
                        ws.inc(v.tensor_copy(prmb[:, :], perm_sb[:, ti * 128:(ti + 1) * 128]))

                    ev_b4 = ev.n
                    sc_b4 = sc.n

                    @blk.tensor
                    def _(t):
                        ws.wait(t)
                        ev.inc(t.transpose(qb0[0:128, 0:64], prmb[:, :], idb_sb[0:64, 0:64]))

                    @blk.scalar
                    def _(s):
                        s.wait_ge(ev.h, ev_b4 + 1)
                        sc.inc(s.activation(prT[:, :], qb0[0:128, 0:64], AF.Copy, bias=0.0, scale=1.0))

                    @blk.vector
                    def _(v):
                        v.wait_ge(sc.h, sc_b4 + 1)
                        for j in range(8):
                            ins = v.tensor_copy(bass.AP(prTE, j, [[512, 128], [8, 64]]), prT[:, :])
                        ws.inc(ins)
                        ws.wait(v)
                        ws.inc(v.tensor_tensor(prBD[:, :], prTE[:, :], mbd2_sb[:, :], OP.mult))

                    ev_b5 = ev.n
                    sc_b5 = sc.n

                    @blk.tensor
                    def _(t):
                        ws.wait(t)
                        for qq in range(4):
                            for f in range(2):
                                g = qq * 2 + f
                                if g >= 2:
                                    t.wait_ge(sc.h, sc_b5 + g - 1)
                                mm = t.matmul(q[g % 2][:, :], prBD[:, qq * 128:(qq + 1) * 128],
                                              l2b[:, f * 512:(f + 1) * 512], start=True, stop=True)
                                ev.inc(mm)

                    @blk.scalar
                    def _(s):
                        for g in range(8):
                            qq, f = g // 2, g % 2
                            s.wait_ge(ev.h, ev_b5 + g + 1)
                            sc.inc(s.activation(
                                xmt4[:, qq * DM + f * 512:qq * DM + (f + 1) * 512],
                                q[g % 2][:, :], AF.Copy, bias=0.0, scale=1.0))

                    @blk.sync
                    def _(sy):
                        sc.wait(sy)
                        st.dma(sy.dma_start(out=o1_sp[ti * 128:(ti + 1) * 128, :], in_=o1t[:, :]))
                        for qq in range(4):
                            st.dma(sy.dma_start(
                                out=xmt_sp[ti * 512 + qq * 128:ti * 512 + (qq + 1) * 128, :],
                                in_=xmt4[:, qq * DM:(qq + 1) * DM]))
                        st.wait(sy)

        _p6b(E, dict(locals()))
        _p6c(E, dict(locals()))

        if d_perm is not None:
            with nc.Block() as blk:
                @blk.sync
                def _(sy):
                    st.dma(sy.dma_start(out=d_perm[:, :], in_=perm_sb[:, :]))
                    st.wait(sy)


def _p6b(E, env):
    nc = E["nc"]
    ld, gd, st, ev, sc, ws = E["ld"], E["gd"], E["st"], E["ev"], E["sc"], E["ws"]
    f1t, f2m, fbias = E["f1t"], E["f2m"], E["fbias"]
    xmt_sp, int_sp = E["xmt_sp"], E["int_sp"]

    _es = contextlib.ExitStack()
    with _es:
        xmte = _es.enter_context(nc.sbuf_tensor("xmte", [128, 2048], BF16))
        xme = _es.enter_context(nc.sbuf_tensor("xme", [128, 8 * 256], BF16))
        f1es = _es.enter_context(nc.sbuf_tensor("f1es", [128, 512], BF16))
        f2es = _es.enter_context(nc.sbuf_tensor("f2es", [64, DM], BF16))
        fb_sb = _es.enter_context(nc.sbuf_tensor("fb_sb", [64, 1], F32))
        inner_sb = _es.enter_context(nc.sbuf_tensor("inner_sb", [64, 256], BF16))
        intr = _es.enter_context(nc.sbuf_tensor("intr", [128, 2 * DM], BF16))
        pb0 = _es.enter_context(nc.psum_tensor("pb0", [128, 512], BF16))
        pb1 = _es.enter_context(nc.psum_tensor("pb1", [128, 512], BF16))
        pf = _es.enter_context(nc.psum_tensor("pf", [128, 512], F32))
        pg0 = _es.enter_context(nc.psum_tensor("pg0", [128, 512], F32))
        pg1 = _es.enter_context(nc.psum_tensor("pg1", [128, 512], F32))
        pb = [pb0, pb1]
        pg = [pg0, pg1]
        for es in range(64):
            q = es // 16
            off = (q * 128 + (es % 16) * 8) * DM
            with nc.Block() as blk:
                @blk.sync
                def _(sy):
                    for half in range(2):
                        src = bass.AP(xmt_sp, off + half * 16 * 512 * DM,
                                      [[512 * DM, 16], [DM, 8], [1, DM]])
                        ld.dma(sy.dma_start(out=xmte[:, half * DM:(half + 1) * DM], in_=src))
                    ld.dma(sy.dma_start(out=fb_sb[:, 0:1],
                                        in_=bass.AP(fbias, es * 64, [[1, 64], [1, 1]])))

                @blk.gpsimd
                def _(g):
                    gd.dma(g.dma_start(out=f1es[:, :],
                                       in_=bass.AP(f1t, es * 128 * 512, [[512, 128], [1, 512]])))
                    gd.dma(g.dma_start(out=f2es[:, :],
                                       in_=bass.AP(f2m, es * 64 * DM, [[DM, 64], [1, DM]])))

                ev_b = ev.n
                sc_b = sc.n

                @blk.tensor
                def _(t):
                    ld.wait(t)
                    gd.wait(t)
                    for grp in range(4):
                        if grp >= 2:
                            t.wait_ge(sc.h, sc_b + (grp - 1) * 4)
                        for j in range(4):
                            g = grp * 4 + j
                            half, c = g // 8, g % 8
                            mm = t.transpose(pb[grp % 2][:, j * 128:(j + 1) * 128],
                                             xmte[:, half * DM + c * 128:half * DM + (c + 1) * 128],
                                             E["idb_sb"][:, :])
                        ev.inc(mm)

                @blk.scalar
                def _(s):
                    for grp in range(4):
                        s.wait_ge(ev.h, ev_b + grp + 1)
                        for j in range(4):
                            g = grp * 4 + j
                            half, c = g // 8, g % 8
                            sc.inc(s.activation(
                                xme[:, c * 256 + half * 128:c * 256 + (half + 1) * 128],
                                pb[grp % 2][:, j * 128:(j + 1) * 128],
                                AF.Copy, bias=0.0, scale=1.0))

                ev_b2 = ev.n
                sc_b2 = sc.n

                @blk.tensor
                def _(t):
                    t.wait_ge(sc.h, sc.n)
                    for c in range(8):
                        mm = t.matmul(pf[0:64, 0:256], f1es[:, c * 64:(c + 1) * 64],
                                      xme[:, c * 256:(c + 1) * 256],
                                      start=(c == 0), stop=(c == 7))
                    ev.inc(mm)

                @blk.scalar
                def _(s):
                    s.wait_ge(ev.h, ev_b2 + 1)
                    sc.inc(s.activation(inner_sb[:, :], pf[0:64, 0:256], AF.Relu,
                                        bias=fb_sb[:, 0:1], scale=1.0))

                ev_b3 = ev.n
                sc_b3 = sc.n

                @blk.tensor
                def _(t):
                    t.wait_ge(sc.h, sc.n)
                    for m in range(2):
                        for f in range(2):
                            g = m * 2 + f
                            if g >= 2:
                                t.wait_ge(sc.h, sc_b3 + g - 1)
                            mm = t.matmul(pg[g % 2][:, :], inner_sb[:, m * 128:(m + 1) * 128],
                                          f2es[:, f * 512:(f + 1) * 512], start=True, stop=True)
                            ev.inc(mm)

                @blk.scalar
                def _(s):
                    for g in range(4):
                        m, f = g // 2, g % 2
                        s.wait_ge(ev.h, ev_b3 + g + 1)
                        sc.inc(s.activation(intr[:, m * DM + f * 512:m * DM + (f + 1) * 512],
                                            pg[g % 2][:, :], AF.Copy, bias=0.0, scale=1.0))

                @blk.sync
                def _(sy):
                    sc.wait(sy)
                    for m in range(2):
                        st.dma(sy.dma_start(
                            out=int_sp[es * 256 + m * 128:es * 256 + (m + 1) * 128, :],
                            in_=intr[:, m * DM:(m + 1) * DM]))
                    st.wait(sy)


def _p6c(E, env):
    nc = E["nc"]
    ld, st, ev, sc, ws = E["ld"], E["st"], E["ev"], E["sc"], E["ws"]
    x_tok, out_tok, int_sp, o1_sp = E["x_tok"], E["out_tok"], E["int_sp"], E["o1_sp"]
    perm_sb = env["perm_sb"]
    esg_sb = env["esg_sb"]
    mbdt_sb = env["mbdt_sb"]

    _es = contextlib.ExitStack()
    with _es:
        pbdt = _es.enter_context(nc.sbuf_tensor("pbdt", [128, 4 * TTOK], BF16))
        scrE = _es.enter_context(nc.sbuf_tensor("scrE", [128, 512], F32))
        ir0 = _es.enter_context(nc.sbuf_tensor("ir0", [128, DM], BF16))
        ir1 = _es.enter_context(nc.sbuf_tensor("ir1", [128, DM], BF16))
        ir2 = _es.enter_context(nc.sbuf_tensor("ir2", [128, DM], BF16))
        ir3 = _es.enter_context(nc.sbuf_tensor("ir3", [128, DM], BF16))
        o1r = _es.enter_context(nc.sbuf_tensor("o1r", [128, DM], F32))
        outt = _es.enter_context(nc.sbuf_tensor("outt", [128, DM], F32))
        pp0 = _es.enter_context(nc.psum_tensor("pp0", [128, 512], F32))
        pp1 = _es.enter_context(nc.psum_tensor("pp1", [128, 512], F32))
        irs = [ir0, ir1, ir2, ir3]
        pp = [pp0, pp1]
        for esg in range(4):
            with nc.Block() as blk:
                ev_b = ev.n
                sc_b = sc.n
                ws_b = ws.n

                @blk.tensor
                def _(t):
                    for F in range(8):
                        if F >= 2:
                            t.wait_ge(sc.h, sc_b + F - 1)
                        mm = t.matmul(pp[F % 2][:, :], esg_sb[:, esg * 128:(esg + 1) * 128],
                                      perm_sb[:, F * 512:(F + 1) * 512], start=True, stop=True)
                        ev.inc(mm)

                @blk.scalar
                def _(s):
                    for F in range(8):
                        s.wait_ge(ev.h, ev_b + F + 1)
                        s.wait_ge(ws.h, ws_b + F)
                        sc.inc(s.activation(scrE[:, :], pp[F % 2][:, :], AF.Copy,
                                            bias=0.0, scale=1.0))

                @blk.vector
                def _(v):
                    for F in range(8):
                        v.wait_ge(sc.h, sc_b + F + 1)
                        ws.inc(v.tensor_tensor(pbdt[:, esg * TTOK + F * 512:esg * TTOK + (F + 1) * 512],
                                               scrE[:, :], mbdt_sb[:, F * 512 % 4096:(F * 512 % 4096) + 512],
                                               OP.mult))

        for ti in range(TTOK // 128):
            with nc.Block() as blk:
                @blk.sync
                def _(sy):
                    for esg in range(4):
                        src = bass.AP(int_sp, ((esg * 16) * 256 + ti * 8) * DM,
                                      [[256 * DM, 16], [DM, 8], [1, DM]])
                        ld.dma(sy.dma_start(out=irs[esg][:, :], in_=src))
                    ld.dma(sy.dma_start(out=o1r[:, :], in_=o1_sp[ti * 128:(ti + 1) * 128, :]))

                ev_b = ev.n

                @blk.tensor
                def _(t):
                    ld.wait(t)
                    for f in range(2):
                        for esg in range(4):
                            mm = t.matmul(pp[f][:, :], pbdt[:, esg * TTOK + ti * 128:esg * TTOK + (ti + 1) * 128],
                                          irs[esg][:, f * 512:(f + 1) * 512],
                                          start=(esg == 0), stop=(esg == 3))
                        ev.inc(mm)

                @blk.vector
                def _(v):
                    for f in range(2):
                        v.wait_ge(ev.h, ev_b + f + 1)
                        ws.inc(v.tensor_tensor(outt[:, f * 512:(f + 1) * 512], pp[f][:, :],
                                               o1r[:, f * 512:(f + 1) * 512], OP.add))

                @blk.sync
                def _(sy):
                    ws.wait(sy)
                    st.dma(sy.dma_start(out=out_tok[ti * 128:(ti + 1) * 128, :], in_=outt[:, :]))
                    st.wait(sy)


_NC_CACHE = {}


def _host_prep(inputs):
    x = np.ascontiguousarray(np.asarray(inputs["L0_attn_x"], dtype=np.float32))
    W_in = np.asarray(inputs["L0_attn_W_in"], dtype=np.float32)
    W_conv = np.asarray(inputs["L0_attn_W_conv"], dtype=np.float32)
    W_xproj = np.asarray(inputs["L0_attn_W_xproj"], dtype=np.float32)
    W_dt = np.asarray(inputs["L0_attn_W_dt"], dtype=np.float32)
    b_dt_a = np.asarray(inputs["L0_attn_b_dt"], dtype=np.float32)
    A_log = np.asarray(inputs["L0_attn_A_log"], dtype=np.float32)
    D_a = np.asarray(inputs["L0_attn_D"], dtype=np.float32)
    W_out = np.asarray(inputs["L0_attn_W_out"], dtype=np.float32)
    controller = np.asarray(inputs["L0_ffn_controller"], dtype=np.float32)
    f1 = np.asarray(inputs["L0_ffn_f1"], dtype=np.float32)
    f2 = np.asarray(inputs["L0_ffn_f2"], dtype=np.float32)
    fb = np.asarray(inputs["L0_ffn_bias"], dtype=np.float32)

    def ktiles(a):
        K, M = a.shape
        return np.ascontiguousarray(a.reshape(K // 128, 128, M).transpose(1, 0, 2).reshape(128, -1))

    w_out_h = ktiles(np.ascontiguousarray(W_out.T))
    ctrl_h = ktiles(controller.reshape(DM, 64))
    f1t_h = np.ascontiguousarray(
        f1.transpose(1, 2, 0, 3).reshape(64, DM, 64).reshape(64, 8, 128, 64)
        .transpose(0, 2, 1, 3).reshape(64, 128, 512))
    f2m_h = np.ascontiguousarray(f2.reshape(64, 64, DM))
    fb_h = np.ascontiguousarray(fb.reshape(64, 64))

    in_maps = []
    for core in range(8):
        b, g = core // 4, core % 4
        sl = slice(g * DC, (g + 1) * DC)
        d = dict(
            x_tok=np.ascontiguousarray(x[b, g * TTOK:(g + 1) * TTOK, :]),
            w_xc=ktiles(np.ascontiguousarray(W_in[sl].T)),
            w_z=ktiles(np.ascontiguousarray(W_in[DIN + g * DC:DIN + (g + 1) * DC].T)),
            w_conv=np.ascontiguousarray(W_conv[sl, 0, :]),
            w_xp=ktiles(np.ascontiguousarray(W_xproj[:, sl].T)),
            w_dt=np.ascontiguousarray(W_dt[sl].T),
            b_dt=np.ascontiguousarray(b_dt_a[sl, None]),
            a_log=np.ascontiguousarray(A_log[sl]),
            dvec=np.ascontiguousarray(D_a[sl, None]),
            w_out=w_out_h,
            ctrl=ctrl_h,
            f1t=f1t_h,
            f2m=f2m_h,
            fbias=fb_h,
            coff=(g * TTOK + 128 * np.arange(32, dtype=np.int32))[None, :],
        )
        in_maps.append(d)
    return in_maps


def run(inputs, dbg=(), upto="full"):
    key = (tuple(sorted(dbg)), upto)
    if key not in _NC_CACHE:
        _NC_CACHE[key] = build(dbg, upto)
    nc = _NC_CACHE[key]
    in_maps = _host_prep(inputs)
    res = run_bass_kernel_spmd(nc, in_maps, core_ids=list(range(8)))
    return res


def kernel(**inputs):
    res = run(inputs)
    out = np.empty((B, LSEQ, DM), np.float32)
    for core in range(8):
        b, g = core // 4, core % 4
        out[b, g * TTOK:(g + 1) * TTOK, :] = res.results[core]["out_tok"]
    return out


# version 06 (final)
# speedup vs baseline: 1.1061x; 1.0133x over previous
import contextlib
import numpy as np
import ml_dtypes

import concourse.bass as bass
import concourse.mybir as mybir
from concourse.bass_utils import run_bass_kernel_spmd

F32 = mybir.dt.float32
BF16 = mybir.dt.bfloat16
OP = mybir.AluOpType
AF = mybir.ActivationFunctionType
AX = mybir.AxisListType

B, LSEQ, DM = 2, 16384, 1024
DIN, NST, DTR = 2048, 16, 64
DC = 512
TTOK = 4096
NDT = DC // 128
P2C = 2048
NP2 = LSEQ // P2C
P4C = 512
NP4 = LSEQ // P4C
EPS = 1e-5
TIE_STEP = 1e-6 / 15.0
BF = ml_dtypes.bfloat16


class Sem:
    def __init__(self, nc, es, name):
        self.h = es.enter_context(nc.semaphore(name))
        self.n = 0

    def inc(self, instr, k=1):
        instr.then_inc(self.h, k)
        self.n += k

    def dma(self, instr):
        self.inc(instr, 16)

    def wait(self, eng):
        eng.wait_ge(self.h, self.n)


def build(dbg=(), upto="full"):
    nc = bass.Bass("TRN2", target_bir_lowering=False)
    E = {}

    x_tok = nc.dram_tensor("x_tok", [TTOK, DM], F32, kind="ExternalInput")
    w_xc = nc.dram_tensor("w_xc", [128, 8 * DC], F32, kind="ExternalInput")
    w_z = nc.dram_tensor("w_z", [128, 8 * DC], F32, kind="ExternalInput")
    w_conv = nc.dram_tensor("w_conv", [DC, 4], F32, kind="ExternalInput")
    w_xp = nc.dram_tensor("w_xp", [128, 4 * 96], F32, kind="ExternalInput")
    w_dt = nc.dram_tensor("w_dt", [64, DC], F32, kind="ExternalInput")
    b_dt = nc.dram_tensor("b_dt", [DC, 1], F32, kind="ExternalInput")
    a_log = nc.dram_tensor("a_log", [DC, NST], F32, kind="ExternalInput")
    dvec = nc.dram_tensor("dvec", [DC, 1], F32, kind="ExternalInput")
    w_out = nc.dram_tensor("w_out", [128, 16 * DM], F32, kind="ExternalInput")
    ctrl = nc.dram_tensor("ctrl", [128, 8 * 64], F32, kind="ExternalInput")
    f1t = nc.dram_tensor("f1t", [64, 128, 8 * 64], F32, kind="ExternalInput")
    f2m = nc.dram_tensor("f2m", [64, 64, DM], F32, kind="ExternalInput")
    fbias = nc.dram_tensor("fbias", [64, 64], F32, kind="ExternalInput")
    coff = nc.dram_tensor("coff", [1, 32], mybir.dt.int32, kind="ExternalInput")
    out_tok = nc.dram_tensor("out_tok", [TTOK, DM], F32, kind="ExternalOutput")

    ln1_my = nc.dram_tensor("ln1_my", [DM, TTOK], BF16)
    ln1_all = nc.dram_tensor("ln1_all", [4 * DM, TTOK], BF16)
    dbc_part = nc.dram_tensor("dbc_part", [96, LSEQ], F32)
    dbc_full = nc.dram_tensor("dbc_full", [96, LSEQ], F32)
    xc_sp = nc.dram_tensor("xc_sp", [DC, LSEQ], F32)
    ymid = nc.dram_tensor("ymid", [DC, LSEQ], BF16)
    y_all = nc.dram_tensor("y_all", [4 * DC, LSEQ], BF16)
    xmt_sp = nc.dram_tensor("xmt_sp", [32 * 512, DM], BF16)
    int_sp = nc.dram_tensor("int_sp", [64 * 256, DM], BF16)
    o1_sp = nc.dram_tensor("o1_sp", [TTOK, DM], F32)
    y_loc = nc.dram_tensor("y_loc", [4 * DC, TTOK], BF16)

    def dbg_t(name, shape, dtype=F32):
        if name in dbg:
            return nc.dram_tensor("dbg_" + name, shape, dtype, kind="ExternalOutput")
        return None

    d_ln = dbg_t("ln", [4 * DM, TTOK], BF16)
    d_xc = dbg_t("xc", [DC, LSEQ], F32)
    d_dbc = dbg_t("dbc", [96, LSEQ], F32)
    d_delta = dbg_t("delta", [DC, LSEQ], F32)
    d_ymid = dbg_t("ymid", [DC, LSEQ], BF16)
    d_o1 = dbg_t("o1", [TTOK, DM], F32)
    d_perm = dbg_t("perm", [64, TTOK], F32)
    d_xmt = dbg_t("xmt", [32 * 512, DM], BF16)
    d_int = dbg_t("int", [64 * 256, DM], BF16)

    es_ = contextlib.ExitStack()
    with es_:
        ld = Sem(nc, es_, "ld")
        gd = Sem(nc, es_, "gd")
        st = Sem(nc, es_, "st")
        ev = Sem(nc, es_, "ev")
        sc = Sem(nc, es_, "sc")
        ws = Sem(nc, es_, "ws")
        cc1 = Sem(nc, es_, "cc1")
        cc2 = Sem(nc, es_, "cc2")
        cc3 = Sem(nc, es_, "cc3")
        pc = Sem(nc, es_, "pc")
        E.update(locals())

        wxc_sb = es_.enter_context(nc.sbuf_tensor("wxc_sb", [128, 8 * DC], BF16))
        wz_sb = es_.enter_context(nc.sbuf_tensor("wz_sb", [128, 8 * DC], BF16))
        wxp_sb = es_.enter_context(nc.sbuf_tensor("wxp_sb", [128, 4 * 96], BF16))
        wdt_sb = es_.enter_context(nc.sbuf_tensor("wdt_sb", [64, DC], BF16))
        bdt_sb = es_.enter_context(nc.sbuf_tensor("bdt_sb", [128, NDT], F32))
        dv_sb = es_.enter_context(nc.sbuf_tensor("dv_sb", [128, NDT], F32))
        amat_sb = es_.enter_context(nc.sbuf_tensor("amat_sb", [128, NDT * NST], F32))
        wcv_sb = es_.enter_context(nc.sbuf_tensor("wcv_sb", [128, NDT * 4], F32))
        convd_sb = es_.enter_context(nc.sbuf_tensor("convd_sb", [128, NDT * 4 * 128], BF16))
        idf_sb = es_.enter_context(nc.sbuf_tensor("idf_sb", [128, 128], F32))
        idb_sb = es_.enter_context(nc.sbuf_tensor("idb_sb", [128, 128], BF16))
        sel16_sb = es_.enter_context(nc.sbuf_tensor("sel16_sb", [16, 16 * 128], F32))
        ppi_sb = es_.enter_context(nc.sbuf_tensor("ppi_sb", [128, 2], F32))
        cst_sb = es_.enter_context(nc.sbuf_tensor("cst_sb", [128, 4], F32))
        nc.const_aps.register(F32, EPS, cst_sb[:, 0:1])
        nc.const_aps.register(F32, 1.0, cst_sb[:, 1:2])
        nc.const_aps.register(F32, -1.0, cst_sb[:, 2:3])
        E.update(locals())

        with nc.Block() as blk:
            @blk.gpsimd
            def _(g):
                for s_ in (ld, gd, st, ev, sc, ws, cc1, cc2, cc3, pc):
                    g.sem_clear(s_.h)

        with nc.Block() as blk:
            @blk.sync
            def _(sy):
                for t in range(NDT):
                    ld.dma(sy.dma_start(out=bdt_sb[:, t:t + 1], in_=b_dt[t * 128:(t + 1) * 128, :]))
                    ld.dma(sy.dma_start(out=dv_sb[:, t:t + 1], in_=dvec[t * 128:(t + 1) * 128, :]))
                    ld.dma(sy.dma_start(out=amat_sb[:, t * NST:(t + 1) * NST],
                                        in_=a_log[t * 128:(t + 1) * 128, :]))
                    ld.dma(sy.dma_start(out=wcv_sb[:, t * 4:(t + 1) * 4],
                                        in_=w_conv[t * 128:(t + 1) * 128, :]))
                ld.wait(sy)

            @blk.vector
            def _(v):
                v.memset(cst_sb[:, 0:1], EPS)
                v.memset(cst_sb[:, 1:2], 1.0)
                v.memset(cst_sb[:, 2:3], -1.0)
                ws.inc(v.memset(cst_sb[:, 3:4], 0.0))

            @blk.gpsimd
            def _(g):
                gd.dma(g.dma_start(out=wxc_sb[:, :], in_=w_xc[:, :]))
                gd.dma(g.dma_start(out=wz_sb[:, :], in_=w_z[:, :]))
                gd.dma(g.dma_start(out=wxp_sb[:, :], in_=w_xp[:, :]))
                gd.dma(g.dma_start(out=wdt_sb[:, :], in_=w_dt[:, :]))
                gd.wait(g)
                g.iota(ppi_sb[:, 0:1], [[0, 1]], channel_multiplier=1,
                       allow_small_or_imprecise_dtypes=True)
                g.iota(idf_sb[:, :], [[1, 128]], channel_multiplier=0,
                       allow_small_or_imprecise_dtypes=True)
                g.iota(sel16_sb[:, :], [[1, 16], [0, 128]], channel_multiplier=0,
                       allow_small_or_imprecise_dtypes=True)

        with nc.Block() as blk:
            @blk.scalar
            def _(s):
                s.activation(amat_sb[:, :], amat_sb[:, :], AF.Exp, bias=0.0, scale=1.0)

        with nc.Block() as blk:
            @blk.vector
            def _(v):
                v.tensor_scalar(amat_sb[:, :], amat_sb[:, :], -1.0, None, OP.mult)
                v.tensor_scalar(idf_sb[:, :], idf_sb[:, :], ppi_sb[:, 0:1], None, OP.is_equal)
                ws.inc(v.tensor_scalar(sel16_sb[:, :], sel16_sb[:, :], ppi_sb[0:16, 0:1],
                                       None, OP.is_equal))

            @blk.vector
            def _(v):
                ws.wait(v)
                v.tensor_copy(idb_sb[:, :], idf_sb[:, :])
                for dt in range(NDT):
                    for tap in range(4):
                        v.scalar_tensor_tensor(
                            convd_sb[:, (dt * 4 + tap) * 128:(dt * 4 + tap + 1) * 128],
                            idf_sb[:, :], wcv_sb[:, dt * 4 + tap:dt * 4 + tap + 1],
                            idf_sb[:, :], OP.mult, OP.bypass)

        _es = contextlib.ExitStack()
        with _es:
            xt8 = _es.enter_context(nc.sbuf_tensor("xt8", [128, 8 * DM], F32))
            sq8 = _es.enter_context(nc.sbuf_tensor("sq8", [128, 8 * DM], F32))
            ln8 = _es.enter_context(nc.sbuf_tensor("ln8", [128, 8 * DM], BF16))
            lnT8 = _es.enter_context(nc.sbuf_tensor("lnT8", [128, 8 * DM], BF16))
            lst = _es.enter_context(nc.sbuf_tensor("lst", [128, 64], F32))
            psT0 = _es.enter_context(nc.psum_tensor("psT0", [128, 512], BF16))
            psT1 = _es.enter_context(nc.psum_tensor("psT1", [128, 512], BF16))
            psT = [psT0, psT1]
            for tb in range(TTOK // 128 // 8):
                with nc.Block() as blk:
                    @blk.sync
                    def _(sy):
                        for i in range(8):
                            ti = tb * 8 + i
                            ld.dma(sy.dma_start(out=xt8[:, i * DM:(i + 1) * DM],
                                                in_=x_tok[ti * 128:(ti + 1) * 128, :]))

                    @blk.scalar
                    def _(s):
                        ld.wait(s)
                        for i in range(8):
                            sc.inc(s.activation(sq8[:, i * DM:(i + 1) * DM],
                                                xt8[:, i * DM:(i + 1) * DM],
                                                AF.Square, bias=0.0, scale=1.0))

                    @blk.vector
                    def _(v):
                        ld.wait(v)
                        for i in range(8):
                            ins = v.reduce_sum(lst[:, i * 8:i * 8 + 1], xt8[:, i * DM:(i + 1) * DM],
                                               AX.X)
                        sc.wait(v)
                        for i in range(8):
                            ins = v.reduce_sum(lst[:, i * 8 + 1:i * 8 + 2],
                                               sq8[:, i * DM:(i + 1) * DM], AX.X)
                        ws.inc(ins)
                        ws.wait(v)
                        for i in range(8):
                            c = i * 8
                            ins = v.tensor_scalar(lst[:, c + 2:c + 3], lst[:, c:c + 1],
                                                  1.0 / DM, None, OP.mult)
                            ins = v.tensor_scalar(lst[:, c + 3:c + 4], lst[:, c + 1:c + 2],
                                                  1.0 / DM, None, OP.mult)
                        ws.inc(ins)
                        ws.wait(v)
                        for i in range(8):
                            c = i * 8
                            ins = v.tensor_tensor(lst[:, c + 4:c + 5], lst[:, c + 2:c + 3],
                                                  lst[:, c + 2:c + 3], OP.mult)
                        ws.inc(ins)
                        ws.wait(v)
                        for i in range(8):
                            c = i * 8
                            ins = v.tensor_tensor(lst[:, c + 5:c + 6], lst[:, c + 3:c + 4],
                                                  lst[:, c + 4:c + 5], OP.subtract)
                        ws.inc(ins)

                    @blk.scalar
                    def _(s):
                        ws.wait(s)
                        for i in range(8):
                            c = i * 8
                            ins = s.activation(lst[:, c + 6:c + 7], lst[:, c + 5:c + 6],
                                               AF.Ln, bias=EPS, scale=1.0)
                        sc.inc(ins)
                        sc.wait(s)
                        for i in range(8):
                            c = i * 8
                            sc.inc(s.activation(lst[:, c + 7:c + 8], lst[:, c + 6:c + 7],
                                                AF.Exp, bias=0.0, scale=-0.5))

                    @blk.vector
                    def _(v):
                        sc.wait(v)
                        for i in range(8):
                            c = i * 8
                            ws.inc(v.tensor_scalar(ln8[:, i * DM:(i + 1) * DM],
                                                   xt8[:, i * DM:(i + 1) * DM],
                                                   lst[:, c + 2:c + 3], lst[:, c + 7:c + 8],
                                                   OP.subtract, OP.mult))

                    ev_b = ev.n
                    sc_b = sc.n

                    @blk.tensor
                    def _(t):
                        ws.wait(t)
                        for grp in range(16):
                            if grp >= 2:
                                t.wait_ge(sc.h, sc_b + grp - 1)
                            for j in range(4):
                                g = grp * 4 + j
                                i, c = g // 8, g % 8
                                mm = t.transpose(psT[grp % 2][:, j * 128:(j + 1) * 128],
                                                 ln8[:, i * DM + c * 128:i * DM + (c + 1) * 128],
                                                 idb_sb[:, :])
                            ev.inc(mm)

                    @blk.scalar
                    def _(s):
                        for grp in range(16):
                            s.wait_ge(ev.h, ev_b + grp + 1)
                            g0 = grp * 4
                            i = g0 // 8
                            c = g0 % 8
                            sc.inc(s.activation(lnT8[:, i * DM + c * 128:i * DM + (c + 4) * 128],
                                                psT[grp % 2][:, :], AF.Copy, bias=0.0, scale=1.0))

                    @blk.sync
                    def _(sy):
                        sc.wait(sy)
                        for i in range(8):
                            ti = tb * 8 + i
                            dst = bass.AP(ln1_my, ti * 128, [[TTOK, 128], [128 * TTOK, 8], [1, 128]])
                            src = bass.AP(lnT8, i * DM, [[8 * DM, 128], [128, 8], [1, 128]])
                            st.dma(sy.dma_start(out=dst, in_=src))
                        st.wait(sy)

        with nc.Block() as blk:
            @blk.gpsimd
            def _(g):
                cc1.inc(g.collective_compute(
                    "AllGather", OP.bypass,
                    replica_groups=[[0, 1, 2, 3], [4, 5, 6, 7]],
                    ins=[ln1_my.ap().opt()], outs=[ln1_all.ap().opt()]))
                g.wait_ge(cc1.h, 1)

        if upto != "p1":
            _es = contextlib.ExitStack()
            with _es:
                rhs8 = _es.enter_context(nc.sbuf_tensor("rhs8", [128, 8 * P2C], BF16))
                xcpre = _es.enter_context(nc.sbuf_tensor("xcpre", [128, NDT * (P2C + 3)], BF16))
                xcf = _es.enter_context(nc.sbuf_tensor("xcf", [128, NDT * P2C], F32))
                xcb = _es.enter_context(nc.sbuf_tensor("xcb", [128, NDT * P2C], BF16))
                dbc_sb = _es.enter_context(nc.sbuf_tensor("dbc_sb", [96, P2C], F32))
                psA0 = _es.enter_context(nc.psum_tensor("psA0", [128, 512], F32))
                psA1 = _es.enter_context(nc.psum_tensor("psA1", [128, 512], F32))
                psB0 = _es.enter_context(nc.psum_tensor("psB0", [128, 512], F32))
                psB1 = _es.enter_context(nc.psum_tensor("psB1", [128, 512], F32))
                psC = _es.enter_context(nc.psum_tensor("psC", [128, 512], F32))
                psA = [psA0, psA1]
                psB = [psB0, psB1]
                XP3 = P2C + 3
                for ch in range(NP2):
                    rank = (ch * P2C) // TTOK
                    col0 = ch * P2C - rank * TTOK
                    with nc.Block() as blk:
                        @blk.sync
                        def _(sy):
                            for cc in range(8):
                                src = bass.AP(ln1_all, (rank * DM + cc * 128) * TTOK + col0,
                                              [[TTOK, 128], [1, P2C]])
                                ld.dma(sy.dma_start(out=rhs8[:, cc * P2C:(cc + 1) * P2C], in_=src))

                        @blk.vector
                        def _(v):
                            for dt in range(NDT):
                                o = dt * XP3
                                if ch == 0:
                                    ws.inc(v.memset(xcpre[:, o:o + 3], 0.0))
                                else:
                                    ws.inc(v.tensor_copy(xcpre[:, o:o + 3],
                                                         xcpre[:, o + P2C:o + P2C + 3]))

                        ev_b = ev.n
                        sc_b = sc.n

                        @blk.tensor
                        def _(t):
                            ld.wait(t)
                            ws.wait(t)
                            for dt in range(NDT):
                                for f in range(4):
                                    g = dt * 4 + f
                                    if g >= 2:
                                        t.wait_ge(sc.h, sc_b + g - 1)
                                    for k in range(8):
                                        mm = t.matmul(psA[g % 2][:, :],
                                                      wxc_sb[:, k * DC + dt * 128:k * DC + (dt + 1) * 128],
                                                      rhs8[:, k * P2C + f * 512:k * P2C + (f + 1) * 512],
                                                      start=(k == 0), stop=(k == 7))
                                    ev.inc(mm)
                            for dt in range(NDT):
                                for f in range(4):
                                    g2 = dt * 4 + f
                                    t.wait_ge(sc.h, sc_b + (dt + 1) * 4)
                                    if g2 >= 2:
                                        t.wait_ge(sc.h, sc_b + 16 + g2 - 1)
                                    o = dt * XP3 + f * 512
                                    for tap in range(4):
                                        mm = t.matmul(psB[g2 % 2][:, :],
                                                      convd_sb[:, (dt * 4 + tap) * 128:(dt * 4 + tap + 1) * 128],
                                                      xcpre[:, o + tap:o + tap + 512],
                                                      start=(tap == 0), stop=(tap == 3))
                                    ev.inc(mm)

                        @blk.scalar
                        def _(s):
                            for g in range(16):
                                dt, f = g // 4, g % 4
                                s.wait_ge(ev.h, ev_b + g + 1)
                                sc.inc(s.activation(
                                    xcpre[:, dt * XP3 + 3 + f * 512:dt * XP3 + 3 + (f + 1) * 512],
                                    psA[g % 2][:, :], AF.Copy, bias=0.0, scale=1.0))
                            for g2 in range(16):
                                dt, f = g2 // 4, g2 % 4
                                s.wait_ge(ev.h, ev_b + 16 + g2 + 1)
                                sc.inc(s.activation(xcf[:, dt * P2C + f * 512:dt * P2C + (f + 1) * 512],
                                                    psB[g2 % 2][:, :], AF.Silu, bias=0.0, scale=1.0))

                        @blk.vector
                        def _(v):
                            for g2 in range(16):
                                dt, f = g2 // 4, g2 % 4
                                v.wait_ge(sc.h, sc_b + 16 + g2 + 1)
                                ws.inc(v.tensor_copy(xcb[:, dt * P2C + f * 512:dt * P2C + (f + 1) * 512],
                                                     xcf[:, dt * P2C + f * 512:dt * P2C + (f + 1) * 512]))

                    with nc.Block() as blk:
                        ev_b2 = ev.n
                        sc_b2 = sc.n

                        @blk.tensor
                        def _(t):
                            for f in range(4):
                                if f >= 1:
                                    t.wait_ge(sc.h, sc_b2 + f)
                                for k in range(NDT):
                                    mm = t.matmul(psC[0:96, :],
                                                  wxp_sb[:, k * 96:(k + 1) * 96],
                                                  xcb[:, k * P2C + f * 512:k * P2C + (f + 1) * 512],
                                                  start=(k == 0), stop=(k == NDT - 1))
                                ev.inc(mm)

                        @blk.scalar
                        def _(s):
                            for f in range(4):
                                s.wait_ge(ev.h, ev_b2 + f + 1)
                                sc.inc(s.activation(dbc_sb[:, f * 512:(f + 1) * 512],
                                                    psC[0:96, :], AF.Copy, bias=0.0, scale=1.0))

                        @blk.sync
                        def _(sy):
                            sc.wait(sy)
                            dst = bass.AP(dbc_part, ch * P2C, [[LSEQ, 96], [1, P2C]])
                            st.dma(sy.dma_start(out=dst, in_=dbc_sb[:, :]))
                            for dt in range(NDT):
                                dstx = bass.AP(xc_sp, dt * 128 * LSEQ + ch * P2C, [[LSEQ, 128], [1, P2C]])
                                st.dma(sy.dma_start(out=dstx, in_=xcf[:, dt * P2C:(dt + 1) * P2C]))
                            st.wait(sy)

        if upto not in ("p1", "p2"):
            with nc.Block() as blk:
                @blk.gpsimd
                def _(g):
                    cc2.inc(g.collective_compute(
                        "AllReduce", OP.add,
                        replica_groups=[[0, 1, 2, 3], [4, 5, 6, 7]],
                        ins=[dbc_part.ap().opt()], outs=[dbc_full.ap().opt()]))
                    g.wait_ge(cc2.h, 1)

        E.update(dict(x_tok=x_tok, w_out=w_out, ctrl=ctrl, f1t=f1t, f2m=f2m, fbias=fbias, coff=coff, y_loc=y_loc,
                      out_tok=out_tok, ln1_all=ln1_all, dbc_full=dbc_full, xc_sp=xc_sp,
                      ymid=ymid, y_all=y_all, xmt_sp=xmt_sp, int_sp=int_sp, o1_sp=o1_sp,
                      d_delta=d_delta, d_perm=d_perm, nc=nc))

        if upto in ("p4", "p5", "full"):
            _p4(E)
        if upto in ("p5", "full"):
            with nc.Block() as blk:
                @blk.gpsimd
                def _(g):
                    cc3.inc(g.collective_compute(
                        "AllGather", OP.bypass,
                        replica_groups=[[0, 1, 2, 3], [4, 5, 6, 7]],
                        ins=[ymid.ap().opt()], outs=[y_all.ap().opt()]))
                    g.wait_ge(cc3.h, 1)
        if upto == "full":
            _p6(E)
        else:
            with (
                nc.sbuf_tensor("zz", [128, DM], F32) as zz,
                nc.Block() as blk,
            ):
                @blk.vector
                def _(v):
                    ws.inc(v.memset(zz[:, :], 0.0))

                @blk.sync
                def _(sy):
                    ws.wait(sy)
                    for ti in range(TTOK // 128):
                        st.dma(sy.dma_start(out=out_tok[ti * 128:(ti + 1) * 128, :], in_=zz[:, :]))
                    st.wait(sy)

        if dbg:
            with nc.Block() as blk:
                @blk.sync
                def _(sy):
                    for name, dd, src in (("ln", d_ln, ln1_all), ("xc", d_xc, xc_sp),
                                          ("dbc", d_dbc, dbc_full), ("ymid", d_ymid, ymid),
                                          ("o1", d_o1, o1_sp), ("xmt", d_xmt, xmt_sp),
                                          ("int", d_int, int_sp)):
                        if dd is not None:
                            st.dma(sy.dma_start(out=dd[:, :], in_=src[:, :]))
                    st.wait(sy)
        with nc.Block() as blk:
            @blk.gpsimd
            def _(g):
                for s_ in (ld, gd, st, ev, sc, ws, cc1, cc2, cc3, pc):
                    g.sem_clear(s_.h)
    return nc


def _p4(E):
    nc = E["nc"]
    ld, st, ev, sc, ws, pc = E["ld"], E["st"], E["ev"], E["sc"], E["ws"], E["pc"]
    wdt_sb, bdt_sb, dv_sb, amat_sb, sel16_sb, wz_sb = (
        E["wdt_sb"], E["bdt_sb"], E["dv_sb"], E["amat_sb"], E["sel16_sb"], E["wz_sb"])
    ln1_all, dbc_full, xc_sp, ymid, d_delta = (
        E["ln1_all"], E["dbc_full"], E["xc_sp"], E["ymid"], E["d_delta"])
    C = P4C
    NH = NST // 2

    _es = contextlib.ExitStack()
    with _es:
        rz8 = _es.enter_context(nc.sbuf_tensor("rz8", [128, 2 * 8 * C], BF16))
        dltb = _es.enter_context(nc.sbuf_tensor("dltb", [64, C], BF16))
        dltf = _es.enter_context(nc.sbuf_tensor("dltf", [64, 2 * C], F32))
        bm_sb = _es.enter_context(nc.sbuf_tensor("bm_sb", [16, 2 * C], F32))
        cm_sb = _es.enter_context(nc.sbuf_tensor("cm_sb", [16, 2 * C], F32))
        delta = _es.enter_context(nc.sbuf_tensor("delta", [128, NDT * C], F32))
        xc4 = _es.enter_context(nc.sbuf_tensor("xc4", [128, 2 * NDT * C], F32))
        u4 = _es.enter_context(nc.sbuf_tensor("u4", [128, NDT * C], F32))
        gz4 = _es.enter_context(nc.sbuf_tensor("gz4", [128, NDT * C], F32))
        y4 = _es.enter_context(nc.sbuf_tensor("y4", [128, NDT * C], F32))
        yh4 = _es.enter_context(nc.sbuf_tensor("yh4", [128, NDT * C], F32))
        ymb = _es.enter_context(nc.sbuf_tensor("ymb", [128, NDT * C], BF16))
        bmb = _es.enter_context(nc.sbuf_tensor("bmb", [128, NH * C], F32))
        cmb = _es.enter_context(nc.sbuf_tensor("cmb", [128, NH * C], F32))
        hbuf = _es.enter_context(nc.sbuf_tensor("hbuf", [128, NH * C], F32))
        abuf8 = _es.enter_context(nc.sbuf_tensor("abuf8", [128, 8 * C], F32))
        bbuf8 = _es.enter_context(nc.sbuf_tensor("bbuf8", [128, 8 * C], F32))
        hstate = _es.enter_context(nc.sbuf_tensor("hstate", [128, NDT * NST], F32))
        ps0 = _es.enter_context(nc.psum_tensor("ps0", [128, 512], F32))
        ps1 = _es.enter_context(nc.psum_tensor("ps1", [128, 512], F32))
        ps2 = _es.enter_context(nc.psum_tensor("ps2", [128, 512], F32))
        ps3 = _es.enter_context(nc.psum_tensor("ps3", [128, 512], F32))
        ps = [ps0, ps1, ps2, ps3]
        with nc.Block() as blk:
            @blk.vector
            def _(v):
                v.memset(hstate[:, :], 0.0)

        def emit_loads(sy, ch):
            p_ = ch % 2
            rank_ = (ch * C) // TTOK
            col0_ = ch * C - rank_ * TTOK
            for cc in range(8):
                srcp = bass.AP(ln1_all, (rank_ * DM + cc * 128) * TTOK + col0_,
                               [[TTOK, 128], [1, C]])
                ld.dma(sy.dma_start(out=rz8[:, (p_ * 8 + cc) * C:(p_ * 8 + cc + 1) * C], in_=srcp))
            ld.dma(sy.dma_start(out=dltf[:, p_ * C:(p_ + 1) * C],
                                in_=bass.AP(dbc_full, ch * C, [[LSEQ, 64], [1, C]])))
            ld.dma(sy.dma_start(out=bm_sb[:, p_ * C:(p_ + 1) * C],
                                in_=bass.AP(dbc_full, 64 * LSEQ + ch * C, [[LSEQ, 16], [1, C]])))
            ld.dma(sy.dma_start(out=cm_sb[:, p_ * C:(p_ + 1) * C],
                                in_=bass.AP(dbc_full, 80 * LSEQ + ch * C, [[LSEQ, 16], [1, C]])))
            for dt in range(NDT):
                ld.dma(sy.dma_start(out=xc4[:, (p_ * NDT + dt) * C:(p_ * NDT + dt + 1) * C],
                                    in_=bass.AP(xc_sp, dt * 128 * LSEQ + ch * C,
                                                [[LSEQ, 128], [1, C]])))

        with nc.Block() as blk:
            @blk.sync
            def _(sy):
                emit_loads(sy, 0)

        for ch in range(NP4):
            p = ch % 2
            rank = (ch * C) // TTOK
            col0 = ch * C - rank * TTOK
            with nc.Block() as blk:
                @blk.vector
                def _(v):
                    ld.wait(v)
                    ws.inc(v.tensor_copy(dltb[:, :], dltf[:, p * C:(p + 1) * C]))

                ev_b = ev.n
                sc_b = sc.n
                groups = [("z", dt) for dt in range(NDT)] + [("d", dt) for dt in range(NDT)]

                @blk.tensor
                def _(t):
                    ld.wait(t)
                    ws.wait(t)
                    for g, (kind, dt) in enumerate(groups):
                        if g >= 4:
                            t.wait_ge(sc.h, sc_b + g - 3)
                        if kind == "z":
                            for k in range(8):
                                mm = t.matmul(ps[g % 4][:, :],
                                              wz_sb[:, k * DC + dt * 128:k * DC + (dt + 1) * 128],
                                              rz8[:, (p * 8 + k) * C:(p * 8 + k + 1) * C],
                                              start=(k == 0), stop=(k == 7))
                        else:
                            mm = t.matmul(ps[g % 4][:, :], wdt_sb[:, dt * 128:(dt + 1) * 128],
                                          dltb[:, :], start=True, stop=True)
                        ev.inc(mm)

                @blk.scalar
                def _(s):
                    for g, (kind, dt) in enumerate(groups):
                        s.wait_ge(ev.h, ev_b + g + 1)
                        if kind == "z":
                            sc.inc(s.activation(gz4[:, dt * C:(dt + 1) * C], ps[g % 4][:, :],
                                                AF.Silu, bias=0.0, scale=1.0))
                        else:
                            sc.inc(s.activation(delta[:, dt * C:(dt + 1) * C], ps[g % 4][:, :],
                                                AF.Identity, bias=bdt_sb[:, dt:dt + 1], scale=1.0))

            with nc.Block() as blk:
                @blk.scalar
                def _(s):
                    for dt in range(NDT):
                        sl = slice(dt * C, (dt + 1) * C)
                        ins = s.activation(y4[:, sl], delta[:, sl], AF.Abs, bias=0.0, scale=1.0)
                    sc.inc(ins)
                    sc.wait(s)
                    for dt in range(NDT):
                        sl = slice(dt * C, (dt + 1) * C)
                        ins = s.activation(yh4[:, sl], y4[:, sl], AF.Exp, bias=0.0, scale=-1.0)
                    sc.inc(ins)
                    sc.wait(s)
                    for dt in range(NDT):
                        sl = slice(dt * C, (dt + 1) * C)
                        ins = s.activation(y4[:, sl], yh4[:, sl], AF.Ln, bias=1.0, scale=1.0)
                    sc.inc(ins)

                @blk.vector
                def _(v):
                    sc.wait(v)
                    for dt in range(NDT):
                        sl = slice(dt * C, (dt + 1) * C)
                        ins = v.scalar_tensor_tensor(delta[:, sl], delta[:, sl], 0.0, y4[:, sl],
                                                     OP.max, OP.add)
                    ws.inc(ins)
                    ws.wait(v)
                    for dt in range(NDT):
                        sl = slice(dt * C, (dt + 1) * C)
                        ins = v.tensor_tensor(u4[:, sl], delta[:, sl],
                                              xc4[:, (p * NDT + dt) * C:(p * NDT + dt + 1) * C],
                                              OP.mult)
                    ws.inc(ins)

            if d_delta is not None:
                with nc.Block() as blk:
                    @blk.sync
                    def _(sy):
                        for dt in range(NDT):
                            dstd = bass.AP(d_delta, dt * 128 * LSEQ + ch * C, [[LSEQ, 128], [1, C]])
                            st.dma(sy.dma_start(out=dstd, in_=delta[:, dt * C:(dt + 1) * C]))
                        st.wait(sy)

            for half in range(2):
                with nc.Block() as blk:
                    ev_b = ev.n
                    sc_b = sc.n
                    ws_b = ws.n
                    pc_b = pc.n
                    rgroups = [("b", n) for n in range(NH)] + [("c", n) for n in range(NH)]

                    @blk.tensor
                    def _(t):
                        for g, (kind, n) in enumerate(rgroups):
                            gn = half * NH + n
                            if g >= 4:
                                t.wait_ge(sc.h, sc_b + g - 3)
                            src_ = bm_sb if kind == "b" else cm_sb
                            mm = t.matmul(ps[g % 4][:, :], sel16_sb[:, gn * 128:(gn + 1) * 128],
                                          src_[:, p * C:(p + 1) * C], start=True, stop=True)
                            ev.inc(mm)

                    @blk.scalar
                    def _(s):
                        for g, (kind, n) in enumerate(rgroups):
                            s.wait_ge(ev.h, ev_b + g + 1)
                            dstt = bmb if kind == "b" else cmb
                            sc.inc(s.activation(dstt[:, n * C:(n + 1) * C], ps[g % 4][:, :],
                                                AF.Copy, bias=0.0, scale=1.0))
                        for dt in range(NDT):
                            if dt >= 1:
                                s.wait_ge(ws.h, ws_b + (dt - 1) * 11 + 8)
                            for n in range(NH):
                                gn = half * NH + n
                                sc.inc(s.activation(abuf8[:, n * C:(n + 1) * C],
                                                    delta[:, dt * C:(dt + 1) * C], AF.Exp,
                                                    bias=0.0,
                                                    scale=amat_sb[:, dt * NST + gn:dt * NST + gn + 1]))

                    @blk.gpsimd
                    def _(g):
                        for dt in range(NDT):
                            g.wait_ge(sc.h, sc_b + 16)
                            if dt >= 1:
                                g.wait_ge(ws.h, ws_b + (dt - 1) * 11 + 8)
                            for n in range(NH):
                                pc.inc(g.tensor_tensor(bbuf8[:, n * C:(n + 1) * C],
                                                       u4[:, dt * C:(dt + 1) * C],
                                                       bmb[:, n * C:(n + 1) * C], OP.mult))


                    @blk.vector
                    def _(v):
                        for dt in range(NDT):
                            for n in range(NH):
                                gn = half * NH + n
                                v.wait_ge(sc.h, sc_b + 16 + dt * NH + n + 1)
                                v.wait_ge(pc.h, pc_b + dt * NH + n + 1)
                                ws.inc(v.tensor_tensor_scan(
                                    hbuf[:, n * C:(n + 1) * C],
                                    abuf8[:, n * C:(n + 1) * C],
                                    bbuf8[:, n * C:(n + 1) * C],
                                    hstate[:, dt * NST + gn:dt * NST + gn + 1],
                                    OP.mult, OP.add))
                            v.wait_ge(ws.h, ws.n)
                            lastc = bass.AP(hbuf, C - 1, [[NH * C, 128], [C, NH]])
                            ws.inc(v.tensor_copy(
                                bass.AP(hstate, dt * NST + half * NH, [[NDT * NST, 128], [1, NH]]),
                                lastc))
                            v.wait_ge(ws.h, ws.n)
                            ws.inc(v.tensor_tensor(hbuf[:, :], hbuf[:, :], cmb[:, :], OP.mult))
                            v.wait_ge(ws.h, ws.n)
                            h3 = bass.AP(hbuf, 0, [[NH * C, 128], [1, C], [C, NH]])
                            dst_y = y4 if half == 0 else yh4
                            ws.inc(v.tensor_reduce(dst_y[:, dt * C:(dt + 1) * C], h3, AX.X, OP.add))
                            v.wait_ge(ws.h, ws.n)
            with nc.Block() as blk:
                @blk.vector
                def _(v):
                    for dt in range(NDT):
                        sl = slice(dt * C, (dt + 1) * C)
                        ins = v.tensor_tensor(y4[:, sl], y4[:, sl], yh4[:, sl], OP.add)
                    ws.inc(ins)
                    ws.wait(v)
                    for dt in range(NDT):
                        sl = slice(dt * C, (dt + 1) * C)
                        ins = v.scalar_tensor_tensor(y4[:, sl],
                                                     xc4[:, (p * NDT + dt) * C:(p * NDT + dt + 1) * C],
                                                     dv_sb[:, dt:dt + 1],
                                                     y4[:, sl], OP.mult, OP.add)
                    ws.inc(ins)
                    ws.wait(v)
                    for dt in range(NDT):
                        sl = slice(dt * C, (dt + 1) * C)
                        ws.inc(v.tensor_tensor(ymb[:, sl], y4[:, sl], gz4[:, sl], OP.mult))

                @blk.sync
                def _(sy):
                    if ch + 1 < NP4:
                        emit_loads(sy, ch + 1)
                    ws.wait(sy)
                    for dt in range(NDT):
                        dsty = bass.AP(ymid, dt * 128 * LSEQ + ch * C, [[LSEQ, 128], [1, C]])
                        st.dma(sy.dma_start(out=dsty, in_=ymb[:, dt * C:(dt + 1) * C]))
                    st.wait(sy)


def _p6(E):
    nc = E["nc"]
    ld, gd, st, ev, sc, ws = E["ld"], E["gd"], E["st"], E["ev"], E["sc"], E["ws"]
    idf_sb, idb_sb, ppi_sb = E["idf_sb"], E["idb_sb"], E["ppi_sb"]
    x_tok, w_out, ctrl, f1t, f2m, fbias, out_tok = (
        E["x_tok"], E["w_out"], E["ctrl"], E["f1t"], E["f2m"], E["fbias"], E["out_tok"])
    coff = E["coff"]
    y_loc = E["y_loc"]

    with nc.Block() as blk:
        @blk.gpsimd
        def _(g):
            pid = g.partition_id()
            goff_v = (pid % 4) * TTOK
            gd.dma(g.dma_start(
                out=y_loc.ap(),
                in_=bass.AP(E["y_all"], goff_v, [[LSEQ, 4 * DC], [1, TTOK]])))
            gd.wait(g)
    y_all, xmt_sp, int_sp, o1_sp, d_perm = (
        E["y_all"], E["xmt_sp"], E["int_sp"], E["o1_sp"], E["d_perm"])

    _es = contextlib.ExitStack()
    with _es:
        wo_sb = _es.enter_context(nc.sbuf_tensor("wo_sb", [128, 16 * DM], BF16))
        ctrl_sb = _es.enter_context(nc.sbuf_tensor("ctrl_sb", [128, 8 * 64], F32))
        tie_sb = _es.enter_context(nc.sbuf_tensor("tie_sb", [64, 128], F32))
        mbd2_sb = _es.enter_context(nc.sbuf_tensor("mbd2_sb", [128, 512], BF16))
        mbdt_sb = _es.enter_context(nc.sbuf_tensor("mbdt_sb", [128, 4096], BF16))
        esg_sb = _es.enter_context(nc.sbuf_tensor("esg_sb", [64, 4 * 128], F32))
        perm_sb = _es.enter_context(nc.sbuf_tensor("perm_sb", [64, TTOK], F32))
        scr_sb = _es.enter_context(nc.sbuf_tensor("scr_sb", [128, 4096], F32))
        pint_sb = _es.enter_context(nc.sbuf_tensor("pint_sb", [128, 2], F32))
        coff_sb = _es.enter_context(nc.sbuf_tensor("coff_sb", [1, 32], mybir.dt.int32))
        with nc.Block() as blk:
            @blk.gpsimd
            def _(g):
                gd.dma(g.dma_start(out=wo_sb[:, :], in_=w_out[:, :]))
                gd.wait(g)
                g.iota(tie_sb[:, :], [[0, 8], [1, 16]], channel_multiplier=0,
                       allow_small_or_imprecise_dtypes=True)
                g.iota(scr_sb[:, 0:512], [[0, 64], [1, 8]], channel_multiplier=0,
                       allow_small_or_imprecise_dtypes=True)
                g.iota(scr_sb[:, 512:512 + 4096], [[0, 32], [1, 8], [0, 16]],
                       channel_multiplier=0, allow_small_or_imprecise_dtypes=True) if False else None

            @blk.sync
            def _(sy):
                ld.dma(sy.dma_start(out=ctrl_sb[:, :], in_=ctrl[:, :]))
                ld.dma(sy.dma_start(out=coff_sb[:, :], in_=coff[:, :]))
                ld.wait(sy)

        _es = contextlib.ExitStack()
        with _es:
            scr2_sb = _es.enter_context(nc.sbuf_tensor("scr2_sb", [128, 4096], F32))
            esgi_sb = _es.enter_context(nc.sbuf_tensor("esgi_sb", [64, 4 * 128], F32))
            with nc.Block() as blk:
                @blk.gpsimd
                def _(g):
                    g.iota(scr2_sb[:, :], [[0, 32], [1, 8], [0, 16]], channel_multiplier=0,
                           allow_small_or_imprecise_dtypes=True)
                    pass

                @blk.vector
                def _(v):
                    pass

            with (
                nc.sbuf_tensor("ii_sb", [128, 2], mybir.dt.int32) as ii_sb,
                nc.Block() as blk,
            ):
                @blk.gpsimd
                def _(g):
                    ws.inc(g.iota(ii_sb[:, 0:1], [[0, 1]], channel_multiplier=1))

                @blk.vector
                def _(v):
                    ws.wait(v)
                    ws.inc(v.tensor_scalar(ii_sb[:, 1:2], ii_sb[:, 0:1], 4, None,
                                           OP.logical_shift_right))
                    ws.wait(v)
                    ws.inc(v.tensor_scalar(ii_sb[:, 0:1], ii_sb[:, 0:1], 7, None, OP.bitwise_and))

                @blk.vector
                def _(v):
                    ws.wait(v)
                    v.tensor_copy(pint_sb[:, 0:1], ii_sb[:, 1:2])
                    v.tensor_copy(pint_sb[:, 1:2], ii_sb[:, 0:1])

            with nc.Block() as blk:
                @blk.vector
                def _(v):
                    v.tensor_scalar(tie_sb[:, :], tie_sb[:, :], TIE_STEP, None, OP.mult)
                    ins = v.tensor_scalar(scr_sb[:, 0:512], scr_sb[:, 0:512], pint_sb[:, 0:1],
                                          None, OP.is_equal)
                    ins = v.tensor_scalar(scr2_sb[:, :], scr2_sb[:, :], pint_sb[:, 1:2],
                                          None, OP.is_equal)
                    ws.inc(ins)
                    ws.wait(v)
                    v.tensor_copy(mbd2_sb[:, :], scr_sb[:, 0:512])
                    v.tensor_copy(mbdt_sb[:, :], scr2_sb[:, :])

            with nc.Block() as blk:
                @blk.gpsimd
                def _(g):
                    for esg in range(4):
                        ins = g.iota(esgi_sb[:, esg * 128:(esg + 1) * 128], [[1, 16], [0, 8]],
                                     base=esg * 16, channel_multiplier=0,
                                     allow_small_or_imprecise_dtypes=True)
                    ws.inc(ins)

                @blk.vector
                def _(v):
                    ws.wait(v)
                    v.tensor_scalar(esg_sb[:, :], esgi_sb[:, :], ppi_sb[0:64, 0:1],
                                    None, OP.is_equal)

        _es = contextlib.ExitStack()
        with _es:
            ya_sb = _es.enter_context(nc.sbuf_tensor("ya_sb", [128, 2 * 16 * 128], BF16))
            xt_sb = _es.enter_context(nc.sbuf_tensor("xt_sb", [128, 2 * DM], F32))
            o1t = _es.enter_context(nc.sbuf_tensor("o1t", [128, DM], F32))
            sqt = _es.enter_context(nc.sbuf_tensor("sqt", [128, DM], F32))
            l2 = _es.enter_context(nc.sbuf_tensor("l2", [128, DM], F32))
            l2b = _es.enter_context(nc.sbuf_tensor("l2b", [128, DM], BF16))
            l2f = _es.enter_context(nc.sbuf_tensor("l2f", [128, DM], F32))
            lgt = _es.enter_context(nc.sbuf_tensor("lgt", [64, 128], F32))
            mx_sb = _es.enter_context(nc.sbuf_tensor("mx_sb", [64, 136], F32))
            prmb = _es.enter_context(nc.sbuf_tensor("prmb", [64, 128], BF16))
            prT = _es.enter_context(nc.sbuf_tensor("prT", [128, 64], BF16))
            prTE = _es.enter_context(nc.sbuf_tensor("prTE", [128, 512], BF16))
            prBD = _es.enter_context(nc.sbuf_tensor("prBD", [128, 512], BF16))
            xmt4 = _es.enter_context(nc.sbuf_tensor("xmt4", [128, 4 * DM], BF16))
            lstt = _es.enter_context(nc.sbuf_tensor("lstt", [128, 8], F32))
            q0 = _es.enter_context(nc.psum_tensor("q0", [128, 512], F32))
            q1 = _es.enter_context(nc.psum_tensor("q1", [128, 512], F32))
            qb0 = _es.enter_context(nc.psum_tensor("qb0", [128, 512], BF16))
            qb1 = _es.enter_context(nc.psum_tensor("qb1", [128, 512], BF16))
            q = [q0, q1]
            qb = [qb0, qb1]

            def p6a_loads(sy, g_, tj):
                pj = tj % 2
                if sy is not None:
                    ld.dma(sy.dma_start(out=xt_sb[:, pj * DM:(pj + 1) * DM],
                                        in_=x_tok[tj * 128:(tj + 1) * 128, :]))
                if g_ is not None:
                    gd.dma(g_.dma_start(
                        out=bass.AP(ya_sb, pj * 2048, [[2 * 2048, 128], [128, 16], [1, 128]]),
                        in_=bass.AP(y_loc, tj * 128,
                                    [[TTOK, 128], [128 * TTOK, 16], [1, 128]])))

            with nc.Block() as blk:
                @blk.sync
                def _(sy):
                    p6a_loads(sy, None, 0)

                @blk.gpsimd
                def _(g):
                    p6a_loads(None, g, 0)

            for ti in range(TTOK // 128):
                pt = ti % 2
                with nc.Block() as blk:
                    ld_pre = ld.n
                    gd_pre = gd.n

                    @blk.sync
                    def _(sy):
                        if ti + 1 < TTOK // 128:
                            p6a_loads(sy, None, ti + 1)

                    @blk.gpsimd
                    def _(g):
                        if ti + 1 < TTOK // 128:
                            p6a_loads(None, g, ti + 1)

                    ev_b = ev.n
                    ws_b = ws.n
                    sc_b = sc.n

                    @blk.tensor
                    def _(t):
                        t.wait_ge(ld.h, ld_pre)
                        t.wait_ge(gd.h, gd_pre)
                        for f in range(2):
                            for k in range(16):
                                mm = t.matmul(q[f][:, :],
                                              ya_sb[:, pt * 2048 + k * 128:pt * 2048 + (k + 1) * 128],
                                              wo_sb[:, k * DM + f * 512:k * DM + (f + 1) * 512],
                                              start=(k == 0), stop=(k == 15))
                            ev.inc(mm)

                    @blk.vector
                    def _(v):
                        for f in range(2):
                            v.wait_ge(ev.h, ev_b + f + 1)
                            ws.inc(v.tensor_tensor(o1t[:, f * 512:(f + 1) * 512], q[f][:, :],
                                                   xt_sb[:, pt * DM + f * 512:pt * DM + (f + 1) * 512],
                                                   OP.add))
                        ws.wait(v)
                        ws.inc(v.reduce_sum(lstt[:, 0:1], o1t[:, :], AX.X))

                    @blk.scalar
                    def _(s):
                        ws.wait(s)
                        sc.inc(s.activation(sqt[:, :], o1t[:, :], AF.Square, bias=0.0, scale=1.0))

                    @blk.vector
                    def _(v):
                        sc.wait(v)
                        ws.inc(v.reduce_sum(lstt[:, 1:2], sqt[:, :], AX.X))
                        ws.wait(v)
                        v.tensor_scalar(lstt[:, 2:3], lstt[:, 0:1], 1.0 / DM, None, OP.mult)
                        ws.inc(v.tensor_scalar(lstt[:, 3:4], lstt[:, 1:2], 1.0 / DM, None, OP.mult))
                        ws.wait(v)
                        ws.inc(v.tensor_tensor(lstt[:, 4:5], lstt[:, 2:3], lstt[:, 2:3], OP.mult))
                        ws.wait(v)
                        ws.inc(v.tensor_tensor(lstt[:, 5:6], lstt[:, 3:4], lstt[:, 4:5], OP.subtract))

                    @blk.scalar
                    def _(s):
                        ws.wait(s)
                        sc.inc(s.activation(lstt[:, 6:7], lstt[:, 5:6], AF.Ln, bias=EPS, scale=1.0))
                        sc.wait(s)
                        sc.inc(s.activation(lstt[:, 7:8], lstt[:, 6:7], AF.Exp, bias=0.0, scale=-0.5))

                    @blk.vector
                    def _(v):
                        sc.wait(v)
                        ws.inc(v.tensor_scalar(l2[:, :], o1t[:, :], lstt[:, 2:3], lstt[:, 7:8],
                                               OP.subtract, OP.mult))
                        ws.wait(v)
                        ws.inc(v.tensor_copy(l2b[:, :], l2[:, :]))

                    ev_b2 = ev.n
                    sc_b2 = sc.n

                    @blk.tensor
                    def _(t):
                        ws.wait(t)
                        for grp in range(2):
                            for j in range(4):
                                c = grp * 4 + j
                                mm = t.transpose(q[grp][:, j * 128:(j + 1) * 128],
                                                 l2[:, c * 128:(c + 1) * 128], idf_sb[:, :])
                            ev.inc(mm)

                    @blk.scalar
                    def _(s):
                        for grp in range(2):
                            s.wait_ge(ev.h, ev_b2 + grp + 1)
                            sc.inc(s.activation(l2f[:, grp * 512:(grp + 1) * 512], q[grp][:, :],
                                                AF.Copy, bias=0.0, scale=1.0))

                    ev_b3 = ev.n
                    ws_b3 = ws.n

                    @blk.tensor
                    def _(t):
                        t.wait_ge(sc.h, sc.n)
                        for k in range(8):
                            mm = t.matmul(q0[0:64, 0:128], ctrl_sb[:, k * 64:(k + 1) * 64],
                                          l2f[:, k * 128:(k + 1) * 128],
                                          start=(k == 0), stop=(k == 7))
                        ev.inc(mm)

                    @blk.vector
                    def _(v):
                        v.wait_ge(ev.h, ev_b3 + 1)
                        ws.inc(v.tensor_tensor(lgt[:, :], q0[0:64, 0:128], tie_sb[:, :], OP.add))
                        ws.wait(v)
                        lg3 = bass.AP(lgt, 0, [[128, 64], [16, 8], [1, 16]])
                        ws.inc(v.tensor_reduce(mx_sb[:, 0:8], lg3, AX.X, OP.max))
                        ws.wait(v)
                        for tt_ in range(16):
                            ins = v.tensor_copy(bass.AP(mx_sb, 8 + tt_, [[136, 64], [16, 8]]),
                                                mx_sb[:, 0:8])
                        ws.inc(ins)
                        ws.wait(v)
                        ws.inc(v.tensor_tensor(perm_sb[:, ti * 128:(ti + 1) * 128], lgt[:, :],
                                               mx_sb[:, 8:136], OP.is_equal))
                        ws.wait(v)
                        ws.inc(v.tensor_copy(prmb[:, :], perm_sb[:, ti * 128:(ti + 1) * 128]))

                    ev_b4 = ev.n
                    sc_b4 = sc.n

                    @blk.tensor
                    def _(t):
                        ws.wait(t)
                        ev.inc(t.transpose(qb0[0:128, 0:64], prmb[:, :], idb_sb[0:64, 0:64]))

                    @blk.scalar
                    def _(s):
                        s.wait_ge(ev.h, ev_b4 + 1)
                        sc.inc(s.activation(prT[:, :], qb0[0:128, 0:64], AF.Copy, bias=0.0, scale=1.0))

                    @blk.vector
                    def _(v):
                        v.wait_ge(sc.h, sc_b4 + 1)
                        for j in range(8):
                            ins = v.tensor_copy(bass.AP(prTE, j, [[512, 128], [8, 64]]), prT[:, :])
                        ws.inc(ins)
                        ws.wait(v)
                        ws.inc(v.tensor_tensor(prBD[:, :], prTE[:, :], mbd2_sb[:, :], OP.mult))

                    ev_b5 = ev.n
                    sc_b5 = sc.n

                    @blk.tensor
                    def _(t):
                        ws.wait(t)
                        for qq in range(4):
                            for f in range(2):
                                g = qq * 2 + f
                                if g >= 2:
                                    t.wait_ge(sc.h, sc_b5 + g - 1)
                                mm = t.matmul(q[g % 2][:, :], prBD[:, qq * 128:(qq + 1) * 128],
                                              l2b[:, f * 512:(f + 1) * 512], start=True, stop=True)
                                ev.inc(mm)

                    @blk.scalar
                    def _(s):
                        for g in range(8):
                            qq, f = g // 2, g % 2
                            s.wait_ge(ev.h, ev_b5 + g + 1)
                            sc.inc(s.activation(
                                xmt4[:, qq * DM + f * 512:qq * DM + (f + 1) * 512],
                                q[g % 2][:, :], AF.Copy, bias=0.0, scale=1.0))

                    @blk.sync
                    def _(sy):
                        sc.wait(sy)
                        st.dma(sy.dma_start(out=o1_sp[ti * 128:(ti + 1) * 128, :], in_=o1t[:, :]))
                        for qq in range(4):
                            st.dma(sy.dma_start(
                                out=xmt_sp[ti * 512 + qq * 128:ti * 512 + (qq + 1) * 128, :],
                                in_=xmt4[:, qq * DM:(qq + 1) * DM]))
                        st.wait(sy)

        _p6b(E, dict(locals()))
        _p6c(E, dict(locals()))

        if d_perm is not None:
            with nc.Block() as blk:
                @blk.sync
                def _(sy):
                    st.dma(sy.dma_start(out=d_perm[:, :], in_=perm_sb[:, :]))
                    st.wait(sy)


def _p6b(E, env):
    nc = E["nc"]
    ld, gd, st, ev, sc, ws = E["ld"], E["gd"], E["st"], E["ev"], E["sc"], E["ws"]
    f1t, f2m, fbias = E["f1t"], E["f2m"], E["fbias"]
    xmt_sp, int_sp = E["xmt_sp"], E["int_sp"]

    _es = contextlib.ExitStack()
    with _es:
        xmte = _es.enter_context(nc.sbuf_tensor("xmte", [128, 2048], BF16))
        xme = _es.enter_context(nc.sbuf_tensor("xme", [128, 8 * 256], BF16))
        f1es = _es.enter_context(nc.sbuf_tensor("f1es", [128, 512], BF16))
        f2es = _es.enter_context(nc.sbuf_tensor("f2es", [64, DM], BF16))
        fb_sb = _es.enter_context(nc.sbuf_tensor("fb_sb", [64, 1], F32))
        inner_sb = _es.enter_context(nc.sbuf_tensor("inner_sb", [64, 256], BF16))
        intr = _es.enter_context(nc.sbuf_tensor("intr", [128, 2 * DM], BF16))
        pb0 = _es.enter_context(nc.psum_tensor("pb0", [128, 512], BF16))
        pb1 = _es.enter_context(nc.psum_tensor("pb1", [128, 512], BF16))
        pf = _es.enter_context(nc.psum_tensor("pf", [128, 512], F32))
        pg0 = _es.enter_context(nc.psum_tensor("pg0", [128, 512], F32))
        pg1 = _es.enter_context(nc.psum_tensor("pg1", [128, 512], F32))
        pb = [pb0, pb1]
        pg = [pg0, pg1]
        for es in range(64):
            q = es // 16
            off = (q * 128 + (es % 16) * 8) * DM
            with nc.Block() as blk:
                @blk.sync
                def _(sy):
                    for half in range(2):
                        src = bass.AP(xmt_sp, off + half * 16 * 512 * DM,
                                      [[512 * DM, 16], [DM, 8], [1, DM]])
                        ld.dma(sy.dma_start(out=xmte[:, half * DM:(half + 1) * DM], in_=src))
                    ld.dma(sy.dma_start(out=fb_sb[:, 0:1],
                                        in_=bass.AP(fbias, es * 64, [[1, 64], [1, 1]])))

                @blk.gpsimd
                def _(g):
                    gd.dma(g.dma_start(out=f1es[:, :],
                                       in_=bass.AP(f1t, es * 128 * 512, [[512, 128], [1, 512]])))
                    gd.dma(g.dma_start(out=f2es[:, :],
                                       in_=bass.AP(f2m, es * 64 * DM, [[DM, 64], [1, DM]])))

                ev_b = ev.n
                sc_b = sc.n

                @blk.tensor
                def _(t):
                    ld.wait(t)
                    gd.wait(t)
                    for grp in range(4):
                        if grp >= 2:
                            t.wait_ge(sc.h, sc_b + (grp - 1) * 4)
                        for j in range(4):
                            g = grp * 4 + j
                            half, c = g // 8, g % 8
                            mm = t.transpose(pb[grp % 2][:, j * 128:(j + 1) * 128],
                                             xmte[:, half * DM + c * 128:half * DM + (c + 1) * 128],
                                             E["idb_sb"][:, :])
                        ev.inc(mm)

                @blk.scalar
                def _(s):
                    for grp in range(4):
                        s.wait_ge(ev.h, ev_b + grp + 1)
                        for j in range(4):
                            g = grp * 4 + j
                            half, c = g // 8, g % 8
                            sc.inc(s.activation(
                                xme[:, c * 256 + half * 128:c * 256 + (half + 1) * 128],
                                pb[grp % 2][:, j * 128:(j + 1) * 128],
                                AF.Copy, bias=0.0, scale=1.0))

                ev_b2 = ev.n
                sc_b2 = sc.n

                @blk.tensor
                def _(t):
                    t.wait_ge(sc.h, sc.n)
                    for c in range(8):
                        mm = t.matmul(pf[0:64, 0:256], f1es[:, c * 64:(c + 1) * 64],
                                      xme[:, c * 256:(c + 1) * 256],
                                      start=(c == 0), stop=(c == 7))
                    ev.inc(mm)

                @blk.scalar
                def _(s):
                    s.wait_ge(ev.h, ev_b2 + 1)
                    sc.inc(s.activation(inner_sb[:, :], pf[0:64, 0:256], AF.Relu,
                                        bias=fb_sb[:, 0:1], scale=1.0))

                ev_b3 = ev.n
                sc_b3 = sc.n

                @blk.tensor
                def _(t):
                    t.wait_ge(sc.h, sc.n)
                    for m in range(2):
                        for f in range(2):
                            g = m * 2 + f
                            if g >= 2:
                                t.wait_ge(sc.h, sc_b3 + g - 1)
                            mm = t.matmul(pg[g % 2][:, :], inner_sb[:, m * 128:(m + 1) * 128],
                                          f2es[:, f * 512:(f + 1) * 512], start=True, stop=True)
                            ev.inc(mm)

                @blk.scalar
                def _(s):
                    for g in range(4):
                        m, f = g // 2, g % 2
                        s.wait_ge(ev.h, ev_b3 + g + 1)
                        sc.inc(s.activation(intr[:, m * DM + f * 512:m * DM + (f + 1) * 512],
                                            pg[g % 2][:, :], AF.Copy, bias=0.0, scale=1.0))

                @blk.sync
                def _(sy):
                    sc.wait(sy)
                    for m in range(2):
                        st.dma(sy.dma_start(
                            out=int_sp[es * 256 + m * 128:es * 256 + (m + 1) * 128, :],
                            in_=intr[:, m * DM:(m + 1) * DM]))
                    st.wait(sy)


def _p6c(E, env):
    nc = E["nc"]
    ld, st, ev, sc, ws = E["ld"], E["st"], E["ev"], E["sc"], E["ws"]
    x_tok, out_tok, int_sp, o1_sp = E["x_tok"], E["out_tok"], E["int_sp"], E["o1_sp"]
    perm_sb = env["perm_sb"]
    esg_sb = env["esg_sb"]
    mbdt_sb = env["mbdt_sb"]

    _es = contextlib.ExitStack()
    with _es:
        pbdt = _es.enter_context(nc.sbuf_tensor("pbdt", [128, 4 * TTOK], BF16))
        scrE = _es.enter_context(nc.sbuf_tensor("scrE", [128, 512], F32))
        ir0 = _es.enter_context(nc.sbuf_tensor("ir0", [128, DM], BF16))
        ir1 = _es.enter_context(nc.sbuf_tensor("ir1", [128, DM], BF16))
        ir2 = _es.enter_context(nc.sbuf_tensor("ir2", [128, DM], BF16))
        ir3 = _es.enter_context(nc.sbuf_tensor("ir3", [128, DM], BF16))
        o1r = _es.enter_context(nc.sbuf_tensor("o1r", [128, DM], F32))
        outt = _es.enter_context(nc.sbuf_tensor("outt", [128, DM], F32))
        pp0 = _es.enter_context(nc.psum_tensor("pp0", [128, 512], F32))
        pp1 = _es.enter_context(nc.psum_tensor("pp1", [128, 512], F32))
        irs = [ir0, ir1, ir2, ir3]
        pp = [pp0, pp1]
        for esg in range(4):
            with nc.Block() as blk:
                ev_b = ev.n
                sc_b = sc.n
                ws_b = ws.n

                @blk.tensor
                def _(t):
                    for F in range(8):
                        if F >= 2:
                            t.wait_ge(sc.h, sc_b + F - 1)
                        mm = t.matmul(pp[F % 2][:, :], esg_sb[:, esg * 128:(esg + 1) * 128],
                                      perm_sb[:, F * 512:(F + 1) * 512], start=True, stop=True)
                        ev.inc(mm)

                @blk.scalar
                def _(s):
                    for F in range(8):
                        s.wait_ge(ev.h, ev_b + F + 1)
                        s.wait_ge(ws.h, ws_b + F)
                        sc.inc(s.activation(scrE[:, :], pp[F % 2][:, :], AF.Copy,
                                            bias=0.0, scale=1.0))

                @blk.vector
                def _(v):
                    for F in range(8):
                        v.wait_ge(sc.h, sc_b + F + 1)
                        ws.inc(v.tensor_tensor(pbdt[:, esg * TTOK + F * 512:esg * TTOK + (F + 1) * 512],
                                               scrE[:, :], mbdt_sb[:, F * 512 % 4096:(F * 512 % 4096) + 512],
                                               OP.mult))

        for ti in range(TTOK // 128):
            with nc.Block() as blk:
                @blk.sync
                def _(sy):
                    for esg in range(4):
                        src = bass.AP(int_sp, ((esg * 16) * 256 + ti * 8) * DM,
                                      [[256 * DM, 16], [DM, 8], [1, DM]])
                        ld.dma(sy.dma_start(out=irs[esg][:, :], in_=src))
                    ld.dma(sy.dma_start(out=o1r[:, :], in_=o1_sp[ti * 128:(ti + 1) * 128, :]))

                ev_b = ev.n

                @blk.tensor
                def _(t):
                    ld.wait(t)
                    for f in range(2):
                        for esg in range(4):
                            mm = t.matmul(pp[f][:, :], pbdt[:, esg * TTOK + ti * 128:esg * TTOK + (ti + 1) * 128],
                                          irs[esg][:, f * 512:(f + 1) * 512],
                                          start=(esg == 0), stop=(esg == 3))
                        ev.inc(mm)

                @blk.vector
                def _(v):
                    for f in range(2):
                        v.wait_ge(ev.h, ev_b + f + 1)
                        ws.inc(v.tensor_tensor(outt[:, f * 512:(f + 1) * 512], pp[f][:, :],
                                               o1r[:, f * 512:(f + 1) * 512], OP.add))

                @blk.sync
                def _(sy):
                    ws.wait(sy)
                    st.dma(sy.dma_start(out=out_tok[ti * 128:(ti + 1) * 128, :], in_=outt[:, :]))
                    st.wait(sy)


_NC_CACHE = {}


def _host_prep(inputs):
    x = np.ascontiguousarray(np.asarray(inputs["L0_attn_x"], dtype=np.float32))
    W_in = np.asarray(inputs["L0_attn_W_in"], dtype=np.float32)
    W_conv = np.asarray(inputs["L0_attn_W_conv"], dtype=np.float32)
    W_xproj = np.asarray(inputs["L0_attn_W_xproj"], dtype=np.float32)
    W_dt = np.asarray(inputs["L0_attn_W_dt"], dtype=np.float32)
    b_dt_a = np.asarray(inputs["L0_attn_b_dt"], dtype=np.float32)
    A_log = np.asarray(inputs["L0_attn_A_log"], dtype=np.float32)
    D_a = np.asarray(inputs["L0_attn_D"], dtype=np.float32)
    W_out = np.asarray(inputs["L0_attn_W_out"], dtype=np.float32)
    controller = np.asarray(inputs["L0_ffn_controller"], dtype=np.float32)
    f1 = np.asarray(inputs["L0_ffn_f1"], dtype=np.float32)
    f2 = np.asarray(inputs["L0_ffn_f2"], dtype=np.float32)
    fb = np.asarray(inputs["L0_ffn_bias"], dtype=np.float32)

    def ktiles(a):
        K, M = a.shape
        return np.ascontiguousarray(a.reshape(K // 128, 128, M).transpose(1, 0, 2).reshape(128, -1))

    w_out_h = ktiles(np.ascontiguousarray(W_out.T))
    ctrl_h = ktiles(controller.reshape(DM, 64))
    f1t_h = np.ascontiguousarray(
        f1.transpose(1, 2, 0, 3).reshape(64, DM, 64).reshape(64, 8, 128, 64)
        .transpose(0, 2, 1, 3).reshape(64, 128, 512))
    f2m_h = np.ascontiguousarray(f2.reshape(64, 64, DM))
    fb_h = np.ascontiguousarray(fb.reshape(64, 64))

    in_maps = []
    for core in range(8):
        b, g = core // 4, core % 4
        sl = slice(g * DC, (g + 1) * DC)
        d = dict(
            x_tok=np.ascontiguousarray(x[b, g * TTOK:(g + 1) * TTOK, :]),
            w_xc=ktiles(np.ascontiguousarray(W_in[sl].T)),
            w_z=ktiles(np.ascontiguousarray(W_in[DIN + g * DC:DIN + (g + 1) * DC].T)),
            w_conv=np.ascontiguousarray(W_conv[sl, 0, :]),
            w_xp=ktiles(np.ascontiguousarray(W_xproj[:, sl].T)),
            w_dt=np.ascontiguousarray(W_dt[sl].T),
            b_dt=np.ascontiguousarray(b_dt_a[sl, None]),
            a_log=np.ascontiguousarray(A_log[sl]),
            dvec=np.ascontiguousarray(D_a[sl, None]),
            w_out=w_out_h,
            ctrl=ctrl_h,
            f1t=f1t_h,
            f2m=f2m_h,
            fbias=fb_h,
            coff=(g * TTOK + 128 * np.arange(32, dtype=np.int32))[None, :],
        )
        in_maps.append(d)
    return in_maps


def run(inputs, dbg=(), upto="full"):
    key = (tuple(sorted(dbg)), upto)
    if key not in _NC_CACHE:
        _NC_CACHE[key] = build(dbg, upto)
    nc = _NC_CACHE[key]
    in_maps = _host_prep(inputs)
    res = run_bass_kernel_spmd(nc, in_maps, core_ids=list(range(8)))
    return res


def kernel(**inputs):
    res = run(inputs)
    out = np.empty((B, LSEQ, DM), np.float32)
    for core in range(8):
        b, g = core // 4, core % 4
        out[b, g * TTOK:(g + 1) * TTOK, :] = res.results[core]["out_tok"]
    return out
